# version 16
# speedup vs baseline: 1.3148x; 1.0128x over previous
import contextlib
import numpy as np
import ml_dtypes
import concourse.bass as bass
import concourse.mybir as mybir
from concourse.bass_utils import run_bass_kernel_spmd

F32 = mybir.dt.float32
BF16 = mybir.dt.bfloat16
AF = mybir.ActivationFunctionType
ALU = mybir.AluOpType
AX = mybir.AxisListType

NCORE = 8
S, D, HID = 8192, 2048, 5461
NH, KVH, HD = 16, 4, 128
KVD = KVH * HD
NT = S // 128
TPC = NT // NCORE
NDC = D // 128
NHC = (HID + 127) // 128
LAST_HC = HID - 128 * (NHC - 1)
EPS_LN = 1e-5
EPS_RMS = 1.1920929e-07
GB_MUL = 11.32
N_W = HID * D


def _split2(v):
    c1 = np.float32(v)
    return float(c1), float(np.float32(np.float64(v) - np.float64(c1)))


PI1 = float(np.float32(3.140625))
_pr = np.float64(np.pi) - np.float64(PI1)
PI2 = float(np.float32(_pr))
PI3 = float(np.float32(_pr - np.float64(np.float32(_pr))))
HP1, HP2 = _split2(np.pi / 2)
LNB1, LNB2 = _split2(-np.log(10000.0) / 64.0)
ISQ1, ISQ2 = _split2(1.0 / np.sqrt(128.0))
INW1, INW2 = _split2(1.0 / N_W)


class Sched:
    ENGS = ("sync", "gpsimd", "vector", "scalar", "tensor")

    def __init__(self, nc, es):
        self.nc, self.es = nc, es
        self.ops = {e: [] for e in self.ENGS}
        self.dma_sems = set()
        self.cnt = {}
        self.mark = {}
        self.sems = {}

    def _sem(self, name):
        if name not in self.sems:
            self.sems[name] = self.es.enter_context(self.nc.semaphore(name))
            self.cnt[name] = 0
        return self.sems[name]

    def now(self, name):
        name = self._tr(name)
        self._sem(name)
        return self.cnt[name]

    AUTO = {"vector": "auV", "scalar": "auS", "gpsimd": "auG"}

    @staticmethod
    def _tr(n):
        if n.startswith("ve"):
            return "auV"
        if n.startswith("se"):
            return "auS"
        if n.startswith("po"):
            return "auG"
        if n.startswith("pe"):
            return "pe"
        return n

    def op(self, eng, fn, wait=(), inc=(), dma=False):
        wait = tuple((self._tr(n), v) for n, v in wait)
        inc = tuple((self._tr(n), kk) for n, kk in inc
                    if self._tr(n) not in ("auV", "auS", "auG"))
        if not dma and eng in self.AUTO:
            au = self.AUTO[eng]
            self._sem(au)
            wait = wait + ((au, self.cnt[au]),)
            inc = inc + ((au, 1),)
        seen = set()
        wait2 = []
        for n, v in wait:
            self._sem(n)
            assert isinstance(v, int), (n, v)
            if (n, v) not in seen:
                seen.add((n, v))
                wait2.append((n, v))
        best = {}
        for n, v in wait2:
            best[n] = max(best.get(n, -1), v)
        wait = tuple(best.items())
        for n, kk in inc:
            self._sem(n)
            self.cnt[n] += kk
        self.ops[eng].append((fn, wait, inc))

    def dma(self, eng, out, in_, sem, wait=()):
        self.dma_sems.add(sem)
        self.op(eng, lambda e, o=out, i=in_: e.dma_start(out=o, in_=i),
                wait=wait, inc=((sem, 16),), dma=True)

    def fence(self):
        names = {"auV", "auS", "auG", "pe"} | self.dma_sems
        return tuple((n, self.cnt[n]) for n in names if n in self.cnt)

    def emit(self, block):
        for eng_name in self.ENGS:
            items = self.ops[eng_name]
            if not items:
                continue

            def body(e, items=items):
                lw = {}
                for fn, waits, incs in items:
                    for n, v in waits:
                        if lw.get(n, -1) < v:
                            e.wait_ge(self.sems[n], v)
                            lw[n] = v
                    instr = fn(e)
                    for n, kk in incs:
                        instr = instr.then_inc(self.sems[n], kk)

            getattr(block, eng_name)(body)


def build_nc(debug=False):
    nc = bass.Bass("TRN2", target_bir_lowering=False)
    t = lambda name, shape, dt=F32: nc.dram_tensor(name, list(shape), dt, kind="ExternalInput").ap()
    xf = t("xf", (S, D))
    xfTt = t("xfTt", (NT, D, 128))
    xo = t("xo", (TPC * 128, D))
    xoTt = t("xoTt", (TPC, D, 128))
    wqT = t("wqT", (D, D)); wkT = t("wkT", (D, KVD)); wvT = t("wvT", (D, KVD))
    woT = t("woT", (D, D))
    bqr = t("bqr", (1, D)); bkr = t("bkr", (1, KVD)); bvr = t("bvr", (1, KVD)); bor = t("bor", (1, D))
    gain = t("gain", (1, NH))
    w1T = t("w1T", (D, HID)); w3T = t("w3T", (D, HID)); w2T = t("w2T", (HID, D))
    identity = t("identity", (128, 128), BF16)
    irow = t("irow", (1, 64))
    tokix = t("tokix", (128, NT))
    tokq = t("tokq", (128, TPC))
    k2s = t("k2s", (128, NT * 64)); sgs = t("sgs", (128, NT * 64))
    k2c = t("k2c", (128, NT * 64)); sgc = t("sgc", (128, NT * 64))
    qk2s = t("qk2s", (128, TPC * 64)); qsgs = t("qsgs", (128, TPC * 64))
    qk2c = t("qk2c", (128, TPC * 64)); qsgc = t("qsgc", (128, TPC * 64))
    binm = t("binm", (16, 128, 1024), BF16)
    out = nc.dram_tensor("out", [TPC * 128, D], F32, kind="ExternalOutput").ap()
    skind = "ExternalOutput" if debug else "Internal"
    Kd = nc.dram_tensor("Kd", [KVH, 128, S], BF16, kind=skind).ap()
    Vd = nc.dram_tensor("Vd", [KVH, S, HD], BF16, kind=skind).ap()
    Qsc = nc.dram_tensor("Qsc", [128, NH * TPC * 128], BF16, kind=skind).ap()
    AOsc = nc.dram_tensor("AOsc", [128, NH * TPC * 128], BF16, kind=skind).ap()
    X2d = nc.dram_tensor("X2d", [TPC * 128, D], F32, kind=skind).ap()
    Gsc = nc.dram_tensor("Gsc", [NHC * 128, TPC * 128], BF16, kind=skind).ap()
    Y2sc = nc.dram_tensor("Y2sc", [128, TPC * D], F32, kind=skind).ap()

    es = contextlib.ExitStack()
    k = Sched(nc, es)
    M = k.mark

    def SB(st, n, s_, d=F32):
        return st.enter_context(nc.sbuf_tensor(n, list(s_), d))

    def PS(st, n, s_, d=F32):
        return st.enter_context(nc.psum_tensor(n, list(s_), d))

    identb = SB(es, "identb", (128, 128), BF16)
    onesc_b = SB(es, "onesc_b", (128, 1), BF16)
    onesr_b = SB(es, "onesr_b", (1, 128), BF16)
    onesr_f = SB(es, "onesr_f", (1, 128), F32)
    onesc_f = SB(es, "onesc_f", (128, 1), F32)
    eps_ln = SB(es, "eps_ln", (128, 1), F32)
    eps_rms = SB(es, "eps_rms", (128, 1), F32)
    guard_c = SB(es, "guard_c", (1, 1), F32)
    biasr = SB(es, "biasr", (1, 2 * D + 2 * KVD), BF16)
    BQ0, BK0, BV0, BO0 = 0, D, D + KVD, D + 2 * KVD
    gain_r = SB(es, "gain_r", (1, NH), F32)
    negm_row = SB(es, "negm_row", (1, NH), F32)
    negm_bc = SB(es, "negm_bc", (128, NH), F32)
    mult_g = SB(es, "mult_g", (1, NH), F32)
    gs_bc = SB(es, "gs_bc", (128, NH), F32)
    sS = SB(es, "sS", (128, 4), F32)
    th_p = SB(es, "th_p", (128, 4), F32)
    th_n = SB(es, "th_n", (128, 4), F32)
    s_bc = SB(es, "s_bc", (128, 4), F32)
    invf_r = SB(es, "invf_r", (1, 64), F32)
    invfb = SB(es, "invfb", (128, 64), F32)
    irow_s = SB(es, "irow_s", (1, 64), F32)
    tokq_s = SB(es, "tokq_s", (128, TPC), F32)
    tokk_s = SB(es, "tokk_s", (128, NT), F32)
    stat = SB(es, "stat", (128, 8), F32)
    sqsc = SB(es, "sqsc", (128, D), F32)
    xstg = [SB(es, f"xstg{i}", (128, D)) for i in range(2)]
    rmsd = SB(es, "rmsd", (128, NH), F32)
    ropeu = SB(es, "ropeu", (128, NH * 64), BF16)
    ropev = SB(es, "ropev", (128, NH * 64), BF16)
    ropeu3 = ropeu.rearrange("p (h d) -> p h d", d=64)
    ropev3 = ropev.rearrange("p (h d) -> p h d", d=64)

    k.dma("sync", identb[:], identity, "cld")
    k.dma("sync", gain_r[:], gain, "cld")
    k.dma("sync", irow_s[:], irow, "cld")
    k.dma("sync", tokq_s[:], tokq, "cld")
    k.dma("sync", tokk_s[:], tokix, "cld")
    CLD = k.now("cld")
    for fn in (lambda e: e.memset(onesc_b[:], 1.0), lambda e: e.memset(onesr_b[:], 1.0),
               lambda e: e.memset(onesr_f[:], 1.0), lambda e: e.memset(onesc_f[:], 1.0),
               lambda e: e.memset(eps_ln[:], EPS_LN), lambda e: e.memset(eps_rms[:], EPS_RMS),
               lambda e: e.memset(guard_c[:], 1e-37)):
        k.op("vector", fn)
    k.op("vector", lambda e: e.memset(sS[:], 0.0))
    M_SS = k.now("veA")
    k.dma("gpsimd", biasr[0:1, BQ0:BQ0 + D], bqr, "bld")
    k.dma("gpsimd", biasr[0:1, BK0:BK0 + KVD], bkr, "bld")
    k.dma("gpsimd", biasr[0:1, BV0:BV0 + KVD], bvr, "bld")
    k.dma("gpsimd", biasr[0:1, BO0:BO0 + D], bor, "bld")
    BLD = k.now("bld")
    k.op("vector", lambda e: e.tensor_scalar(mult_g[:], gain_r[:], ISQ1, None, ALU.mult),
         wait=(("cld", CLD), ("veA", k.now("veA"))))
    k.op("vector", lambda e: e.scalar_tensor_tensor(mult_g[:], gain_r[:], ISQ2, mult_g[:], ALU.mult, ALU.add),
         inc=(("veA", 1),))
    k.op("vector", lambda e: e.tensor_scalar(negm_row[:], gain_r[:], -1.0, None, ALU.mult),
         wait=(("veA", k.now("veA")),))
    k.op("vector", lambda e: e.tensor_max(negm_row[:], negm_row[:], gain_r[:]))
    k.op("vector", lambda e: e.tensor_scalar(negm_row[:], negm_row[:], -GB_MUL, -1.0, ALU.mult, ALU.add),
         inc=(("veA", 1),))
    VA = k.now("veA")
    with contextlib.ExitStack() as st0:
        psB = PS(st0, "psB", (128, NH))
        k.op("tensor", lambda e: e.matmul(psB[:], onesr_f[0:1, :], negm_row[0:1, :], start=True, stop=True),
             wait=(("veA", VA),), inc=(("peA", 1),))
        k.op("vector", lambda e: e.tensor_copy(negm_bc[:], psB[:]),
             wait=(("peA", k.now("peA")),), inc=(("veA", 1),))
        k.op("tensor", lambda e: e.matmul(psB[:], onesr_f[0:1, :], mult_g[0:1, :], start=True, stop=True),
             wait=(("veA", k.now("veA")),), inc=(("peA", 1),))
        k.op("vector", lambda e: e.tensor_copy(gs_bc[:], psB[:]),
             wait=(("peA", k.now("peA")),), inc=(("veA", 1),))
        k.op("vector", lambda e: e.tensor_scalar(invf_r[:], irow_s[:], LNB1, None, ALU.mult),
             wait=(("veA", k.now("veA")),))
        k.op("vector", lambda e: e.scalar_tensor_tensor(invf_r[:], irow_s[:], LNB2, invf_r[:], ALU.mult, ALU.add),
             inc=(("veA", 1),))
        k.op("scalar", lambda e: e.activation(invf_r[:], invf_r[:], AF.Exp),
             wait=(("veA", k.now("veA")),), inc=(("seA", 1),))
        psI = PS(st0, "psI", (128, 64))
        k.op("tensor", lambda e: e.matmul(psI[:], onesr_f[0:1, :], invf_r[0:1, :], start=True, stop=True),
             wait=(("seA", k.now("seA")),), inc=(("peA", 1),))
        k.op("vector", lambda e: e.tensor_copy(invfb[:], psI[:]),
             wait=(("peA", k.now("peA")),), inc=(("veA", 1),))
    VE_INVF = k.now("veA")

    def gen_trig(st, tag, ntile, tok_sb, k2s_ap, sgs_ap, k2c_ap, sgc_ap, sin_t, cos_t, R):
        n = ntile * 64
        fr = SB(st, f"fr_{tag}", (128, n))
        rs = SB(st, f"rs_{tag}", (128, n))
        kbuf = SB(st, f"kbuf_{tag}", (128, n))
        sgb = SB(st, f"sgb_{tag}", (128, n))
        sem = f"tg_{tag}"
        k.dma("sync", kbuf[:], k2s_ap, sem, wait=k.fence())
        k.dma("sync", sgb[:], sgs_ap, sem)
        tl = k.now(sem)
        for t_ in range(ntile):
            k.op("vector", lambda e, t_=t_: e.tensor_scalar(
                fr[:, t_ * 64:(t_ + 1) * 64], invfb[:], tok_sb[:, t_:t_ + 1], None, ALU.mult),
                wait=((("veA", VE_INVF), (sem, tl), ("cld", CLD)) if t_ == 0 else ()))
        k.op("vector", lambda e: e.engine_nop(), inc=(("veA", 1),))
        k.op("vector", lambda e: e.scalar_tensor_tensor(rs[:], kbuf[:], -PI1, fr[:], ALU.mult, ALU.add),
             wait=(("veA", k.now("veA")),))
        k.op("vector", lambda e: e.scalar_tensor_tensor(rs[:], kbuf[:], -PI2, rs[:], ALU.mult, ALU.add))
        k.op("vector", lambda e: e.scalar_tensor_tensor(rs[:], kbuf[:], -PI3, rs[:], ALU.mult, ALU.add),
             inc=(("veA", 1),))
        k.op("scalar", lambda e: e.activation(rs[:], rs[:], AF.Sin),
             wait=(("veA", k.now("veA")), ("seA", k.now("seA"))), inc=(("seA", 1),))
        k.op("vector", lambda e: e.tensor_mul(rs[:], rs[:], sgb[:]),
             wait=(("seA", k.now("seA")),), inc=(("veA", 1),))
        stx3 = rs.rearrange("p (t i) -> p t i", i=64)
        sin4 = sin_t.rearrange("p (t r i) -> p t r i", r=R, i=64)
        for r_ in range(R):
            k.op("vector", lambda e, r_=r_: e.tensor_copy(sin4[:, :, r_, :], stx3[:, :, :]),
                 inc=((("veA", 1),) if r_ == R - 1 else ()))
        k.dma("sync", kbuf[:], k2c_ap, sem, wait=(("veA", k.now("veA")),))
        k.dma("sync", sgb[:], sgc_ap, sem)
        tl = k.now(sem)
        k.op("vector", lambda e: e.scalar_tensor_tensor(rs[:], kbuf[:], -PI1, fr[:], ALU.mult, ALU.add),
             wait=((sem, tl),))
        k.op("vector", lambda e: e.scalar_tensor_tensor(rs[:], kbuf[:], -PI2, rs[:], ALU.mult, ALU.add))
        k.op("vector", lambda e: e.scalar_tensor_tensor(rs[:], kbuf[:], -PI3, rs[:], ALU.mult, ALU.add))
        k.op("vector", lambda e: e.tensor_scalar(rs[:], rs[:], HP1, HP2, ALU.add, ALU.add),
             inc=(("veA", 1),))
        k.op("scalar", lambda e: e.activation(rs[:], rs[:], AF.Sin),
             wait=(("veA", k.now("veA")), ("seA", k.now("seA"))), inc=(("seA", 1),))
        k.op("vector", lambda e: e.tensor_mul(rs[:], rs[:], sgb[:]),
             wait=(("seA", k.now("seA")),), inc=(("veA", 1),))
        cos4 = cos_t.rearrange("p (t r i) -> p t r i", r=R, i=64)
        for r_ in range(R):
            k.op("vector", lambda e, r_=r_: e.tensor_copy(cos4[:, :, r_, :], stx3[:, :, :]),
                 inc=((("veA", 1),) if r_ == R - 1 else ()))

    def ln_tile(xin_ap, xout_ap, tv, ts_, extra_wait=()):
        k.op("vector", lambda e: e.tensor_reduce(stat[:, 0:1], xin_ap, AX.X, ALU.add),
             wait=extra_wait + ((tv, k.now(tv)),), inc=((tv, 1),))
        k.op("scalar", lambda e: e.activation(sqsc[:], xin_ap, AF.Square, accum_out=stat[:, 1:2]),
             wait=extra_wait + ((tv, k.now(tv)), (ts_, k.now(ts_))), inc=((ts_, 1),))
        vw, sw = k.now(tv), k.now(ts_)
        k.op("vector", lambda e: e.tensor_scalar(stat[:, 2:3], stat[:, 0:1], 1.0 / D, None, ALU.mult),
             wait=((tv, vw), (ts_, sw)))
        k.op("vector", lambda e: e.tensor_scalar(stat[:, 3:4], stat[:, 1:2], 1.0 / D, None, ALU.mult),
             inc=((tv, 1),))
        k.op("vector", lambda e: e.tensor_mul(stat[:, 4:5], stat[:, 2:3], stat[:, 2:3]),
             wait=((tv, k.now(tv)),))
        k.op("vector", lambda e: e.tensor_sub(stat[:, 4:5], stat[:, 3:4], stat[:, 4:5]),
             inc=((tv, 1),))
        k.op("scalar", lambda e: e.activation(stat[:, 5:6], stat[:, 4:5], AF.Ln, bias=eps_ln[:, 0:1]),
             wait=((tv, k.now(tv)), (ts_, k.now(ts_))), inc=((ts_, 1),))
        k.op("scalar", lambda e: e.activation(stat[:, 5:6], stat[:, 5:6], AF.Exp, scale=-0.5),
             wait=((ts_, k.now(ts_)),), inc=((ts_, 1),))
        k.op("vector", lambda e: e.scalar_tensor_tensor(stat[:, 6:7], stat[:, 2:3], -1.0, stat[:, 5:6], ALU.mult, ALU.mult),
             wait=((ts_, k.now(ts_)),), inc=((tv, 1),))
        k.op("vector", lambda e: e.tensor_scalar(xout_ap, xin_ap, stat[:, 5:6], stat[:, 6:7], ALU.mult, ALU.add),
             wait=((tv, k.now(tv)),), inc=((tv, 1),))
        return k.now(tv)

    def rope(src, dst, nheads, sin_ap, cos_ap, tv, wait0, eng="vector", rep=None):
        rep = nheads if rep is None else rep
        ng = nheads // rep
        s3 = src.rearrange("p (h d) -> p h d", d=HD)
        d3 = dst.rearrange("p (h d) -> p h d", d=HD)
        sin3 = sin_ap.rearrange("p (h i) -> p h i", i=64)
        cos3 = cos_ap.rearrange("p (h i) -> p h i", i=64)
        for gi in range(ng):
            hh = slice(gi * rep, (gi + 1) * rep)
            t1, t2 = s3[:, hh, 0:64], s3[:, hh, 64:128]
            k.op(eng, lambda e, hh=hh, t1=t1: e.tensor_tensor(d3[:, hh, 0:64], t1, cos3, ALU.mult),
                 wait=wait0 if gi == 0 else ())
            k.op(eng, lambda e, hh=hh, t2=t2: e.tensor_tensor(d3[:, hh, 64:128], t2, cos3, ALU.mult))
            k.op(eng, lambda e, hh=hh, t2=t2: e.tensor_tensor(ropeu3[:, hh, :], t2, sin3, ALU.mult))
            k.op(eng, lambda e, hh=hh, t1=t1: e.tensor_tensor(ropev3[:, hh, :], t1, sin3, ALU.mult),
                 inc=((tv, 1),) if gi == ng - 1 else ())
        vv = k.now(tv)
        for gi in range(ng):
            hh = slice(gi * rep, (gi + 1) * rep)
            k.op(eng, lambda e, hh=hh: e.tensor_tensor(d3[:, hh, 0:64], d3[:, hh, 0:64], ropeu3[:, hh, :], ALU.add),
                 wait=((tv, vv),) if gi == 0 else ())
            k.op(eng, lambda e, hh=hh: e.tensor_tensor(d3[:, hh, 64:128], d3[:, hh, 64:128], ropev3[:, hh, :], ALU.subtract),
                 inc=((tv, 1),) if gi == ng - 1 else ())
        return k.now(tv)

    SPW = 2048
    NROW = N_W // 128
    NCH1 = (NROW + SPW - 1) // SPW
    s_chunks = []
    for mi, wT in ((0, w1T), (2, w3T), (1, w2T)):
        flat = wT.flatten().rearrange("(p r) -> p r", p=128)
        for cc in range(NCH1):
            c0 = cc * SPW
            c1 = min(NROW, c0 + SPW)
            s_chunks.append((mi, flat[:, c0:c1], c1 - c0))

    spstg = [SB(es, f"spstg{i}", (128, SPW)) for i in range(2)]
    spred = SB(es, "spred", (128, 1), F32)
    spredV = SB(es, "spredV", (128, 1), F32)
    spabs = sqsc
    nsp = [0]

    nld = [0]
    nrd = [0]

    def sched_s_load():
        idx = nld[0]
        if idx >= len(s_chunks):
            return
        nld[0] += 1
        mi, ap, w = s_chunks[idx]
        par = idx % 2
        k.dma("sync", spstg[par][:, 0:w], ap, "sp%d" % par,
              wait=(M.get("spfree%d" % par, ("veA", 0)),))
        M["spld%d" % idx] = k.now("sp%d" % par)

    def sched_s_reduce(eng):
        idx = nrd[0]
        if idx >= nld[0] or idx >= len(s_chunks):
            return
        nrd[0] += 1
        mi, ap, w = s_chunks[idx]
        par = idx % 2
        ldw = (("sp%d" % par, M["spld%d" % idx]),)
        if eng == "scalar":
            k.op("scalar", lambda e, w=w, par=par: e.activation(
                spabs[:, 0:w], spstg[par][:, 0:w], AF.Abs, accum_out=spred[:, 0:1]),
                wait=ldw + (("poS", k.now("poS")), ("veA", M_SS)))
            red, rsem = spred, ("seA", k.now("seA"))
        else:
            k.op("vector", lambda e, w=w, par=par: e.tensor_reduce(
                spredV[:], spstg[par][:, 0:w], AX.X, ALU.add, apply_absolute_value=True),
                wait=ldw + (("poS", k.now("poS")), ("veA", M_SS)))
            red, rsem = spredV, ("veA", k.now("veA"))
        k.op("gpsimd", lambda e, mi=mi, red=red: e.tensor_add(sS[:, mi:mi + 1], sS[:, mi:mi + 1], red[:]),
             wait=(rsem,), inc=(("poS", 1),))
        M["spfree%d" % par] = rsem

    with contextlib.ExitStack() as stS:
        xnfP2 = [PS(stS, f"xnfP{i}", (128, D), BF16) for i in range(2)]
        xTst = [SB(stS, f"xTst{i}", (128, D), F32) for i in range(2)]
        xT_bf = [SB(stS, f"xT_bf{i}", (128, D), BF16) for i in range(2)]
        stat2 = [SB(stS, f"stat2_{i}", (128, 32), F32) for i in range(2)]

        def front_dma(par, xsrc_tok, xsrc_ft, xsem, tsem):
            k.dma("sync", xstg[par][:], xsrc_tok, xsem,
                  wait=(("veA", M.get("xfree%d" % par, 0)),))
            k.dma("scalar", xTst[par].rearrange("p (c q) -> p c q", q=128), xsrc_ft,
                  tsem, wait=(("seA", M.get("xtc%d" % par, 0)),))

        def front_cast(par, tsem):
            k.op("scalar", lambda e, par=par: e.activation(xT_bf[par][:], xTst[par][:], AF.Identity),
                 wait=((tsem, k.now(tsem)), ("peA", M.get("xTfree%d" % par, 0))),
                 inc=(("seA", 1),))
            M["xtc%d" % par] = k.now("seA")
            M["cast%d" % par] = k.now("seA")

        def front_stats(par, xsem):
            st_ = stat2[par]
            ldw = ((xsem, k.now(xsem)),)
            for i4 in range(4):
                k.op("vector", lambda e, st_=st_, par=par, i4=i4: e.bn_stats(
                    st_[:, i4 * 6:(i4 + 1) * 6], xstg[par][:, i4 * 512:(i4 + 1) * 512]),
                    wait=ldw if i4 == 0 else ())
            M["lnsq%d" % par] = k.now("veA")
            M["xfree%d" % par] = k.now("veA")

        def stats_back(t):
            par = t % 2
            st_ = stat2[par]
            k.op("vector", lambda e, st_=st_: e.bn_aggr(st_[:, 24:26], st_[:, 0:24]),
                 wait=(("veA", M["lnsq%d" % par]),))
            k.op("scalar", lambda e, st_=st_: e.activation(st_[:, 27:28], st_[:, 25:26], AF.Ln,
                                                           bias=eps_ln[:, 0:1]),
                 wait=(("veA", k.now("veA")),))
            k.op("scalar", lambda e, st_=st_: e.activation(st_[:, 27:28], st_[:, 27:28], AF.Exp, scale=-0.5),
                 inc=(("seA", 1),))
            k.op("vector", lambda e, st_=st_: e.tensor_scalar(st_[:, 28:29], st_[:, 24:25], st_[:, 27:28],
                                                              -1.0, ALU.mult, ALU.mult),
                 wait=(("seA", k.now("seA")),), inc=(("veA", 1),))
            M["nmr%d" % par] = k.now("veA")

        def build_corr(st, tag, wTb, nfeat, bias_off, wld_val):
            cs = SB(st, f"cs_{tag}", (128, nfeat), F32)
            bb = SB(st, f"bb_{tag}", (128, nfeat), F32)
            csrow = SB(st, f"csrow_{tag}", (1, nfeat), F32)
            with contextlib.ExitStack() as st0:
                psC = PS(st0, f"psC_{tag}", (128, 512))
                psB2 = PS(st0, f"psB2_{tag}", (128, 512))
                for n in range(nfeat // 512):
                    for c in range(NDC):
                        k.op("tensor", lambda e, n=n, c=c: e.matmul(
                            psC[0:1, :], onesc_b[:, 0:1],
                            wTb[:, c * nfeat + n * 512:c * nfeat + (n + 1) * 512],
                            start=(c == 0), stop=(c == NDC - 1)),
                            wait=((("wld", wld_val), ("veA", M.get("csf_" + tag, 0)))
                                  if c == 0 else ()),
                            inc=((("peA", 1),) if c == NDC - 1 else ()))
                    k.op("vector", lambda e, n=n: e.tensor_copy(
                        csrow[0:1, n * 512:(n + 1) * 512], psC[0:1, :]),
                        wait=(("peA", k.now("peA")),), inc=(("veA", 1),))
                    M["csf_" + tag] = k.now("veA")
                    k.op("tensor", lambda e, n=n: e.matmul(
                        psB2[:, :], onesr_f[0:1, :], csrow[0:1, n * 512:(n + 1) * 512],
                        start=True, stop=True),
                        wait=(("veA", k.now("veA")),),
                        inc=(("peA", 1),))
                    k.op("vector", lambda e, n=n: e.tensor_copy(
                        cs[:, n * 512:(n + 1) * 512], psB2[:, :]),
                        wait=(("peA", k.now("peA")),), inc=(("veA", 1),))
                    k.op("tensor", lambda e, n=n: e.matmul(
                        psB2[:, :], onesr_b[0:1, :],
                        biasr[0:1, bias_off + n * 512:bias_off + (n + 1) * 512],
                        start=True, stop=True),
                        wait=(("veA", k.now("veA")), ("bld", BLD)), inc=(("peA", 1),))
                    k.op("vector", lambda e, n=n: e.tensor_copy(
                        bb[:, n * 512:(n + 1) * 512], psB2[:, :]),
                        wait=(("peA", k.now("peA")),), inc=(("veA", 1),))
                    M["bbf_" + tag] = k.now("veA")
            return cs, bb

        with contextlib.ExitStack() as stQ:
            qsin = SB(stQ, "qsin", (128, TPC * 64 * KVH), BF16)
            qcos = SB(stQ, "qcos", (128, TPC * 64 * KVH), BF16)
            with contextlib.ExitStack() as stT:
                gen_trig(stT, "q", TPC, tokq_s, qk2s, qsgs, qk2c, qsgc, qsin, qcos, KVH)
            wqTb = SB(stQ, "wqTb", (128, NDC * D), BF16)
            k.dma("gpsimd", wqTb.rearrange("p (c f) -> p c f", c=NDC)[:, :, :],
                  wqT.rearrange("(c p) f -> p c f", p=128), "wld",
                  wait=(("bld", BLD),) + k.fence())
            WQ_LD = k.now("wld")
            cs_q, bb_q = build_corr(stQ, "q", wqTb, D, BQ0, WQ_LD)
            M["qpsfree"] = k.now("veA")
            psQ = PS(stQ, "psQ", (128, D))
            qcbuf = SB(stQ, "qcbuf", (128, D), F32)
            q_bf = SB(stQ, "q_bf", (128, D), BF16)
            q_ro = SB(stQ, "q_ro", (128, D), BF16)
            qspill = SB(stQ, "qspill", (128, D), BF16)
            qsc4 = Qsc.rearrange("p (h j q) -> p h j q", j=TPC, q=128)

            def q_front(j, what):
                par = j % 2
                if what == "dma":
                    front_dma(par, xo[j * 128:(j + 1) * 128, :],
                              xoTt[j:j + 1, :, :].rearrange("o (c p) q -> p (o c) q", p=128),
                              "qx%d" % par, "qt%d" % par)
                else:
                    front_cast(par, "qt%d" % par)
                    front_stats(par, "qx%d" % par)

            def q_back(j):
                par = j % 2
                stats_back(j)
                st_ = stat2[par]
                for n in range(4):
                    for c in range(NDC):
                        w = ()
                        if n == 0 and c == 0:
                            w = (("seA", M["cast%d" % par]), ("wld", WQ_LD),
                                 ("veA", M.get("qpsfree", 0)))
                        k.op("tensor", lambda e, n=n, c=c, par=par: e.matmul(
                            psQ[:, n * 512:(n + 1) * 512], xT_bf[par][:, c * 128:(c + 1) * 128],
                            wqTb[:, c * D + n * 512:c * D + (n + 1) * 512],
                            start=(c == 0), stop=(c == NDC - 1)), wait=w,
                            inc=(("peA", 1),) if (n == 3 and c == NDC - 1) else ())
                M["xTfree%d" % par] = k.now("peA")
                peq = k.now("peA")
                k.op("vector", lambda e, st_=st_: e.scalar_tensor_tensor(
                    qcbuf[:], psQ[:], st_[:, 27:28], bb_q[:], ALU.mult, ALU.add),
                     wait=(("peA", peq), ("veA", M["nmr%d" % par])))
                k.op("vector", lambda e, st_=st_: e.scalar_tensor_tensor(
                    qcbuf[:], cs_q[:], st_[:, 28:29], qcbuf[:], ALU.mult, ALU.add),
                     inc=(("veA", 1),))
                M["qpsfree"] = k.now("veA")
                k.op("scalar", lambda e: e.activation(sqsc[:], qcbuf[:], AF.Square),
                     wait=(("veA", k.now("veA")),), inc=(("seA", 1),))
                sq3 = sqsc.rearrange("p (h d) -> p h d", d=HD)
                k.op("vector", lambda e: e.tensor_reduce(rmsd[:, 0:NH], sq3[:, :, :], AX.X, ALU.add),
                     wait=(("seA", k.now("seA")),))
                k.op("scalar", lambda e: e.activation(rmsd[:, 0:NH], rmsd[:, 0:NH], AF.Ln,
                                                      bias=eps_rms[:, 0:1], scale=1.0 / HD),
                     wait=(("veA", k.now("veA")),))
                k.op("scalar", lambda e: e.activation(rmsd[:, 0:NH], rmsd[:, 0:NH], AF.Exp, scale=-0.5),
                     inc=(("seA", 1),))
                k.op("vector", lambda e: e.tensor_mul(rmsd[:, 0:NH], rmsd[:, 0:NH], gs_bc[:, 0:NH]),
                     wait=(("seA", k.now("seA")),))
                for h in range(NH):
                    k.op("vector", lambda e, h=h: e.tensor_scalar(
                        q_bf[:, h * HD:(h + 1) * HD], qcbuf[:, h * HD:(h + 1) * HD],
                        rmsd[:, h:h + 1], None, ALU.mult),
                        inc=(("veA", 1),) if h == NH - 1 else ())
                vro = rope(q_bf, q_ro, NH, qsin[:, j * 256:(j + 1) * 256], qcos[:, j * 256:(j + 1) * 256],
                           "veA", (), rep=KVH)
                for h in range(NH):
                    k.op("tensor", lambda e, h=h, par=par: e.transpose(
                        xnfP2[par][:, h * 128:(h + 1) * 128], q_ro[:, h * 128:(h + 1) * 128], identb[:]),
                        wait=(("veA", vro), ("veA", M.get("qspv%d" % par, 0))) if h == 0 else (),
                        inc=(("peA", 1),) if h == NH - 1 else ())
                k.op("vector", lambda e, par=par: e.tensor_copy(qspill[:], xnfP2[par][:]),
                     wait=(("peA", k.now("peA")), ("qw", k.now("qw"))), inc=(("veA", 1),))
                M["qspv%d" % par] = k.now("veA")
                qsp3 = qspill.rearrange("p (h q) -> p h q", q=128)
                k.dma("sync", qsc4[:, :, j, :], qsp3[:, :, :], "qw", wait=(("veA", k.now("veA")),))

            q_front(0, "dma")
            q_front(0, "rest")
            q_front(1, "dma")
            q_front(1, "rest")
            for j in range(TPC):
                if j + 2 < TPC:
                    q_front(j + 2, "dma")
                q_back(j)
                if j + 2 < TPC:
                    q_front(j + 2, "rest")
        QS_DONE = (("qw", k.now("qw")),)

        with contextlib.ExitStack() as stK:
            ksin = SB(stK, "ksin", (128, NT * 64 * KVH), BF16)
            kcos = SB(stK, "kcos", (128, NT * 64 * KVH), BF16)
            NH2 = NT // 2
            for hf in range(2):
                c0, c1 = hf * NH2 * 64, (hf + 1) * NH2 * 64
                s0, s1 = hf * NH2 * 64 * KVH, (hf + 1) * NH2 * 64 * KVH
                with contextlib.ExitStack() as stT:
                    gen_trig(stT, "k%d" % hf, NH2, tokk_s[:, hf * NH2:(hf + 1) * NH2],
                             k2s[:, c0:c1], sgs[:, c0:c1], k2c[:, c0:c1], sgc[:, c0:c1],
                             ksin[:, s0:s1], kcos[:, s0:s1], KVH)
            wkTb = SB(stK, "wkTb", (128, NDC * KVD), BF16)
            wvTb = SB(stK, "wvTb", (128, NDC * KVD), BF16)
            k.dma("gpsimd", wkTb.rearrange("p (c f) -> p c f", c=NDC)[:, :, :],
                  wkT.rearrange("(c p) f -> p c f", p=128), "wld", wait=k.fence())
            k.dma("gpsimd", wvTb.rearrange("p (c f) -> p c f", c=NDC)[:, :, :],
                  wvT.rearrange("(c p) f -> p c f", p=128), "wld")
            WKV_LD = k.now("wld")
            cs_k, bb_k = build_corr(stK, "k", wkTb, KVD, BK0, WKV_LD)
            cs_v, bb_v = build_corr(stK, "v", wvTb, KVD, BV0, WKV_LD)
            for _s in ("kpsfree0", "kpsfree1", "vpsfree0", "vpsfree1"):
                M[_s] = k.now("veA")
            psK = [PS(stK, f"psK{i}", (128, KVD)) for i in range(2)]
            psV = [PS(stK, f"psV{i}", (128, KVD)) for i in range(2)]
            kcbuf = SB(stK, "kcbuf", (128, KVD), F32)
            k_bf = [SB(stK, f"k_bf{i}", (128, KVD), BF16) for i in range(2)]
            k_ro = [SB(stK, f"k_ro{i}", (128, KVD), BF16) for i in range(2)]
            kT_st = [SB(stK, f"kT_st{i}", (128, KVD), BF16) for i in range(2)]
            v_st = [SB(stK, f"v_st{i}", (128, KVD), BF16) for i in range(2)]
            sqk = SB(stK, "sqk", (128, KVD), F32)
            vcbuf = sqk

            def kv_front(tt, what):
                par = tt % 2
                if what == "dma":
                    front_dma(par, xf[tt * 128:(tt + 1) * 128, :],
                              xfTt[tt:tt + 1, :, :].rearrange("o (c p) q -> p (o c) q", p=128),
                              "kx%d" % par, "kt%d" % par)
                else:
                    front_cast(par, "kt%d" % par)
                    front_stats(par, "kx%d" % par)

            def kv_s2(tt):
                stats_back(tt)

            def kv_s34K(tt):
                par = tt % 2
                for c in range(NDC):
                    w = ()
                    if c == 0:
                        w = (("seA", M["cast%d" % par]), ("wld", WKV_LD),
                             ("veA", M.get("kpsfree%d" % par, 0)))
                    k.op("tensor", lambda e, c=c, par=par: e.matmul(
                        psK[par][:], xT_bf[par][:, c * 128:(c + 1) * 128], wkTb[:, c * KVD:(c + 1) * KVD],
                        start=(c == 0), stop=(c == NDC - 1)), wait=w,
                        inc=(("peA", 1),) if c == NDC - 1 else ())
                M["pek%d" % par] = k.now("peA")

            def kv_s4V(tt, ktrans_of=None):
                par = tt % 2
                for c in range(NDC):
                    w = (("veA", M.get("vpsfree%d" % par, 0)),) if c == 0 else ()
                    k.op("tensor", lambda e, c=c, par=par: e.matmul(
                        psV[par][:], xT_bf[par][:, c * 128:(c + 1) * 128], wvTb[:, c * KVD:(c + 1) * KVD],
                        start=(c == 0), stop=(c == NDC - 1)), wait=w,
                        inc=(("peA", 1),) if c == NDC - 1 else ())
                    if ktrans_of is not None and 4 <= c < 8:
                        g = c - 4
                        kpar = ktrans_of % 2
                        wt = ()
                        if g == 0:
                            wt = (("poA", M["ropedone%d" % kpar]),
                                  ("seA", M.get("ktcs%d" % kpar, 0)),
                                  ("veA", M.get("qspv%d" % kpar, 0)))
                        k.op("tensor", lambda e, g=g, kpar=kpar: e.transpose(
                            xnfP2[kpar][:, g * 128:(g + 1) * 128], k_ro[kpar][:, g * 128:(g + 1) * 128], identb[:]),
                            wait=wt, inc=(("peA", 1),) if g == KVH - 1 else ())
                        if g == KVH - 1:
                            M["ktrans%d" % kpar] = k.now("peA")
                M["pev%d" % par] = k.now("peA")
                M["xTfree%d" % par] = k.now("peA")

            def kv_s5a(tt):
                par = tt % 2
                st_ = stat2[par]
                k.op("vector", lambda e, par=par, st_=st_: e.scalar_tensor_tensor(
                    kcbuf[:], psK[par][:], st_[:, 27:28], bb_k[:], ALU.mult, ALU.add),
                     wait=(("peA", M["pek%d" % par]), ("veA", M["nmr%d" % par]),
                           ("seA", M.get("sqdone", 0))))
                k.op("vector", lambda e, st_=st_: e.scalar_tensor_tensor(
                    kcbuf[:], cs_k[:], st_[:, 28:29], kcbuf[:], ALU.mult, ALU.add),
                     inc=(("veA", 1),))
                M["kpsfree%d" % par] = k.now("veA")
                k.op("scalar", lambda e: e.activation(sqk[:], kcbuf[:], AF.Square),
                     wait=(("veA", k.now("veA")),), inc=(("seA", 1),))
                M["sqdone"] = k.now("seA")
                sqk3 = sqk.rearrange("p (h d) -> p h d", d=HD)
                k.op("vector", lambda e: e.tensor_reduce(rmsd[:, 0:KVH], sqk3[:, :, :], AX.X, ALU.add),
                     wait=(("seA", k.now("seA")),))
                k.op("scalar", lambda e: e.activation(rmsd[:, 0:KVH], rmsd[:, 0:KVH], AF.Ln,
                                                      bias=eps_rms[:, 0:1], scale=1.0 / HD),
                     wait=(("veA", k.now("veA")),))
                k.op("scalar", lambda e: e.activation(rmsd[:, 0:KVH], rmsd[:, 0:KVH], AF.Exp, scale=-0.5),
                     inc=(("seA", 1),))
                sk = k.now("seA")
                for g in range(KVH):
                    k.op("vector", lambda e, g=g, par=par: e.tensor_scalar(
                        k_bf[par][:, g * HD:(g + 1) * HD], kcbuf[:, g * HD:(g + 1) * HD],
                        rmsd[:, g:g + 1], None, ALU.mult),
                        wait=(("seA", sk), ("poA", M.get("ropedone%d" % par, 0))) if g == 0 else (),
                        inc=(("veA", 1),) if g == KVH - 1 else ())
                vro = rope(k_bf[par], k_ro[par], KVH, ksin[:, tt * 256:(tt + 1) * 256], kcos[:, tt * 256:(tt + 1) * 256],
                           "poA", (("veA", k.now("veA")), ("pe", M.get("ktrans%d" % par, 0))), eng="gpsimd")
                M["ropedone%d" % par] = vro
                k.op("vector", lambda e, par=par, st_=st_: e.scalar_tensor_tensor(
                    vcbuf[:], psV[par][:], st_[:, 27:28], bb_v[:], ALU.mult, ALU.add),
                     wait=(("peA", M["pev%d" % par]),))
                k.op("vector", lambda e, par=par, st_=st_: e.scalar_tensor_tensor(
                    v_st[par][:], cs_v[:], st_[:, 28:29], vcbuf[:], ALU.mult, ALU.add),
                     wait=(("vsp%d" % par, k.now("vsp%d" % par)),), inc=(("veA", 1),))
                M["vpsfree%d" % par] = k.now("veA")
                vst3 = v_st[par].rearrange("p (g h) -> p g h", h=HD)
                vd_t = Vd[:, tt * 128:(tt + 1) * 128, :].rearrange("g p h -> p g h")
                k.dma("sync", vd_t, vst3[:, :, :], "vsp%d" % par, wait=(("veA", k.now("veA")),))

            def kv_s5b(tt):
                par = tt % 2
                k.op("scalar", lambda e, par=par: e.activation(kT_st[par][:], xnfP2[par][:, 0:KVD], AF.Identity),
                     wait=(("peA", M["ktrans%d" % par]), ("ksp%d" % par, k.now("ksp%d" % par))), inc=(("seA", 1),))
                M["ktcs%d" % par] = k.now("seA")
                kst3 = kT_st[par].rearrange("p (g q) -> p g q", q=128)
                kd_t = Kd[:, :, tt * 128:(tt + 1) * 128].rearrange("g p q -> p g q")
                k.dma("scalar", kd_t, kst3[:, :, :], "ksp%d" % par, wait=(("seA", M["ktcs%d" % par]),))

            kv_front(0, "dma")
            kv_front(0, "rest")
            kv_front(1, "dma")
            kv_front(1, "rest")
            kv_s2(0)
            for tt in range(NT):
                if tt + 2 < NT:
                    kv_front(tt + 2, "dma")
                if tt + 1 < NT:
                    kv_s2(tt + 1)
                kv_s34K(tt)
                kv_s4V(tt, ktrans_of=(tt - 1) if tt >= 1 else None)
                kv_s5a(tt)
                if tt >= 1:
                    kv_s5b(tt - 1)
                if tt % 4 == 3:
                    sched_s_load()
                    if nld[0] >= 2:
                        sched_s_reduce("scalar")
                if tt + 2 < NT:
                    kv_front(tt + 2, "rest")
            kparF = (NT - 1) % 2
            for g in range(KVH):
                wt = ()
                if g == 0:
                    wt = (("poA", M["ropedone%d" % kparF]),
                          ("seA", M.get("ktcs%d" % kparF, 0)),
                          ("veA", M.get("qspv%d" % kparF, 0)))
                k.op("tensor", lambda e, g=g, kparF=kparF: e.transpose(
                    xnfP2[kparF][:, g * 128:(g + 1) * 128], k_ro[kparF][:, g * 128:(g + 1) * 128], identb[:]),
                    wait=wt, inc=(("peA", 1),) if g == KVH - 1 else ())
            M["ktrans%d" % kparF] = k.now("peA")
            kv_s5b(NT - 1)
    KVSP = (("ksp0", k.now("ksp0")), ("ksp1", k.now("ksp1")),
            ("vsp0", k.now("vsp0")), ("vsp1", k.now("vsp1"))) + QS_DONE

    xn2_fm = SB(es, "xn2_fm", (128, NDC * TPC * 128), BF16)
    xn2f4 = xn2_fm.rearrange("p (c j q) -> p c j q", j=TPC, q=128)
    with contextlib.ExitStack() as stF:
        KBLK = [SB(stF, f"KBLK{i}", (128, 16 * 512), BF16) for i in range(2)]
        VBLK = [SB(stF, f"VBLK{i}", (128, 16 * 512), BF16) for i in range(2)]
        MBUF = SB(stF, "MBUF", (128, 4 * 1024), BF16)
        PT = [SB(stF, f"PT{i}", (128, 1024), BF16) for i in range(2)]
        Qst = [SB(stF, f"Qst{i}", (128, 256), BF16) for i in range(2)]
        aost = SB(stF, "aost", (128, 256), BF16)
        sum_g = SB(stF, "sum_g", (1, 256), F32)
        rec_b = SB(stF, "rec_b", (128, 256), F32)
        ST = [PS(stF, f"ST{i}", (128, 1024)) for i in range(2)]
        PV = [PS(stF, f"PV{i}", (128, 512)) for i in range(2)]
        SMPB = [PS(stF, f"SMPB{i}", (128, 512)) for i in range(2)]
        gstep = [0]
        sched_s_load()

        for a in range(4):
            K_a = 4 * a + 4
            gw = {}

            def load_g(a_, g_):
                gpar_ = (a_ * KVH + g_) % 2
                K_a_ = 4 * a_ + 4
                w = (("peF", M.get("gfree%d" % gpar_, 0)),)
                if a_ == 0 and g_ < 2:
                    w = w + KVSP + k.fence()
                k.dma("sync", KBLK[gpar_][:, 0:K_a_ * 512], Kd[g_, :, 0:K_a_ * 512], "kbl%d" % gpar_, wait=w)
                vsrc = Vd[g_, 0:K_a_ * 512, :].rearrange("(n p) h -> p n h", p=128)
                k.dma("sync", VBLK[gpar_].rearrange("p (n h) -> p n h", h=HD)[:, 0:K_a_ * 4, :], vsrc,
                      "kbl%d" % gpar_)
                gw[g_] = k.now("kbl%d" % gpar_)

            load_g(a, 0)
            k.dma("sync", MBUF.rearrange("p (n f) -> p n f", n=4)[:, :, :],
                  binm[a * 4:(a + 1) * 4, :, :].rearrange("n p f -> p n f"), "mld",
                  wait=(("veF", M.get("mfree", 0)),))
            MLD = k.now("mld")
            def qload(a_, h_):
                qp = h_ % 2
                k.dma("sync", Qst[qp][:], Qsc[:, h_ * 1024 + a_ * 256:h_ * 1024 + (a_ + 1) * 256],
                      "qld%d" % qp, wait=(("peF", M.get("quse%d" % qp, 0)),))
                M["qldc%d" % qp] = k.now("qld%d" % qp)

            qload(a, 0)
            for g in range(KVH):
                gpar = (a * KVH + g) % 2
                if g + 1 < KVH:
                    load_g(a, g + 1)
                for u in range(4):
                    h = g * 4 + u
                    hpar = h % 2
                    qpar = h % 2
                    sched_s_reduce("vector")
                    sched_s_reduce("vector")
                    sched_s_load()
                    sched_s_load()
                    if h + 1 < NH:
                        qload(a, h + 1)
                    QLD = M["qldc%d" % qpar]
                    base = gstep[0]

                    def fqk(i, kb, gpar=gpar, qpar=qpar, gw_=None, QLD=QLD, a=a):
                        par = i % 2
                        trim = (kb - 4 * a) >= 2
                        qo, qw_ = (128, 128) if trim else (0, 256)
                        w = (("seF", M.get("stfree%d" % par, 0)),
                             ("kbl%d" % gpar, gw_), ("qld%d" % qpar, QLD))
                        for kc in range(4):
                            k.op("tensor", lambda e, par=par, kc=kc, kb=kb, gpar=gpar, qpar=qpar,
                                 qo=qo, qw_=qw_: e.matmul(
                                ST[par][:, kc * 256 + qo:kc * 256 + qo + qw_],
                                KBLK[gpar][:, kb * 512 + kc * 128:kb * 512 + (kc + 1) * 128],
                                Qst[qpar][:, qo:qo + qw_], start=True, stop=True),
                                wait=w if kc == 0 else (),
                                inc=(("peF", 1),) if kc == 3 else ())
                        M["qk%d" % par] = k.now("peF")

                    def fexp(i, kb, h=h, a=a):
                        par = i % 2
                        ki = kb - 4 * a
                        trim = ki >= 2
                        wq = (("peF", M["qk%d" % par]), ("peF", M.get("ptfree%d" % par, 0)),
                              ("seF", k.now("seF")))
                        if trim:
                            pt3 = PT[par].rearrange("p (c q) -> p c q", q=256)
                            st3 = ST[par].rearrange("p (c q) -> p c q", q=256)
                            k.op("scalar", lambda e, par=par, h=h, pt3=pt3, st3=st3: e.activation(
                                pt3[:, :, 128:256], st3[:, :, 128:256], AF.Exp, bias=negm_bc[:, h:h + 1]),
                                wait=wq, inc=(("seF", 1),))
                            M["stfree%d" % par] = k.now("seF")
                            mb3 = MBUF.rearrange("p (n c q) -> p n c q", n=4, q=256)
                            k.op("vector", lambda e, par=par, ki=ki, pt3=pt3, mb3=mb3: e.tensor_tensor(
                                pt3[:, :, 128:256], pt3[:, :, 128:256], mb3[:, ki, :, 128:256], ALU.mult),
                                wait=(("seF", k.now("seF")), ("mld", MLD), ("veF", k.now("veF"))),
                                inc=(("veF", 1),))
                            M["ptdepA%d" % par] = ("veF", k.now("veF"))
                            M["ptdepB%d" % par] = ("veF", k.now("veF"))
                            return
                        k.op("scalar", lambda e, par=par, h=h: e.activation(
                            PT[par][:, 0:512], ST[par][:, 0:512], AF.Exp, bias=negm_bc[:, h:h + 1]),
                            wait=wq, inc=(("seF", 1),))
                        sA = k.now("seF")
                        k.op("scalar", lambda e, par=par, h=h: e.activation(
                            PT[par][:, 512:1024], ST[par][:, 512:1024], AF.Exp, bias=negm_bc[:, h:h + 1]),
                            inc=(("seF", 1),))
                        M["stfree%d" % par] = k.now("seF")
                        if ki >= 0:
                            k.op("vector", lambda e, par=par, ki=ki: e.tensor_mul(
                                PT[par][:, 0:512], PT[par][:, 0:512], MBUF[:, ki * 1024:ki * 1024 + 512]),
                                wait=(("seF", sA), ("mld", MLD), ("veF", k.now("veF"))),
                                inc=(("veF", 1),))
                            M["ptdepA%d" % par] = ("veF", k.now("veF"))
                            k.op("vector", lambda e, par=par, ki=ki: e.tensor_mul(
                                PT[par][:, 512:1024], PT[par][:, 512:1024], MBUF[:, ki * 1024 + 512:(ki + 1) * 1024]),
                                wait=(("seF", k.now("seF")),),
                                inc=(("veF", 1),))
                            M["ptdepB%d" % par] = ("veF", k.now("veF"))
                        else:
                            M["ptdepA%d" % par] = ("seF", sA)
                            M["ptdepB%d" % par] = ("seF", k.now("seF"))

                    def fpv(i, kb, K_a=K_a, gpar=gpar, hpar=hpar, a=a):
                        par = i % 2
                        first, last = (kb == 0), (kb == K_a - 1)
                        ki = kb - 4 * a
                        trim = ki >= 2
                        qo, qw_ = (128, 128) if trim else (0, 256)
                        wA = [M["ptdepA%d" % par]]
                        wB = [M["ptdepB%d" % par]]
                        if first:
                            wA.append(("veF", M.get("finfree%d" % hpar, 0)))
                        for kc in range(4):
                            w = (tuple(wA) if kc == 0 else ()) + (tuple(wB) if kc == 2 else ())
                            k.op("tensor", lambda e, par=par, kc=kc, kb=kb, gpar=gpar, hpar=hpar,
                                 first=first, last=last, qo=qo, qw_=qw_: e.matmul(
                                PV[hpar][:, qo:qo + qw_],
                                VBLK[gpar][:, (kb * 4 + kc) * 128:(kb * 4 + kc + 1) * 128],
                                PT[par][:, kc * 256 + qo:kc * 256 + qo + qw_],
                                start=(first and kc == 0), stop=(last and kc == 3)),
                                wait=w)
                        for kc in range(4):
                            k.op("tensor", lambda e, par=par, kc=kc, hpar=hpar, first=first, last=last,
                                 qo=qo, qw_=qw_: e.matmul(
                                SMPB[hpar][0:1, qo:qo + qw_], onesc_b[:, 0:1],
                                PT[par][:, kc * 256 + qo:kc * 256 + qo + qw_],
                                start=(first and kc == 0), stop=(last and kc == 3)),
                                inc=(("peF", 1),) if kc == 3 else ())
                        M["ptfree%d" % par] = k.now("peF")

                    fqk(base + 0, 0, gw_=gw[g])
                    fexp(base + 0, 0)
                    for kb in range(1, K_a):
                        fqk(base + kb, kb, gw_=gw[g])
                        fpv(base + kb - 1, kb - 1)
                        if kb == 1 and M.get("_fin") is not None:
                            M.pop("_fin")()
                        fexp(base + kb, kb)
                    fpv(base + K_a - 1, K_a - 1)
                    gstep[0] = base + K_a
                    M["quse%d" % qpar] = k.now("peF")

                    def finalize(a=a, h=h, hpar=hpar, qpar=qpar):
                        pw = k.now("peF")
                        k.op("vector", lambda e, hpar=hpar: e.tensor_scalar(
                            sum_g[:], SMPB[hpar][0:1, 0:256], guard_c[0:1, 0:1], None, ALU.max),
                            wait=(("peF", pw), ("veF", k.now("veF"))), inc=(("veF", 1),))
                        k.op("scalar", lambda e: e.activation(sum_g[:], sum_g[:], AF.Ln),
                             wait=(("veF", k.now("veF")), ("seF", k.now("seF"))), inc=(("seF", 1),))
                        k.op("scalar", lambda e: e.activation(sum_g[:], sum_g[:], AF.Exp, scale=-1.0),
                             wait=(("seF", k.now("seF")),), inc=(("seF", 1),))
                        k.op("tensor", lambda e, hpar=hpar: e.matmul(
                            SMPB[hpar][:, 256:512], onesr_f[0:1, :], sum_g[0:1, 0:256], start=True, stop=True),
                            wait=(("seF", k.now("seF")),), inc=(("peF", 1),))
                        k.op("vector", lambda e, hpar=hpar: e.tensor_copy(rec_b[:], SMPB[hpar][:, 256:512]),
                             wait=(("peF", k.now("peF")), ("veF", k.now("veF"))), inc=(("veF", 1),))
                        k.op("vector", lambda e, hpar=hpar: e.tensor_tensor(
                            aost[:], PV[hpar][:, 0:256], rec_b[:], ALU.mult),
                            wait=(("veF", k.now("veF")), ("aow", k.now("aow"))), inc=(("veF", 1),))
                        M["finfree%d" % hpar] = k.now("veF")
                        M["qfree%d" % qpar] = k.now("peF")
                        k.dma("sync", AOsc[:, h * 1024 + a * 256:h * 1024 + (a + 1) * 256], aost[:],
                              "aow", wait=(("veF", k.now("veF")),))

                    M["_fin"] = finalize
                M["gfree%d" % gpar] = k.now("peF")
            M["mfree"] = k.now("veF")
        if M.get("_fin") is not None:
            M.pop("_fin")()
        assert nld[0] == len(s_chunks), (nld[0], len(s_chunks))
        while nrd[0] < nld[0]:
            sched_s_reduce("vector")
    AO_DONE = k.now("aow")

    with contextlib.ExitStack() as stB:
        woTb = SB(stB, "woTb", (128, NDC * D), BF16)
        k.dma("gpsimd", woTb.rearrange("p (c f) -> p c f", c=NDC)[:, 0:8, :], woT[0:1024, :].rearrange("(c p) f -> p c f", p=128), "wld",
              wait=k.fence())
        k.dma("gpsimd", woTb.rearrange("p (c f) -> p c f", c=NDC)[:, 8:16, :], woT[1024:2048, :].rearrange("(c p) f -> p c f", p=128), "wld")
        WO_LD = k.now("wld")
        psO = PS(stB, "psO", (128, D))
        psT2 = PS(stB, "psT2", (128, D), BF16)
        aostg = [SB(stB, f"aostg{i}", (128, NH * 128), BF16) for i in range(2)]
        x2t = SB(stB, "x2t", (128, D), F32)
        radj = SB(stB, "radj", (128, D), F32)
        xn2b = SB(stB, "xn2b", (128, D), BF16)
        ao4 = AOsc.rearrange("p (h j q) -> p h j q", j=TPC, q=128)
        ybuf = SB(stB, "ybuf", (128, D), F32)

        def b2_dma(j):
            par = j % 2
            k.dma("sync", aostg[par][:], ao4[:, :, j, :], "aold%d" % par,
                  wait=(("aow", AO_DONE), ("peO", M.get("aofree%d" % par, 0)))
                  + (k.fence() if j < 2 else ()))
            M["ald%d" % par] = k.now("aold%d" % par)
            k.dma("sync", xstg[par][:], xo[j * 128:(j + 1) * 128, :], "bx%d" % par,
                  wait=(("veA", M.get("xfree%d" % par, 0)), ("veO", M.get("bxfree%d" % par, 0))))

        def b2_mm(j):
            par = j % 2
            do_tr = j >= 1
            for c in range(NDC):
                for n in range(4):
                    w = ()
                    if c == 0 and n == 0:
                        w = (("wld", WO_LD), ("veO", M.get("psOfree", 0)),
                             ("aold%d" % par, M["ald%d" % par]), ("seO", k.now("seO")))
                    k.op("tensor", lambda e, c=c, n=n, par=par: e.matmul(
                        psO[:, n * 512:(n + 1) * 512],
                        aostg[par][:, c * 128:(c + 1) * 128],
                        woTb[:, c * D + n * 512:c * D + (n + 1) * 512],
                        start=(c == 0), stop=False), wait=w)
                if do_tr and 8 <= c < 16:
                    for tsub in range(2):
                        tc = (c - 8) * 2 + tsub
                        wt = ()
                        if tc == 0:
                            wt = (("veO", M["xn2v"]), ("veO", M.get("pst2free", 0)))
                        k.op("tensor", lambda e, tc=tc: e.transpose(
                            psT2[:, tc * 128:(tc + 1) * 128], xn2b[:, tc * 128:(tc + 1) * 128], identb[:]),
                            wait=wt, inc=(("peO", 1),) if tc == NDC - 1 else ())
            if do_tr:
                M["trdone"] = k.now("peO")
            for n in range(4):
                k.op("tensor", lambda e, n=n: e.matmul(
                    psO[:, n * 512:(n + 1) * 512], onesr_b[0:1, :],
                    biasr[0:1, BO0 + n * 512:BO0 + (n + 1) * 512],
                    start=False, stop=True),
                    inc=(("peO", 1),) if n == 3 else ())
            M["aofree%d" % par] = k.now("peO")
            if do_tr:
                pst3 = psT2.rearrange("p (c q) -> p c q", q=128)
                k.op("vector", lambda e, j=j: e.tensor_copy(xn2f4[:, :, j - 1, :], pst3[:, :, :]),
                     wait=(("peO", M["trdone"]), ("veO", k.now("veO"))), inc=(("veO", 1),))
                M["pst2free"] = k.now("veO")
            k.op("vector", lambda e: e.tensor_copy(ybuf[:], psO[:]),
                 wait=(("peO", M["aofree%d" % par]), ("veO", k.now("veO")), ("seO", k.now("seO"))),
                 inc=(("veO", 1),))
            M["psOfree"] = k.now("veO")

        def b2_post(j):
            par = j % 2
            vev = k.now("veO")
            k.op("vector", lambda e: e.tensor_reduce(stat[:, 0:1], ybuf[:], AX.X, ALU.add),
                 wait=(("veO", vev), ("veA", k.now("veA")), ("seA", k.now("seA"))),
                 inc=(("veO", 1),))
            k.op("scalar", lambda e: e.activation(sqsc[:], ybuf[:], AF.Square, accum_out=stat[:, 1:2]),
                 wait=(("veO", vev), ("seO", k.now("seO")), ("seA", k.now("seA"))),
                 inc=(("seO", 1),))
            vw, sw = k.now("veO"), k.now("seO")
            k.op("vector", lambda e: e.tensor_scalar(stat[:, 2:3], stat[:, 0:1], 1.0 / D, None, ALU.mult),
                 wait=(("veO", vw), ("seO", sw)))
            k.op("vector", lambda e: e.tensor_scalar(stat[:, 3:4], stat[:, 1:2], 1.0 / D, None, ALU.mult),
                 inc=(("veO", 1),))
            k.op("vector", lambda e: e.tensor_mul(stat[:, 4:5], stat[:, 2:3], stat[:, 2:3]),
                 wait=(("veO", k.now("veO")),))
            k.op("vector", lambda e: e.tensor_sub(stat[:, 4:5], stat[:, 3:4], stat[:, 4:5]),
                 inc=(("veO", 1),))
            k.op("scalar", lambda e: e.activation(stat[:, 5:6], stat[:, 4:5], AF.Ln, bias=eps_ln[:, 0:1]),
                 wait=(("veO", k.now("veO")), ("seO", k.now("seO"))), inc=(("seO", 1),))
            k.op("scalar", lambda e: e.activation(stat[:, 5:6], stat[:, 5:6], AF.Exp, scale=-0.5),
                 wait=(("seO", k.now("seO")),), inc=(("seO", 1),))
            k.op("vector", lambda e: e.scalar_tensor_tensor(stat[:, 6:7], stat[:, 2:3], -1.0, stat[:, 5:6],
                                                            ALU.mult, ALU.mult),
                 wait=(("seO", k.now("seO")),), inc=(("veO", 1),))
            k.op("vector", lambda e, par=par: e.tensor_scalar(radj[:], xstg[par][:], stat[:, 6:7], None, ALU.add),
                 wait=(("bx%d" % par, k.now("bx%d" % par)), ("veO", k.now("veO"))), inc=(("veO", 1),))
            M["bxfree%d" % par] = k.now("veO")
            k.op("vector", lambda e: e.tensor_scalar(x2t[:], ybuf[:], stat[:, 5:6], None, ALU.mult),
                 wait=(("x2w", k.now("x2w")), ("veO", k.now("veO"))))
            k.op("vector", lambda e: e.tensor_add(x2t[:], x2t[:], radj[:]), inc=(("veO", 1),))
            k.dma("sync", X2d[j * 128:(j + 1) * 128, :], x2t[:], "x2w", wait=(("veO", k.now("veO")),))
            vln2 = ln_tile(x2t[:], xn2b[:], "veO", "seO", extra_wait=(("veO", k.now("veO")),))
            M["xn2v"] = vln2

        b2_dma(0)
        b2_dma(1)
        for j in range(TPC):
            b2_mm(j)
            b2_post(j)
            if j + 2 < TPC:
                b2_dma(j + 2)
        for tc in range(NDC):
            wt = (("veO", M["xn2v"]), ("veO", M.get("pst2free", 0))) if tc == 0 else ()
            k.op("tensor", lambda e, tc=tc: e.transpose(
                psT2[:, tc * 128:(tc + 1) * 128], xn2b[:, tc * 128:(tc + 1) * 128], identb[:]),
                wait=wt, inc=(("peO", 1),) if tc == NDC - 1 else ())
        pst3 = psT2.rearrange("p (c q) -> p c q", q=128)
        k.op("vector", lambda e: e.tensor_copy(xn2f4[:, :, TPC - 1, :], pst3[:, :, :]),
             wait=(("peO", k.now("peO")), ("veO", k.now("veO"))), inc=(("veO", 1),))
    X2W = k.now("x2w")
    XN2_DONE = (("veO", k.now("veO")),)

    while nld[0] < len(s_chunks):
        sched_s_load()
        sched_s_reduce("vector")
    while nrd[0] < nld[0]:
        sched_s_reduce("vector")
    with contextlib.ExitStack() as st1:
        psS = PS(st1, "psS", (1, 4))
        psS2 = PS(st1, "psS2", (128, 4))
        s_row = SB(st1, "s_row", (1, 4), F32)
        k.op("tensor", lambda e: e.matmul(psS[0:1, 0:3], onesc_f[:, 0:1], sS[:, 0:3], start=True, stop=True),
             wait=(("poS", k.now("poS")),), inc=(("peC", 1),))
        k.op("vector", lambda e: e.tensor_scalar(s_row[0:1, 0:3], psS[0:1, 0:3], INW1, None, ALU.mult),
             wait=(("peC", k.now("peC")),))
        k.op("vector", lambda e: e.scalar_tensor_tensor(s_row[0:1, 0:3], psS[0:1, 0:3], INW2, s_row[0:1, 0:3],
                                                        ALU.mult, ALU.add),
             inc=(("veC", 1),))
        k.op("tensor", lambda e: e.matmul(psS2[:, 0:3], onesr_f[0:1, :], s_row[0:1, 0:3], start=True, stop=True),
             wait=(("veC", k.now("veC")),), inc=(("peC", 1),))
        k.op("vector", lambda e: e.tensor_copy(s_bc[:, 0:3], psS2[:, 0:3]),
             wait=(("peC", k.now("peC")),), inc=(("veC", 1),))
        k.op("vector", lambda e: e.tensor_scalar(th_p[:, 0:3], s_bc[:, 0:3], 0.5, None, ALU.mult),
             wait=(("veC", k.now("veC")),))
        k.op("vector", lambda e: e.tensor_scalar(th_n[:, 0:3], s_bc[:, 0:3], -0.5, None, ALU.mult),
             inc=(("veC", 1),))
    TH = k.now("veC")

    with contextlib.ExitStack() as stA:
        w13stg = [SB(stA, f"w13stg{i}", (128, 2 * D)) for i in range(2)]
        q13 = [SB(stA, f"q13_{i}", (128, 2 * D), BF16) for i in range(2)]
        posb = SB(stA, "posb", (128, D), BF16)
        negb = SB(stA, "negb", (128, D), BF16)
        silu_s = SB(stA, "silu_s", (128, 1024), F32)
        g_st = SB(stA, "g_st", (128, 1024), BF16)
        psH1 = [PS(stA, f"psH1{i}", (128, 1024)) for i in range(2)]
        psH3 = [PS(stA, f"psH3{i}", (128, 1024)) for i in range(2)]
        def a_front(hc):
            par = hc % 2
            hcp = 128 if hc < NHC - 1 else LAST_HC
            for half, (wTsrc, thi, qu, snm) in enumerate(((w1T, 0, "sync", "wa"),
                                                          (w3T, 2, "scalar", "wc"))):
                sem = "%s%d" % (snm, par)
                wc = wTsrc[:, hc * 128:hc * 128 + hcp].rearrange("(c p) h -> p c h", p=128)
                w13dst = w13stg[par][:, half * D:half * D + NDC * hcp].rearrange("p (c h) -> p c h", h=hcp)
                k.dma(qu, w13dst[:, :, :], wc, sem,
                      wait=(("veC", M.get("wafree%d_%d" % (par, half), 0)),) + (k.fence() if hc < 2 else ()))
                WL = k.now(sem)
                qw = ((sem, WL), ("veC", TH), ("poS", k.now("poS")))
                k.op("vector", lambda e, par=par, hcp=hcp, thi=thi, half=half: e.tensor_scalar(
                    posb[:, 0:NDC * hcp], w13stg[par][:, half * D:half * D + NDC * hcp], th_p[:, thi:thi + 1], None, ALU.is_gt),
                    wait=qw)
                k.op("vector", lambda e, par=par, hcp=hcp, thi=thi, half=half: e.tensor_scalar(
                    negb[:, 0:NDC * hcp], w13stg[par][:, half * D:half * D + NDC * hcp], th_n[:, thi:thi + 1], None, ALU.is_lt),
                    inc=(("veC", 1),))
                M["wafree%d_%d" % (par, half)] = k.now("veC")
                k.op("gpsimd", lambda e, par=par, hcp=hcp, half=half: e.tensor_sub(
                    q13[par][:, half * D:half * D + NDC * hcp], posb[:, 0:NDC * hcp], negb[:, 0:NDC * hcp]),
                    wait=(("veC", k.now("veC")), ("poS", k.now("poS")),
                          ("peC", M.get("q13free%d" % par, 0))),
                    inc=(("poS", 1),))
            M["qdone%d" % par] = k.now("poS")

        def a_mm(hc):
            par = hc % 2
            hcp = 128 if hc < NHC - 1 else LAST_HC
            w0 = (("poS", M["qdone%d" % par]),) + XN2_DONE + (("seC", M.get("hafree%d" % par, 0)),
                                                             ("veC", M.get("gstt%d" % par, 0)))
            for c in range(NDC):
                for n in range(2):
                    k.op("tensor", lambda e, par=par, c=c, n=n, hcp=hcp: e.matmul(
                        psH1[par][0:hcp, n * 512:(n + 1) * 512],
                        q13[par][:, c * hcp:(c + 1) * hcp],
                        xn2_fm[:, c * 1024 + n * 512:c * 1024 + (n + 1) * 512],
                        start=(c == 0), stop=(c == NDC - 1)),
                        wait=w0 if (c == 0 and n == 0) else ())
            for c in range(NDC):
                for n in range(2):
                    k.op("tensor", lambda e, par=par, c=c, n=n, hcp=hcp: e.matmul(
                        psH3[par][0:hcp, n * 512:(n + 1) * 512],
                        q13[par][:, D + c * hcp:D + (c + 1) * hcp],
                        xn2_fm[:, c * 1024 + n * 512:c * 1024 + (n + 1) * 512],
                        start=(c == 0), stop=(c == NDC - 1)),
                        inc=(("peC", 1),) if (c == NDC - 1 and n == 1) else ())
            M["q13free%d" % par] = k.now("peC")
            M["amm%d" % par] = k.now("peC")

        def a_epi(hc):
            par = hc % 2
            hcp = 128 if hc < NHC - 1 else LAST_HC
            k.op("scalar", lambda e, par=par, hcp=hcp: e.activation(
                silu_s[0:hcp, :], psH1[par][0:hcp, :], AF.Silu, scale=s_bc[0:hcp, 0:1]),
                wait=(("peC", M["amm%d" % par]), ("seC", k.now("seC")), ("veC", M.get("gw", 0))),
                inc=(("seC", 1),))
            M["hafree%d" % par] = k.now("seC")
            k.op("vector", lambda e, par=par, hc=hc, hcp=hcp: e.scalar_tensor_tensor(
                g_st[0:hcp, :], psH3[par][0:hcp, :], s_bc[0:hcp, 2:3], silu_s[0:hcp, :], ALU.mult, ALU.mult),
                wait=(("seC", k.now("seC")), ("gws", k.now("gws")), ("veC", k.now("veC"))),
                inc=(("veC", 1),))
            M["gstt%d" % par] = k.now("veC")
            M["gw"] = k.now("veC")
            k.dma("sync", Gsc[hc * 128:hc * 128 + hcp, :], g_st[0:hcp, :], "gws",
                  wait=(("veC", k.now("veC")),))

        a_front(0)
        a_front(1)
        for hc in range(NHC):
            a_mm(hc)
            if hc + 2 < NHC:
                a_front(hc + 2)
            a_epi(hc)
        G_DONE = k.now("gws")

    with contextlib.ExitStack() as stBB:
        w2stg = [SB(stBB, f"w2stg{i}", (128, 512)) for i in range(2)]
        q2b = [SB(stBB, f"q2b{i}", (128, 512), BF16) for i in range(2)]
        pos2 = SB(stBB, "pos2", (128, 512), BF16)
        neg2 = SB(stBB, "neg2", (128, 512), BF16)
        gstg = [SB(stBB, f"gstg{i}", (128, 1024), BF16) for i in range(2)]
        y2ev = SB(stBB, "y2ev", (128, 512), F32)
        psY = [PS(stBB, f"psY{i}", (128, 512)) for i in range(TPC)]
        def b_front(dc, hc):
            par = hc % 2
            hcp = 128 if hc < NHC - 1 else LAST_HC
            k.dma("sync", w2stg[par][0:hcp, :], w2T[hc * 128:hc * 128 + hcp, dc * 512:(dc + 1) * 512],
                  "wb%d" % par, wait=(("veC", M.get("wbfree%d" % par, 0)),)
                  + (k.fence() if dc == 0 and hc < 2 else ()))
            k.dma("scalar", gstg[par][0:hcp, :], Gsc[hc * 128:hc * 128 + hcp, :],
                  "gl%d" % par, wait=(("gws", G_DONE), ("peC", M.get("glfree%d" % par, 0)))
                  + (k.fence() if dc == 0 and hc < 2 else ()))
            M["gld%d" % par] = k.now("gl%d" % par)
            qw = (("wb%d" % par, k.now("wb%d" % par)), ("veC", TH),
                  ("poS", k.now("poS")))
            k.op("vector", lambda e, par=par, hcp=hcp: e.tensor_scalar(
                pos2[0:hcp, :], w2stg[par][0:hcp, :], th_p[0:hcp, 1:2], None, ALU.is_gt), wait=qw)
            k.op("vector", lambda e, par=par, hcp=hcp: e.tensor_scalar(
                neg2[0:hcp, :], w2stg[par][0:hcp, :], th_n[0:hcp, 1:2], None, ALU.is_lt),
                inc=(("veC", 1),))
            M["wbfree%d" % par] = k.now("veC")
            k.op("gpsimd", lambda e, par=par, hcp=hcp: e.tensor_sub(
                q2b[par][0:hcp, :], pos2[0:hcp, :], neg2[0:hcp, :]),
                wait=(("veC", k.now("veC")), ("poS", k.now("poS")),
                      ("peC", M.get("q2free%d" % par, 0))),
                inc=(("poS", 1),))
            M["q2done%d" % par] = k.now("poS")

        def b_mm(dc, hc):
            par = hc % 2
            hcp = 128 if hc < NHC - 1 else LAST_HC
            w0 = (("poS", M["q2done%d" % par]), ("gl%d" % par, M["gld%d" % par]))
            if hc == 0:
                w0 = w0 + (("veC", M.get("yev", 0)),)
            for j in range(TPC):
                k.op("tensor", lambda e, par=par, j=j, hc=hc, hcp=hcp: e.matmul(
                    psY[j][:], gstg[par][0:hcp, j * 128:(j + 1) * 128], q2b[par][0:hcp, :],
                    start=(hc == 0), stop=(hc == NHC - 1)),
                    wait=w0 if j == 0 else (),
                    inc=(("peC", 1),) if j == TPC - 1 else ())
            M["q2free%d" % par] = k.now("peC")
            M["glfree%d" % par] = k.now("peC")

        for dc in range(4):
            b_front(dc, 0)
            b_front(dc, 1)
            for hc in range(NHC):
                b_mm(dc, hc)
                if hc + 2 < NHC:
                    b_front(dc, hc + 2)
            pev = k.now("peC")
            for j in range(TPC):
                k.op("vector", lambda e, j=j, dc=dc: e.tensor_scalar(
                    y2ev[:], psY[j][:], s_bc[:, 1:2], None, ALU.mult),
                    wait=(("peC", pev), ("y2w", k.now("y2w")), ("veC", k.now("veC"))), inc=(("veC", 1),))
                k.dma("sync", Y2sc[:, j * D + dc * 512:j * D + (dc + 1) * 512], y2ev[:], "y2w",
                      wait=(("veC", k.now("veC")),))
            M["yev"] = k.now("veC")
        Y2W = k.now("y2w")

    with contextlib.ExitStack() as stE:
        y2stg = [SB(stE, f"y2stg{i}", (128, D)) for i in range(2)]
        lnout = SB(stE, "lnout", (128, D), F32)
        outt = SB(stE, "outt", (128, D), F32)
        for j in range(TPC):
            par = j % 2
            k.dma("sync", xstg[par][:], X2d[j * 128:(j + 1) * 128, :], "ex%d" % par,
                  wait=(("x2w", X2W), ("veO", M.get("bxfree%d" % par, 0)),
                        ("veC", M.get("exfree%d" % par, 0))) + (k.fence() if j < 2 else ()))
            k.dma("scalar", y2stg[par][:], Y2sc[:, j * D:(j + 1) * D], "ey%d" % par,
                  wait=(("y2w", Y2W), ("veC", M.get("eyfree%d" % par, 0))) + (k.fence() if j < 2 else ()))
            vln3 = ln_tile(y2stg[par][:], lnout[:], "veC", "seC",
                           extra_wait=(("ey%d" % par, k.now("ey%d" % par)),
                                       ("veC", k.now("veC")), ("veO", k.now("veO")),
                                       ("seO", k.now("seO"))))
            M["eyfree%d" % par] = vln3
            k.op("vector", lambda e, par=par: e.tensor_tensor(outt[:], xstg[par][:], lnout[:], ALU.add),
                 wait=(("veC", vln3), ("ex%d" % par, k.now("ex%d" % par)),
                       ("outw", k.now("outw"))),
                 inc=(("veC", 1),))
            M["exfree%d" % par] = k.now("veC")
            k.dma("sync", out[j * 128:(j + 1) * 128, :], outt[:], "outw", wait=(("veC", k.now("veC")),))
    k.op("sync", lambda e: e.wait_ge(k.sems["outw"], k.now("outw")))

    block = es.enter_context(nc.Block())
    k.emit(block)
    es.close()
    return nc


def _trig_tables(pos):
    invf = 10000.0 ** (-np.arange(64, dtype=np.float64) / 64.0)
    fr = pos[:, :, None] * invf[None, None, :]
    k2s = np.round(fr / np.pi)
    sgs = 1.0 - 2.0 * (np.abs(k2s) % 2)
    k2c = np.round((fr + np.pi / 2) / np.pi)
    sgc = 1.0 - 2.0 * (np.abs(k2c) % 2)
    tob = lambda a: np.ascontiguousarray(a.transpose(1, 0, 2).reshape(128, -1).astype(np.float32))
    return tob(k2s), tob(sgs), tob(k2c), tob(sgc)


def _prep(inputs):
    x = np.ascontiguousarray(np.asarray(inputs["L0_attn_x"], np.float32).reshape(S, D))
    d = {}
    d["wqT"] = np.ascontiguousarray(np.asarray(inputs["L0_attn_Wq"], np.float32).T)
    d["wkT"] = np.ascontiguousarray(np.asarray(inputs["L0_attn_Wk"], np.float32).T)
    d["wvT"] = np.ascontiguousarray(np.asarray(inputs["L0_attn_Wv"], np.float32).T)
    d["woT"] = np.ascontiguousarray(np.asarray(inputs["L0_attn_Wo"], np.float32).T)
    d["bqr"] = np.asarray(inputs["L0_attn_bq"], np.float32).reshape(1, D)
    d["bkr"] = np.asarray(inputs["L0_attn_bk"], np.float32).reshape(1, KVD)
    d["bvr"] = np.asarray(inputs["L0_attn_bv"], np.float32).reshape(1, KVD)
    d["bor"] = np.asarray(inputs["L0_attn_bo"], np.float32).reshape(1, D)
    d["gain"] = np.asarray(inputs["L0_attn_q_gain"], np.float32).reshape(1, NH)
    d["w1T"] = np.ascontiguousarray(np.asarray(inputs["L0_ffn_w1"], np.float32).T)
    d["w3T"] = np.ascontiguousarray(np.asarray(inputs["L0_ffn_w3"], np.float32).T)
    d["w2T"] = np.ascontiguousarray(np.asarray(inputs["L0_ffn_w2"], np.float32).T)
    d["identity"] = np.eye(128).astype(ml_dtypes.bfloat16)
    d["irow"] = np.arange(64, dtype=np.float32).reshape(1, 64)
    pos = np.arange(S, dtype=np.float64).reshape(NT, 128)
    d["k2s"], d["sgs"], d["k2c"], d["sgc"] = _trig_tables(pos)
    d["tokix"] = np.ascontiguousarray(pos.T.astype(np.float32))
    d["_xfTt"] = np.ascontiguousarray(x.T.reshape(D, NT, 128).transpose(1, 0, 2))
    return x, d


def _core_inputs(x, shared, s_):
    xt = x.reshape(NT, 128, D)
    pos = np.arange(S, dtype=np.float64).reshape(NT, 128)
    m = dict(shared)
    m["xf"] = x
    m["xfTt"] = shared["_xfTt"]
    m.pop("_xfTt", None)
    own = np.arange(s_, NT, NCORE)
    m["xo"] = np.ascontiguousarray(xt[own].reshape(TPC * 128, D))
    m["xoTt"] = np.ascontiguousarray(xt[own].transpose(0, 2, 1))
    posq = pos[own]
    m["tokq"] = np.ascontiguousarray(posq.T.astype(np.float32))
    m["qk2s"], m["qsgs"], m["qk2c"], m["qsgc"] = _trig_tables(posq)
    bn = np.zeros((16, 128, 1024), np.float32)
    for a in range(4):
        for ki in range(4):
            kb = 4 * a + ki
            for kc in range(4):
                kglob = kb * 512 + kc * 128 + np.arange(128)
                for sub in range(2):
                    jslot = 2 * a + sub
                    qglob = (8 * jslot + s_) * 128 + np.arange(128)
                    bn[a * 4 + ki, :, kc * 256 + sub * 128:kc * 256 + (sub + 1) * 128] = (
                        kglob[:, None] <= qglob[None, :])
    m["binm"] = bn.astype(ml_dtypes.bfloat16)
    return m


def kernel(**inputs):
    x, shared = _prep(inputs)
    nc = build_nc(debug=False)
    in_maps = [_core_inputs(x, shared, s_) for s_ in range(NCORE)]
    res = run_bass_kernel_spmd(nc, in_maps, core_ids=list(range(NCORE)))
    outf = np.empty((S, D), np.float32)
    outv = outf.reshape(NT, 128, D)
    for s_ in range(NCORE):
        outv[s_::NCORE] = np.asarray(res.results[s_]["out"]).reshape(TPC, 128, D)
    return outf.reshape(1, S, D)


# version 17
# speedup vs baseline: 1.3700x; 1.0420x over previous
import contextlib
import numpy as np
import ml_dtypes
import concourse.bass as bass
import concourse.mybir as mybir
from concourse.bass_utils import run_bass_kernel_spmd

F32 = mybir.dt.float32
BF16 = mybir.dt.bfloat16
AF = mybir.ActivationFunctionType
ALU = mybir.AluOpType
AX = mybir.AxisListType

NCORE = 8
S, D, HID = 8192, 2048, 5461
NH, KVH, HD = 16, 4, 128
KVD = KVH * HD
NT = S // 128
TPC = NT // NCORE
NDC = D // 128
NHC = (HID + 127) // 128
LAST_HC = HID - 128 * (NHC - 1)
EPS_LN = 1e-5
EPS_RMS = 1.1920929e-07
GB_MUL = 11.32
N_W = HID * D


def _split2(v):
    c1 = np.float32(v)
    return float(c1), float(np.float32(np.float64(v) - np.float64(c1)))


PI1 = float(np.float32(3.140625))
_pr = np.float64(np.pi) - np.float64(PI1)
PI2 = float(np.float32(_pr))
PI3 = float(np.float32(_pr - np.float64(np.float32(_pr))))
HP1, HP2 = _split2(np.pi / 2)
LNB1, LNB2 = _split2(-np.log(10000.0) / 64.0)
ISQ1, ISQ2 = _split2(1.0 / np.sqrt(128.0))
INW1, INW2 = _split2(1.0 / N_W)


class Sched:
    ENGS = ("sync", "gpsimd", "vector", "scalar", "tensor")

    def __init__(self, nc, es):
        self.nc, self.es = nc, es
        self.ops = {e: [] for e in self.ENGS}
        self.dma_sems = set()
        self.cnt = {}
        self.mark = {}
        self.sems = {}

    def _sem(self, name):
        if name not in self.sems:
            self.sems[name] = self.es.enter_context(self.nc.semaphore(name))
            self.cnt[name] = 0
        return self.sems[name]

    def now(self, name):
        name = self._tr(name)
        self._sem(name)
        return self.cnt[name]

    AUTO = {"vector": "auV", "scalar": "auS", "gpsimd": "auG"}

    @staticmethod
    def _tr(n):
        if n.startswith("ve"):
            return "auV"
        if n.startswith("se"):
            return "auS"
        if n.startswith("po"):
            return "auG"
        if n.startswith("pe"):
            return "pe"
        return n

    def op(self, eng, fn, wait=(), inc=(), dma=False):
        wait = tuple((self._tr(n), v) for n, v in wait)
        inc = tuple((self._tr(n), kk) for n, kk in inc
                    if self._tr(n) not in ("auV", "auS", "auG"))
        if not dma and eng in self.AUTO:
            au = self.AUTO[eng]
            self._sem(au)
            wait = wait + ((au, self.cnt[au]),)
            inc = inc + ((au, 1),)
        seen = set()
        wait2 = []
        for n, v in wait:
            self._sem(n)
            assert isinstance(v, int), (n, v)
            if (n, v) not in seen:
                seen.add((n, v))
                wait2.append((n, v))
        best = {}
        for n, v in wait2:
            best[n] = max(best.get(n, -1), v)
        wait = tuple(best.items())
        for n, kk in inc:
            self._sem(n)
            self.cnt[n] += kk
        self.ops[eng].append((fn, wait, inc))

    def dma(self, eng, out, in_, sem, wait=()):
        self.dma_sems.add(sem)
        self.op(eng, lambda e, o=out, i=in_: e.dma_start(out=o, in_=i),
                wait=wait, inc=((sem, 16),), dma=True)

    def fence(self):
        names = {"auV", "auS", "auG", "pe"} | self.dma_sems
        return tuple((n, self.cnt[n]) for n in names if n in self.cnt)

    def emit(self, block):
        for eng_name in self.ENGS:
            items = self.ops[eng_name]
            if not items:
                continue

            def body(e, items=items):
                lw = {}
                for fn, waits, incs in items:
                    for n, v in waits:
                        if lw.get(n, -1) < v:
                            e.wait_ge(self.sems[n], v)
                            lw[n] = v
                    instr = fn(e)
                    for n, kk in incs:
                        instr = instr.then_inc(self.sems[n], kk)

            getattr(block, eng_name)(body)


def build_nc(debug=False):
    nc = bass.Bass("TRN2", target_bir_lowering=False)
    t = lambda name, shape, dt=F32: nc.dram_tensor(name, list(shape), dt, kind="ExternalInput").ap()
    xf = t("xf", (S, D))
    xfTt = t("xfTt", (NT, D, 128))
    xo = t("xo", (TPC * 128, D))
    xoTt = t("xoTt", (TPC, D, 128))
    wqT = t("wqT", (D, D)); wkT = t("wkT", (D, KVD)); wvT = t("wvT", (D, KVD))
    woT = t("woT", (D, D))
    bqr = t("bqr", (1, D)); bkr = t("bkr", (1, KVD)); bvr = t("bvr", (1, KVD)); bor = t("bor", (1, D))
    gain = t("gain", (1, NH))
    w1T = t("w1T", (D, HID)); w3T = t("w3T", (D, HID)); w2T = t("w2T", (HID, D))
    identity = t("identity", (128, 128), BF16)
    irow = t("irow", (1, 64))
    tokix = t("tokix", (128, NT))
    tokq = t("tokq", (128, TPC))
    k2s = t("k2s", (128, NT * 64)); sgs = t("sgs", (128, NT * 64))
    k2c = t("k2c", (128, NT * 64)); sgc = t("sgc", (128, NT * 64))
    qk2s = t("qk2s", (128, TPC * 64)); qsgs = t("qsgs", (128, TPC * 64))
    qk2c = t("qk2c", (128, TPC * 64)); qsgc = t("qsgc", (128, TPC * 64))
    binm = t("binm", (16, 128, 1024), BF16)
    out = nc.dram_tensor("out", [TPC * 128, D], F32, kind="ExternalOutput").ap()
    skind = "ExternalOutput" if debug else "Internal"
    Kd = nc.dram_tensor("Kd", [KVH, 128, S], BF16, kind=skind).ap()
    Vd = nc.dram_tensor("Vd", [KVH, S, HD], BF16, kind=skind).ap()
    Qsc = nc.dram_tensor("Qsc", [128, NH * TPC * 128], BF16, kind=skind).ap()
    AOsc = nc.dram_tensor("AOsc", [128, NH * TPC * 128], BF16, kind=skind).ap()
    X2d = nc.dram_tensor("X2d", [TPC * 128, D], F32, kind=skind).ap()
    Gsc = nc.dram_tensor("Gsc", [NHC * 128, TPC * 128], BF16, kind=skind).ap()
    Y2sc = nc.dram_tensor("Y2sc", [128, TPC * D], F32, kind=skind).ap()

    es = contextlib.ExitStack()
    k = Sched(nc, es)
    M = k.mark

    def SB(st, n, s_, d=F32):
        return st.enter_context(nc.sbuf_tensor(n, list(s_), d))

    def PS(st, n, s_, d=F32):
        return st.enter_context(nc.psum_tensor(n, list(s_), d))

    identb = SB(es, "identb", (128, 128), BF16)
    onesc_b = SB(es, "onesc_b", (128, 1), BF16)
    onesr_b = SB(es, "onesr_b", (1, 128), BF16)
    onesr_f = SB(es, "onesr_f", (1, 128), F32)
    onesc_f = SB(es, "onesc_f", (128, 1), F32)
    eps_ln = SB(es, "eps_ln", (128, 1), F32)
    eps_rms = SB(es, "eps_rms", (128, 1), F32)
    guard_c = SB(es, "guard_c", (1, 1), F32)
    biasr = SB(es, "biasr", (1, 2 * D + 2 * KVD), BF16)
    BQ0, BK0, BV0, BO0 = 0, D, D + KVD, D + 2 * KVD
    gain_r = SB(es, "gain_r", (1, NH), F32)
    negm_row = SB(es, "negm_row", (1, NH), F32)
    negm_bc = SB(es, "negm_bc", (128, NH), F32)
    mult_g = SB(es, "mult_g", (1, NH), F32)
    gs_bc = SB(es, "gs_bc", (128, NH), F32)
    sS = SB(es, "sS", (128, 4), F32)
    th_p = SB(es, "th_p", (128, 4), F32)
    th_n = SB(es, "th_n", (128, 4), F32)
    s_bc = SB(es, "s_bc", (128, 4), F32)
    invf_r = SB(es, "invf_r", (1, 64), F32)
    invfb = SB(es, "invfb", (128, 64), F32)
    irow_s = SB(es, "irow_s", (1, 64), F32)
    tokq_s = SB(es, "tokq_s", (128, TPC), F32)
    tokk_s = SB(es, "tokk_s", (128, NT), F32)
    stat = SB(es, "stat", (128, 8), F32)
    sqsc = SB(es, "sqsc", (128, D), F32)
    xstg = [SB(es, f"xstg{i}", (128, D)) for i in range(2)]
    rmsd = SB(es, "rmsd", (128, NH), F32)
    ropeu = SB(es, "ropeu", (128, NH * 64), BF16)
    ropev = SB(es, "ropev", (128, NH * 64), BF16)
    ropeu3 = ropeu.rearrange("p (h d) -> p h d", d=64)
    ropev3 = ropev.rearrange("p (h d) -> p h d", d=64)

    k.dma("sync", identb[:], identity, "cld")
    k.dma("sync", gain_r[:], gain, "cld")
    k.dma("sync", irow_s[:], irow, "cld")
    k.dma("sync", tokq_s[:], tokq, "cld")
    k.dma("sync", tokk_s[:], tokix, "cld")
    CLD = k.now("cld")
    for fn in (lambda e: e.memset(onesc_b[:], 1.0), lambda e: e.memset(onesr_b[:], 1.0),
               lambda e: e.memset(onesr_f[:], 1.0), lambda e: e.memset(onesc_f[:], 1.0),
               lambda e: e.memset(eps_ln[:], EPS_LN), lambda e: e.memset(eps_rms[:], EPS_RMS),
               lambda e: e.memset(guard_c[:], 1e-37)):
        k.op("vector", fn)
    k.op("vector", lambda e: e.memset(sS[:], 0.0))
    M_SS = k.now("veA")
    k.dma("gpsimd", biasr[0:1, BQ0:BQ0 + D], bqr, "bld")
    k.dma("gpsimd", biasr[0:1, BK0:BK0 + KVD], bkr, "bld")
    k.dma("gpsimd", biasr[0:1, BV0:BV0 + KVD], bvr, "bld")
    k.dma("gpsimd", biasr[0:1, BO0:BO0 + D], bor, "bld")
    BLD = k.now("bld")
    k.op("vector", lambda e: e.tensor_scalar(mult_g[:], gain_r[:], ISQ1, None, ALU.mult),
         wait=(("cld", CLD), ("veA", k.now("veA"))))
    k.op("vector", lambda e: e.scalar_tensor_tensor(mult_g[:], gain_r[:], ISQ2, mult_g[:], ALU.mult, ALU.add),
         inc=(("veA", 1),))
    k.op("vector", lambda e: e.tensor_scalar(negm_row[:], gain_r[:], -1.0, None, ALU.mult),
         wait=(("veA", k.now("veA")),))
    k.op("vector", lambda e: e.tensor_max(negm_row[:], negm_row[:], gain_r[:]))
    k.op("vector", lambda e: e.tensor_scalar(negm_row[:], negm_row[:], -GB_MUL, -1.0, ALU.mult, ALU.add),
         inc=(("veA", 1),))
    VA = k.now("veA")
    with contextlib.ExitStack() as st0:
        psB = PS(st0, "psB", (128, NH))
        k.op("tensor", lambda e: e.matmul(psB[:], onesr_f[0:1, :], negm_row[0:1, :], start=True, stop=True),
             wait=(("veA", VA),), inc=(("peA", 1),))
        k.op("vector", lambda e: e.tensor_copy(negm_bc[:], psB[:]),
             wait=(("peA", k.now("peA")),), inc=(("veA", 1),))
        k.op("tensor", lambda e: e.matmul(psB[:], onesr_f[0:1, :], mult_g[0:1, :], start=True, stop=True),
             wait=(("veA", k.now("veA")),), inc=(("peA", 1),))
        k.op("vector", lambda e: e.tensor_copy(gs_bc[:], psB[:]),
             wait=(("peA", k.now("peA")),), inc=(("veA", 1),))
        k.op("vector", lambda e: e.tensor_scalar(invf_r[:], irow_s[:], LNB1, None, ALU.mult),
             wait=(("veA", k.now("veA")),))
        k.op("vector", lambda e: e.scalar_tensor_tensor(invf_r[:], irow_s[:], LNB2, invf_r[:], ALU.mult, ALU.add),
             inc=(("veA", 1),))
        k.op("scalar", lambda e: e.activation(invf_r[:], invf_r[:], AF.Exp),
             wait=(("veA", k.now("veA")),), inc=(("seA", 1),))
        psI = PS(st0, "psI", (128, 64))
        k.op("tensor", lambda e: e.matmul(psI[:], onesr_f[0:1, :], invf_r[0:1, :], start=True, stop=True),
             wait=(("seA", k.now("seA")),), inc=(("peA", 1),))
        k.op("vector", lambda e: e.tensor_copy(invfb[:], psI[:]),
             wait=(("peA", k.now("peA")),), inc=(("veA", 1),))
    VE_INVF = k.now("veA")

    def gen_trig(st, tag, ntile, tok_sb, k2s_ap, sgs_ap, k2c_ap, sgc_ap, sin_t, cos_t, R):
        n = ntile * 64
        fr = SB(st, f"fr_{tag}", (128, n))
        rs = SB(st, f"rs_{tag}", (128, n))
        kbuf = SB(st, f"kbuf_{tag}", (128, n))
        sgb = SB(st, f"sgb_{tag}", (128, n))
        sem = f"tg_{tag}"
        k.dma("sync", kbuf[:], k2s_ap, sem, wait=k.fence())
        k.dma("sync", sgb[:], sgs_ap, sem)
        tl = k.now(sem)
        for t_ in range(ntile):
            k.op("vector", lambda e, t_=t_: e.tensor_scalar(
                fr[:, t_ * 64:(t_ + 1) * 64], invfb[:], tok_sb[:, t_:t_ + 1], None, ALU.mult),
                wait=((("veA", VE_INVF), (sem, tl), ("cld", CLD)) if t_ == 0 else ()))
        k.op("vector", lambda e: e.engine_nop(), inc=(("veA", 1),))
        k.op("vector", lambda e: e.scalar_tensor_tensor(rs[:], kbuf[:], -PI1, fr[:], ALU.mult, ALU.add),
             wait=(("veA", k.now("veA")),))
        k.op("vector", lambda e: e.scalar_tensor_tensor(rs[:], kbuf[:], -PI2, rs[:], ALU.mult, ALU.add))
        k.op("vector", lambda e: e.scalar_tensor_tensor(rs[:], kbuf[:], -PI3, rs[:], ALU.mult, ALU.add),
             inc=(("veA", 1),))
        k.op("scalar", lambda e: e.activation(rs[:], rs[:], AF.Sin),
             wait=(("veA", k.now("veA")), ("seA", k.now("seA"))), inc=(("seA", 1),))
        k.op("vector", lambda e: e.tensor_mul(rs[:], rs[:], sgb[:]),
             wait=(("seA", k.now("seA")),), inc=(("veA", 1),))
        stx3 = rs.rearrange("p (t i) -> p t i", i=64)
        sin4 = sin_t.rearrange("p (t r i) -> p t r i", r=R, i=64)
        for r_ in range(R):
            k.op("vector", lambda e, r_=r_: e.tensor_copy(sin4[:, :, r_, :], stx3[:, :, :]),
                 inc=((("veA", 1),) if r_ == R - 1 else ()))
        k.dma("sync", kbuf[:], k2c_ap, sem, wait=(("veA", k.now("veA")),))
        k.dma("sync", sgb[:], sgc_ap, sem)
        tl = k.now(sem)
        k.op("vector", lambda e: e.scalar_tensor_tensor(rs[:], kbuf[:], -PI1, fr[:], ALU.mult, ALU.add),
             wait=((sem, tl),))
        k.op("vector", lambda e: e.scalar_tensor_tensor(rs[:], kbuf[:], -PI2, rs[:], ALU.mult, ALU.add))
        k.op("vector", lambda e: e.scalar_tensor_tensor(rs[:], kbuf[:], -PI3, rs[:], ALU.mult, ALU.add))
        k.op("vector", lambda e: e.tensor_scalar(rs[:], rs[:], HP1, HP2, ALU.add, ALU.add),
             inc=(("veA", 1),))
        k.op("scalar", lambda e: e.activation(rs[:], rs[:], AF.Sin),
             wait=(("veA", k.now("veA")), ("seA", k.now("seA"))), inc=(("seA", 1),))
        k.op("vector", lambda e: e.tensor_mul(rs[:], rs[:], sgb[:]),
             wait=(("seA", k.now("seA")),), inc=(("veA", 1),))
        cos4 = cos_t.rearrange("p (t r i) -> p t r i", r=R, i=64)
        for r_ in range(R):
            k.op("vector", lambda e, r_=r_: e.tensor_copy(cos4[:, :, r_, :], stx3[:, :, :]),
                 inc=((("veA", 1),) if r_ == R - 1 else ()))

    def ln_tile(xin_ap, xout_ap, tv, ts_, extra_wait=()):
        k.op("vector", lambda e: e.tensor_reduce(stat[:, 0:1], xin_ap, AX.X, ALU.add),
             wait=extra_wait + ((tv, k.now(tv)),), inc=((tv, 1),))
        k.op("scalar", lambda e: e.activation(sqsc[:], xin_ap, AF.Square, accum_out=stat[:, 1:2]),
             wait=extra_wait + ((tv, k.now(tv)), (ts_, k.now(ts_))), inc=((ts_, 1),))
        vw, sw = k.now(tv), k.now(ts_)
        k.op("vector", lambda e: e.tensor_scalar(stat[:, 2:3], stat[:, 0:1], 1.0 / D, None, ALU.mult),
             wait=((tv, vw), (ts_, sw)))
        k.op("vector", lambda e: e.tensor_scalar(stat[:, 3:4], stat[:, 1:2], 1.0 / D, None, ALU.mult),
             inc=((tv, 1),))
        k.op("vector", lambda e: e.tensor_mul(stat[:, 4:5], stat[:, 2:3], stat[:, 2:3]),
             wait=((tv, k.now(tv)),))
        k.op("vector", lambda e: e.tensor_sub(stat[:, 4:5], stat[:, 3:4], stat[:, 4:5]),
             inc=((tv, 1),))
        k.op("scalar", lambda e: e.activation(stat[:, 5:6], stat[:, 4:5], AF.Ln, bias=eps_ln[:, 0:1]),
             wait=((tv, k.now(tv)), (ts_, k.now(ts_))), inc=((ts_, 1),))
        k.op("scalar", lambda e: e.activation(stat[:, 5:6], stat[:, 5:6], AF.Exp, scale=-0.5),
             wait=((ts_, k.now(ts_)),), inc=((ts_, 1),))
        k.op("vector", lambda e: e.scalar_tensor_tensor(stat[:, 6:7], stat[:, 2:3], -1.0, stat[:, 5:6], ALU.mult, ALU.mult),
             wait=((ts_, k.now(ts_)),), inc=((tv, 1),))
        k.op("vector", lambda e: e.tensor_scalar(xout_ap, xin_ap, stat[:, 5:6], stat[:, 6:7], ALU.mult, ALU.add),
             wait=((tv, k.now(tv)),), inc=((tv, 1),))
        return k.now(tv)

    def rope(src, dst, nheads, sin_ap, cos_ap, tv, wait0, eng="vector", rep=None):
        rep = nheads if rep is None else rep
        ng = nheads // rep
        s3 = src.rearrange("p (h d) -> p h d", d=HD)
        d3 = dst.rearrange("p (h d) -> p h d", d=HD)
        sin3 = sin_ap.rearrange("p (h i) -> p h i", i=64)
        cos3 = cos_ap.rearrange("p (h i) -> p h i", i=64)
        for gi in range(ng):
            hh = slice(gi * rep, (gi + 1) * rep)
            t1, t2 = s3[:, hh, 0:64], s3[:, hh, 64:128]
            k.op(eng, lambda e, hh=hh, t1=t1: e.tensor_tensor(d3[:, hh, 0:64], t1, cos3, ALU.mult),
                 wait=wait0 if gi == 0 else ())
            k.op(eng, lambda e, hh=hh, t2=t2: e.tensor_tensor(d3[:, hh, 64:128], t2, cos3, ALU.mult))
            k.op(eng, lambda e, hh=hh, t2=t2: e.tensor_tensor(ropeu3[:, hh, :], t2, sin3, ALU.mult))
            k.op(eng, lambda e, hh=hh, t1=t1: e.tensor_tensor(ropev3[:, hh, :], t1, sin3, ALU.mult),
                 inc=((tv, 1),) if gi == ng - 1 else ())
        vv = k.now(tv)
        for gi in range(ng):
            hh = slice(gi * rep, (gi + 1) * rep)
            k.op(eng, lambda e, hh=hh: e.tensor_tensor(d3[:, hh, 0:64], d3[:, hh, 0:64], ropeu3[:, hh, :], ALU.add),
                 wait=((tv, vv),) if gi == 0 else ())
            k.op(eng, lambda e, hh=hh: e.tensor_tensor(d3[:, hh, 64:128], d3[:, hh, 64:128], ropev3[:, hh, :], ALU.subtract),
                 inc=((tv, 1),) if gi == ng - 1 else ())
        return k.now(tv)

    SPW = 2048
    NROW = N_W // 128
    NCH1 = (NROW + SPW - 1) // SPW
    s_chunks = []
    for mi, wT in ((0, w1T), (2, w3T), (1, w2T)):
        flat = wT.flatten().rearrange("(p r) -> p r", p=128)
        for cc in range(NCH1):
            c0 = cc * SPW
            c1 = min(NROW, c0 + SPW)
            s_chunks.append((mi, flat[:, c0:c1], c1 - c0))

    spstg = [SB(es, f"spstg{i}", (128, SPW)) for i in range(2)]
    spred = SB(es, "spred", (128, 1), F32)
    spredV = SB(es, "spredV", (128, 1), F32)
    spabs = sqsc
    nsp = [0]

    nld = [0]
    nrd = [0]

    def sched_s_load():
        idx = nld[0]
        if idx >= len(s_chunks):
            return
        nld[0] += 1
        mi, ap, w = s_chunks[idx]
        par = idx % 2
        k.dma("sync", spstg[par][:, 0:w], ap, "sp%d" % par,
              wait=(M.get("spfree%d" % par, ("veA", 0)),))
        M["spld%d" % idx] = k.now("sp%d" % par)

    def sched_s_reduce(eng):
        idx = nrd[0]
        if idx >= nld[0] or idx >= len(s_chunks):
            return
        nrd[0] += 1
        mi, ap, w = s_chunks[idx]
        par = idx % 2
        ldw = (("sp%d" % par, M["spld%d" % idx]),)
        if eng == "scalar":
            k.op("scalar", lambda e, w=w, par=par: e.activation(
                spabs[:, 0:w], spstg[par][:, 0:w], AF.Abs, accum_out=spred[:, 0:1]),
                wait=ldw + (("poS", k.now("poS")), ("veA", M_SS)))
            red, rsem = spred, ("seA", k.now("seA"))
        else:
            k.op("vector", lambda e, w=w, par=par: e.tensor_reduce(
                spredV[:], spstg[par][:, 0:w], AX.X, ALU.add, apply_absolute_value=True),
                wait=ldw + (("poS", k.now("poS")), ("veA", M_SS)))
            red, rsem = spredV, ("veA", k.now("veA"))
        k.op("gpsimd", lambda e, mi=mi, red=red: e.tensor_add(sS[:, mi:mi + 1], sS[:, mi:mi + 1], red[:]),
             wait=(rsem,), inc=(("poS", 1),))
        M["spfree%d" % par] = rsem

    with contextlib.ExitStack() as stS:
        xnfP2 = [PS(stS, f"xnfP{i}", (128, D), BF16) for i in range(2)]
        xTst = [SB(stS, f"xTst{i}", (128, D), F32) for i in range(2)]
        xT_bf = [SB(stS, f"xT_bf{i}", (128, D), BF16) for i in range(2)]
        stat2 = [SB(stS, f"stat2_{i}", (128, 32), F32) for i in range(2)]

        def front_dma(par, xsrc_tok, xsrc_ft, xsem, tsem):
            k.dma("sync", xstg[par][:], xsrc_tok, xsem,
                  wait=(("veA", M.get("xfree%d" % par, 0)),))
            k.dma("scalar", xTst[par].rearrange("p (c q) -> p c q", q=128), xsrc_ft,
                  tsem, wait=(("seA", M.get("xtc%d" % par, 0)),))

        def front_cast(par, tsem):
            k.op("scalar", lambda e, par=par: e.activation(xT_bf[par][:], xTst[par][:], AF.Identity),
                 wait=((tsem, k.now(tsem)), ("peA", M.get("xTfree%d" % par, 0))),
                 inc=(("seA", 1),))
            M["xtc%d" % par] = k.now("seA")
            M["cast%d" % par] = k.now("seA")

        def front_stats(par, xsem):
            st_ = stat2[par]
            ldw = ((xsem, k.now(xsem)),)
            for i4 in range(4):
                k.op("vector", lambda e, st_=st_, par=par, i4=i4: e.bn_stats(
                    st_[:, i4 * 6:(i4 + 1) * 6], xstg[par][:, i4 * 512:(i4 + 1) * 512]),
                    wait=ldw if i4 == 0 else ())
            M["lnsq%d" % par] = k.now("veA")
            M["xfree%d" % par] = k.now("veA")

        def stats_back(t):
            par = t % 2
            st_ = stat2[par]
            k.op("vector", lambda e, st_=st_: e.bn_aggr(st_[:, 24:26], st_[:, 0:24]),
                 wait=(("veA", M["lnsq%d" % par]),))
            k.op("scalar", lambda e, st_=st_: e.activation(st_[:, 27:28], st_[:, 25:26], AF.Ln,
                                                           bias=eps_ln[:, 0:1]),
                 wait=(("veA", k.now("veA")),))
            k.op("scalar", lambda e, st_=st_: e.activation(st_[:, 27:28], st_[:, 27:28], AF.Exp, scale=-0.5),
                 inc=(("seA", 1),))
            k.op("vector", lambda e, st_=st_: e.tensor_scalar(st_[:, 28:29], st_[:, 24:25], st_[:, 27:28],
                                                              -1.0, ALU.mult, ALU.mult),
                 wait=(("seA", k.now("seA")),), inc=(("veA", 1),))
            M["nmr%d" % par] = k.now("veA")

        def build_corr(st, tag, wTb, nfeat, bias_off, wld_val):
            cs = SB(st, f"cs_{tag}", (128, nfeat), F32)
            bb = SB(st, f"bb_{tag}", (128, nfeat), F32)
            csrow = SB(st, f"csrow_{tag}", (1, nfeat), F32)
            with contextlib.ExitStack() as st0:
                psC = PS(st0, f"psC_{tag}", (128, 512))
                psB2 = PS(st0, f"psB2_{tag}", (128, 512))
                for n in range(nfeat // 512):
                    for c in range(NDC):
                        k.op("tensor", lambda e, n=n, c=c: e.matmul(
                            psC[0:1, :], onesc_b[:, 0:1],
                            wTb[:, c * nfeat + n * 512:c * nfeat + (n + 1) * 512],
                            start=(c == 0), stop=(c == NDC - 1)),
                            wait=((("wld", wld_val), ("veA", M.get("csf_" + tag, 0)))
                                  if c == 0 else ()),
                            inc=((("peA", 1),) if c == NDC - 1 else ()))
                    k.op("vector", lambda e, n=n: e.tensor_copy(
                        csrow[0:1, n * 512:(n + 1) * 512], psC[0:1, :]),
                        wait=(("peA", k.now("peA")),), inc=(("veA", 1),))
                    M["csf_" + tag] = k.now("veA")
                    k.op("tensor", lambda e, n=n: e.matmul(
                        psB2[:, :], onesr_f[0:1, :], csrow[0:1, n * 512:(n + 1) * 512],
                        start=True, stop=True),
                        wait=(("veA", k.now("veA")),),
                        inc=(("peA", 1),))
                    k.op("vector", lambda e, n=n: e.tensor_copy(
                        cs[:, n * 512:(n + 1) * 512], psB2[:, :]),
                        wait=(("peA", k.now("peA")),), inc=(("veA", 1),))
                    k.op("tensor", lambda e, n=n: e.matmul(
                        psB2[:, :], onesr_b[0:1, :],
                        biasr[0:1, bias_off + n * 512:bias_off + (n + 1) * 512],
                        start=True, stop=True),
                        wait=(("veA", k.now("veA")), ("bld", BLD)), inc=(("peA", 1),))
                    k.op("vector", lambda e, n=n: e.tensor_copy(
                        bb[:, n * 512:(n + 1) * 512], psB2[:, :]),
                        wait=(("peA", k.now("peA")),), inc=(("veA", 1),))
                    M["bbf_" + tag] = k.now("veA")
            return cs, bb

        with contextlib.ExitStack() as stQ:
            qsin = SB(stQ, "qsin", (128, TPC * 64 * KVH), BF16)
            qcos = SB(stQ, "qcos", (128, TPC * 64 * KVH), BF16)
            with contextlib.ExitStack() as stT:
                gen_trig(stT, "q", TPC, tokq_s, qk2s, qsgs, qk2c, qsgc, qsin, qcos, KVH)
            wqTb = SB(stQ, "wqTb", (128, NDC * D), BF16)
            k.dma("gpsimd", wqTb.rearrange("p (c f) -> p c f", c=NDC)[:, :, :],
                  wqT.rearrange("(c p) f -> p c f", p=128), "wld",
                  wait=(("bld", BLD),) + k.fence())
            WQ_LD = k.now("wld")
            cs_q, bb_q = build_corr(stQ, "q", wqTb, D, BQ0, WQ_LD)
            M["qpsfree"] = k.now("veA")
            psQ = PS(stQ, "psQ", (128, D))
            qcbuf = SB(stQ, "qcbuf", (128, D), F32)
            q_bf = SB(stQ, "q_bf", (128, D), BF16)
            q_ro = SB(stQ, "q_ro", (128, D), BF16)
            qspill = SB(stQ, "qspill", (128, D), BF16)
            qsc4 = Qsc.rearrange("p (h j q) -> p h j q", j=TPC, q=128)

            def q_front(j, what):
                par = j % 2
                if what == "dma":
                    front_dma(par, xo[j * 128:(j + 1) * 128, :],
                              xoTt[j:j + 1, :, :].rearrange("o (c p) q -> p (o c) q", p=128),
                              "qx%d" % par, "qt%d" % par)
                else:
                    front_cast(par, "qt%d" % par)
                    front_stats(par, "qx%d" % par)

            def q_back(j):
                par = j % 2
                stats_back(j)
                st_ = stat2[par]
                for n in range(4):
                    for c in range(NDC):
                        w = ()
                        if n == 0 and c == 0:
                            w = (("seA", M["cast%d" % par]), ("wld", WQ_LD),
                                 ("veA", M.get("qpsfree", 0)))
                        k.op("tensor", lambda e, n=n, c=c, par=par: e.matmul(
                            psQ[:, n * 512:(n + 1) * 512], xT_bf[par][:, c * 128:(c + 1) * 128],
                            wqTb[:, c * D + n * 512:c * D + (n + 1) * 512],
                            start=(c == 0), stop=(c == NDC - 1)), wait=w,
                            inc=(("peA", 1),) if (n == 3 and c == NDC - 1) else ())
                M["xTfree%d" % par] = k.now("peA")
                peq = k.now("peA")
                k.op("vector", lambda e, st_=st_: e.scalar_tensor_tensor(
                    qcbuf[:], psQ[:], st_[:, 27:28], bb_q[:], ALU.mult, ALU.add),
                     wait=(("peA", peq), ("veA", M["nmr%d" % par])))
                k.op("vector", lambda e, st_=st_: e.scalar_tensor_tensor(
                    qcbuf[:], cs_q[:], st_[:, 28:29], qcbuf[:], ALU.mult, ALU.add),
                     inc=(("veA", 1),))
                M["qpsfree"] = k.now("veA")
                k.op("scalar", lambda e: e.activation(sqsc[:], qcbuf[:], AF.Square),
                     wait=(("veA", k.now("veA")),), inc=(("seA", 1),))
                sq3 = sqsc.rearrange("p (h d) -> p h d", d=HD)
                k.op("vector", lambda e: e.tensor_reduce(rmsd[:, 0:NH], sq3[:, :, :], AX.X, ALU.add),
                     wait=(("seA", k.now("seA")),))
                k.op("scalar", lambda e: e.activation(rmsd[:, 0:NH], rmsd[:, 0:NH], AF.Ln,
                                                      bias=eps_rms[:, 0:1], scale=1.0 / HD),
                     wait=(("veA", k.now("veA")),))
                k.op("scalar", lambda e: e.activation(rmsd[:, 0:NH], rmsd[:, 0:NH], AF.Exp, scale=-0.5),
                     inc=(("seA", 1),))
                k.op("vector", lambda e: e.tensor_mul(rmsd[:, 0:NH], rmsd[:, 0:NH], gs_bc[:, 0:NH]),
                     wait=(("seA", k.now("seA")),))
                for h in range(NH):
                    k.op("vector", lambda e, h=h: e.tensor_scalar(
                        q_bf[:, h * HD:(h + 1) * HD], qcbuf[:, h * HD:(h + 1) * HD],
                        rmsd[:, h:h + 1], None, ALU.mult),
                        inc=(("veA", 1),) if h == NH - 1 else ())
                vro = rope(q_bf, q_ro, NH, qsin[:, j * 256:(j + 1) * 256], qcos[:, j * 256:(j + 1) * 256],
                           "veA", (), rep=KVH)
                for h in range(NH):
                    k.op("tensor", lambda e, h=h, par=par: e.transpose(
                        xnfP2[par][:, h * 128:(h + 1) * 128], q_ro[:, h * 128:(h + 1) * 128], identb[:]),
                        wait=(("veA", vro), ("veA", M.get("qspv%d" % par, 0))) if h == 0 else (),
                        inc=(("peA", 1),) if h == NH - 1 else ())
                k.op("vector", lambda e, par=par: e.tensor_copy(qspill[:], xnfP2[par][:]),
                     wait=(("peA", k.now("peA")), ("qw", k.now("qw"))), inc=(("veA", 1),))
                M["qspv%d" % par] = k.now("veA")
                qsp3 = qspill.rearrange("p (h q) -> p h q", q=128)
                k.dma("sync", qsc4[:, :, j, :], qsp3[:, :, :], "qw", wait=(("veA", k.now("veA")),))

            q_front(0, "dma")
            q_front(0, "rest")
            q_front(1, "dma")
            q_front(1, "rest")
            for j in range(TPC):
                if j + 2 < TPC:
                    q_front(j + 2, "dma")
                q_back(j)
                if j + 2 < TPC:
                    q_front(j + 2, "rest")
        QS_DONE = (("qw", k.now("qw")),)

        with contextlib.ExitStack() as stK:
            ksin = SB(stK, "ksin", (128, NT * 64 * KVH), BF16)
            kcos = SB(stK, "kcos", (128, NT * 64 * KVH), BF16)
            NH2 = NT // 2
            for hf in range(2):
                c0, c1 = hf * NH2 * 64, (hf + 1) * NH2 * 64
                s0, s1 = hf * NH2 * 64 * KVH, (hf + 1) * NH2 * 64 * KVH
                with contextlib.ExitStack() as stT:
                    gen_trig(stT, "k%d" % hf, NH2, tokk_s[:, hf * NH2:(hf + 1) * NH2],
                             k2s[:, c0:c1], sgs[:, c0:c1], k2c[:, c0:c1], sgc[:, c0:c1],
                             ksin[:, s0:s1], kcos[:, s0:s1], KVH)
            wkTb = SB(stK, "wkTb", (128, NDC * KVD), BF16)
            wvTb = SB(stK, "wvTb", (128, NDC * KVD), BF16)
            k.dma("gpsimd", wkTb.rearrange("p (c f) -> p c f", c=NDC)[:, :, :],
                  wkT.rearrange("(c p) f -> p c f", p=128), "wld", wait=k.fence())
            k.dma("gpsimd", wvTb.rearrange("p (c f) -> p c f", c=NDC)[:, :, :],
                  wvT.rearrange("(c p) f -> p c f", p=128), "wld")
            WKV_LD = k.now("wld")
            cs_k, bb_k = build_corr(stK, "k", wkTb, KVD, BK0, WKV_LD)
            cs_v, bb_v = build_corr(stK, "v", wvTb, KVD, BV0, WKV_LD)
            for _s in ("kpsfree0", "kpsfree1", "vpsfree0", "vpsfree1"):
                M[_s] = k.now("veA")
            psK = [PS(stK, f"psK{i}", (128, KVD)) for i in range(2)]
            psV = [PS(stK, f"psV{i}", (128, KVD)) for i in range(2)]
            kcbuf = SB(stK, "kcbuf", (128, KVD), F32)
            k_bf = [SB(stK, f"k_bf{i}", (128, KVD), BF16) for i in range(2)]
            k_ro = [SB(stK, f"k_ro{i}", (128, KVD), BF16) for i in range(2)]
            kT_st = [SB(stK, f"kT_st{i}", (128, KVD), BF16) for i in range(2)]
            v_st = [SB(stK, f"v_st{i}", (128, KVD), BF16) for i in range(2)]
            sqk = SB(stK, "sqk", (128, KVD), F32)
            vcbuf = sqk

            def kv_front(tt, what):
                par = tt % 2
                if what == "dma":
                    front_dma(par, xf[tt * 128:(tt + 1) * 128, :],
                              xfTt[tt:tt + 1, :, :].rearrange("o (c p) q -> p (o c) q", p=128),
                              "kx%d" % par, "kt%d" % par)
                else:
                    front_cast(par, "kt%d" % par)
                    front_stats(par, "kx%d" % par)

            def kv_s2(tt):
                stats_back(tt)

            def kv_s34K(tt):
                par = tt % 2
                for c in range(NDC):
                    w = ()
                    if c == 0:
                        w = (("seA", M["cast%d" % par]), ("wld", WKV_LD),
                             ("veA", M.get("kpsfree%d" % par, 0)))
                    k.op("tensor", lambda e, c=c, par=par: e.matmul(
                        psK[par][:], xT_bf[par][:, c * 128:(c + 1) * 128], wkTb[:, c * KVD:(c + 1) * KVD],
                        start=(c == 0), stop=(c == NDC - 1)), wait=w,
                        inc=(("peA", 1),) if c == NDC - 1 else ())
                M["pek%d" % par] = k.now("peA")

            def kv_s4V(tt, ktrans_of=None):
                par = tt % 2
                for c in range(NDC):
                    w = (("veA", M.get("vpsfree%d" % par, 0)),) if c == 0 else ()
                    k.op("tensor", lambda e, c=c, par=par: e.matmul(
                        psV[par][:], xT_bf[par][:, c * 128:(c + 1) * 128], wvTb[:, c * KVD:(c + 1) * KVD],
                        start=(c == 0), stop=(c == NDC - 1)), wait=w,
                        inc=(("peA", 1),) if c == NDC - 1 else ())
                    if ktrans_of is not None and 4 <= c < 8:
                        g = c - 4
                        kpar = ktrans_of % 2
                        wt = ()
                        if g == 0:
                            wt = (("poA", M["ropedone%d" % kpar]),
                                  ("seA", M.get("ktcs%d" % kpar, 0)),
                                  ("veA", M.get("qspv%d" % kpar, 0)))
                        k.op("tensor", lambda e, g=g, kpar=kpar: e.transpose(
                            xnfP2[kpar][:, g * 128:(g + 1) * 128], k_ro[kpar][:, g * 128:(g + 1) * 128], identb[:]),
                            wait=wt, inc=(("peA", 1),) if g == KVH - 1 else ())
                        if g == KVH - 1:
                            M["ktrans%d" % kpar] = k.now("peA")
                M["pev%d" % par] = k.now("peA")
                M["xTfree%d" % par] = k.now("peA")

            def kv_s5a(tt):
                par = tt % 2
                st_ = stat2[par]
                k.op("vector", lambda e, par=par, st_=st_: e.scalar_tensor_tensor(
                    kcbuf[:], psK[par][:], st_[:, 27:28], bb_k[:], ALU.mult, ALU.add),
                     wait=(("peA", M["pek%d" % par]), ("veA", M["nmr%d" % par]),
                           ("seA", M.get("sqdone", 0))))
                k.op("vector", lambda e, st_=st_: e.scalar_tensor_tensor(
                    kcbuf[:], cs_k[:], st_[:, 28:29], kcbuf[:], ALU.mult, ALU.add),
                     inc=(("veA", 1),))
                M["kpsfree%d" % par] = k.now("veA")
                k.op("scalar", lambda e: e.activation(sqk[:], kcbuf[:], AF.Square),
                     wait=(("veA", k.now("veA")),), inc=(("seA", 1),))
                M["sqdone"] = k.now("seA")
                sqk3 = sqk.rearrange("p (h d) -> p h d", d=HD)
                k.op("vector", lambda e: e.tensor_reduce(rmsd[:, 0:KVH], sqk3[:, :, :], AX.X, ALU.add),
                     wait=(("seA", k.now("seA")),))
                k.op("scalar", lambda e: e.activation(rmsd[:, 0:KVH], rmsd[:, 0:KVH], AF.Ln,
                                                      bias=eps_rms[:, 0:1], scale=1.0 / HD),
                     wait=(("veA", k.now("veA")),))
                k.op("scalar", lambda e: e.activation(rmsd[:, 0:KVH], rmsd[:, 0:KVH], AF.Exp, scale=-0.5),
                     inc=(("seA", 1),))
                sk = k.now("seA")
                for g in range(KVH):
                    k.op("vector", lambda e, g=g, par=par: e.tensor_scalar(
                        k_bf[par][:, g * HD:(g + 1) * HD], kcbuf[:, g * HD:(g + 1) * HD],
                        rmsd[:, g:g + 1], None, ALU.mult),
                        wait=(("seA", sk), ("poA", M.get("ropedone%d" % par, 0))) if g == 0 else (),
                        inc=(("veA", 1),) if g == KVH - 1 else ())
                vro = rope(k_bf[par], k_ro[par], KVH, ksin[:, tt * 256:(tt + 1) * 256], kcos[:, tt * 256:(tt + 1) * 256],
                           "poA", (("veA", k.now("veA")), ("pe", M.get("ktrans%d" % par, 0))), eng="gpsimd")
                M["ropedone%d" % par] = vro
                k.op("vector", lambda e, par=par, st_=st_: e.scalar_tensor_tensor(
                    vcbuf[:], psV[par][:], st_[:, 27:28], bb_v[:], ALU.mult, ALU.add),
                     wait=(("peA", M["pev%d" % par]),))
                k.op("vector", lambda e, par=par, st_=st_: e.scalar_tensor_tensor(
                    v_st[par][:], cs_v[:], st_[:, 28:29], vcbuf[:], ALU.mult, ALU.add),
                     wait=(("vsp%d" % par, k.now("vsp%d" % par)),), inc=(("veA", 1),))
                M["vpsfree%d" % par] = k.now("veA")
                vst3 = v_st[par].rearrange("p (g h) -> p g h", h=HD)
                vd_t = Vd[:, tt * 128:(tt + 1) * 128, :].rearrange("g p h -> p g h")
                k.dma("sync", vd_t, vst3[:, :, :], "vsp%d" % par, wait=(("veA", k.now("veA")),))

            def kv_s5b(tt):
                par = tt % 2
                k.op("scalar", lambda e, par=par: e.activation(kT_st[par][:], xnfP2[par][:, 0:KVD], AF.Identity),
                     wait=(("peA", M["ktrans%d" % par]), ("ksp%d" % par, k.now("ksp%d" % par))), inc=(("seA", 1),))
                M["ktcs%d" % par] = k.now("seA")
                kst3 = kT_st[par].rearrange("p (g q) -> p g q", q=128)
                kd_t = Kd[:, :, tt * 128:(tt + 1) * 128].rearrange("g p q -> p g q")
                k.dma("scalar", kd_t, kst3[:, :, :], "ksp%d" % par, wait=(("seA", M["ktcs%d" % par]),))

            kv_front(0, "dma")
            kv_front(0, "rest")
            kv_front(1, "dma")
            kv_front(1, "rest")
            kv_s2(0)
            for tt in range(NT):
                if tt + 2 < NT:
                    kv_front(tt + 2, "dma")
                if tt + 1 < NT:
                    kv_s2(tt + 1)
                kv_s34K(tt)
                kv_s4V(tt, ktrans_of=(tt - 1) if tt >= 1 else None)
                kv_s5a(tt)
                if tt >= 1:
                    kv_s5b(tt - 1)
                if tt % 4 == 3:
                    sched_s_load()
                    if nld[0] >= 2:
                        sched_s_reduce("scalar")
                if tt + 2 < NT:
                    kv_front(tt + 2, "rest")
            kparF = (NT - 1) % 2
            for g in range(KVH):
                wt = ()
                if g == 0:
                    wt = (("poA", M["ropedone%d" % kparF]),
                          ("seA", M.get("ktcs%d" % kparF, 0)),
                          ("veA", M.get("qspv%d" % kparF, 0)))
                k.op("tensor", lambda e, g=g, kparF=kparF: e.transpose(
                    xnfP2[kparF][:, g * 128:(g + 1) * 128], k_ro[kparF][:, g * 128:(g + 1) * 128], identb[:]),
                    wait=wt, inc=(("peA", 1),) if g == KVH - 1 else ())
            M["ktrans%d" % kparF] = k.now("peA")
            kv_s5b(NT - 1)
    KVSP = (("ksp0", k.now("ksp0")), ("ksp1", k.now("ksp1")),
            ("vsp0", k.now("vsp0")), ("vsp1", k.now("vsp1"))) + QS_DONE

    xn2_fm = SB(es, "xn2_fm", (128, NDC * TPC * 128), BF16)
    xn2f4 = xn2_fm.rearrange("p (c j q) -> p c j q", j=TPC, q=128)
    with contextlib.ExitStack() as stF:
        KBLK = [SB(stF, f"KBLK{i}", (128, 16 * 512), BF16) for i in range(2)]
        VBLK = [SB(stF, f"VBLK{i}", (128, 16 * 512), BF16) for i in range(2)]
        MBUF = SB(stF, "MBUF", (128, 4 * 1024), BF16)
        PT = [SB(stF, f"PT{i}", (128, 1024), BF16) for i in range(2)]
        Qst = [SB(stF, f"Qst{i}", (128, 256), BF16) for i in range(2)]
        aost = SB(stF, "aost", (128, 256), BF16)
        sum_g = SB(stF, "sum_g", (1, 256), F32)
        rec_b = SB(stF, "rec_b", (128, 256), F32)
        ST = [PS(stF, f"ST{i}", (128, 1024)) for i in range(2)]
        PV = [PS(stF, f"PV{i}", (128, 512)) for i in range(2)]
        SMPB = [PS(stF, f"SMPB{i}", (128, 512)) for i in range(2)]
        gstep = [0]
        sched_s_load()

        for a in range(4):
            K_a = 4 * a + 4
            gw = {}

            def load_g(a_, g_):
                gpar_ = (a_ * KVH + g_) % 2
                K_a_ = 4 * a_ + 4
                w = (("peF", M.get("gfree%d" % gpar_, 0)),)
                if a_ == 0 and g_ < 2:
                    w = w + KVSP + k.fence()
                k.dma("sync", KBLK[gpar_][:, 0:K_a_ * 512], Kd[g_, :, 0:K_a_ * 512], "kbl%d" % gpar_, wait=w)
                vsrc = Vd[g_, 0:K_a_ * 512, :].rearrange("(n p) h -> p n h", p=128)
                k.dma("sync", VBLK[gpar_].rearrange("p (n h) -> p n h", h=HD)[:, 0:K_a_ * 4, :], vsrc,
                      "kbl%d" % gpar_)
                gw[g_] = k.now("kbl%d" % gpar_)

            load_g(a, 0)
            k.dma("sync", MBUF.rearrange("p (n f) -> p n f", n=4)[:, :, :],
                  binm[a * 4:(a + 1) * 4, :, :].rearrange("n p f -> p n f"), "mld",
                  wait=(("veF", M.get("mfree", 0)),))
            MLD = k.now("mld")
            def qload(a_, h_):
                qp = h_ % 2
                k.dma("sync", Qst[qp][:], Qsc[:, h_ * 1024 + a_ * 256:h_ * 1024 + (a_ + 1) * 256],
                      "qld%d" % qp, wait=(("peF", M.get("quse%d" % qp, 0)),))
                M["qldc%d" % qp] = k.now("qld%d" % qp)

            qload(a, 0)
            for g in range(KVH):
                gpar = (a * KVH + g) % 2
                if g + 1 < KVH:
                    load_g(a, g + 1)
                for u in range(4):
                    h = g * 4 + u
                    hpar = h % 2
                    qpar = h % 2
                    sched_s_reduce("vector")
                    sched_s_reduce("vector")
                    sched_s_load()
                    sched_s_load()
                    if h + 1 < NH:
                        qload(a, h + 1)
                    QLD = M["qldc%d" % qpar]
                    base = gstep[0]

                    def fqk(i, kb, gpar=gpar, qpar=qpar, gw_=None, QLD=QLD, a=a):
                        par = i % 2
                        trim = (kb - 4 * a) >= 2
                        qo, qw_ = (128, 128) if trim else (0, 256)
                        w = (("seF", M.get("stfree%d" % par, 0)),
                             ("kbl%d" % gpar, gw_), ("qld%d" % qpar, QLD))
                        for kc in range(4):
                            k.op("tensor", lambda e, par=par, kc=kc, kb=kb, gpar=gpar, qpar=qpar,
                                 qo=qo, qw_=qw_: e.matmul(
                                ST[par][:, kc * 256 + qo:kc * 256 + qo + qw_],
                                KBLK[gpar][:, kb * 512 + kc * 128:kb * 512 + (kc + 1) * 128],
                                Qst[qpar][:, qo:qo + qw_], start=True, stop=True),
                                wait=w if kc == 0 else (),
                                inc=(("peF", 1),) if kc == 3 else ())
                        M["qk%d" % par] = k.now("peF")

                    def fexp(i, kb, h=h, a=a):
                        par = i % 2
                        ki = kb - 4 * a
                        trim = ki >= 2
                        wq = (("peF", M["qk%d" % par]), ("peF", M.get("ptfree%d" % par, 0)),
                              ("seF", k.now("seF")))
                        if trim:
                            pt3 = PT[par].rearrange("p (c q) -> p c q", q=256)
                            st3 = ST[par].rearrange("p (c q) -> p c q", q=256)
                            k.op("scalar", lambda e, par=par, h=h, pt3=pt3, st3=st3: e.activation(
                                pt3[:, :, 128:256], st3[:, :, 128:256], AF.Exp, bias=negm_bc[:, h:h + 1]),
                                wait=wq, inc=(("seF", 1),))
                            M["stfree%d" % par] = k.now("seF")
                            mb3 = MBUF.rearrange("p (n c q) -> p n c q", n=4, q=256)
                            k.op("vector", lambda e, par=par, ki=ki, pt3=pt3, mb3=mb3: e.tensor_tensor(
                                pt3[:, :, 128:256], pt3[:, :, 128:256], mb3[:, ki, :, 128:256], ALU.mult),
                                wait=(("seF", k.now("seF")), ("mld", MLD), ("veF", k.now("veF"))),
                                inc=(("veF", 1),))
                            M["ptdepA%d" % par] = ("veF", k.now("veF"))
                            M["ptdepB%d" % par] = ("veF", k.now("veF"))
                            return
                        k.op("scalar", lambda e, par=par, h=h: e.activation(
                            PT[par][:, 0:512], ST[par][:, 0:512], AF.Exp, bias=negm_bc[:, h:h + 1]),
                            wait=wq, inc=(("seF", 1),))
                        sA = k.now("seF")
                        k.op("scalar", lambda e, par=par, h=h: e.activation(
                            PT[par][:, 512:1024], ST[par][:, 512:1024], AF.Exp, bias=negm_bc[:, h:h + 1]),
                            inc=(("seF", 1),))
                        M["stfree%d" % par] = k.now("seF")
                        if ki >= 0:
                            k.op("vector", lambda e, par=par, ki=ki: e.tensor_mul(
                                PT[par][:, 0:512], PT[par][:, 0:512], MBUF[:, ki * 1024:ki * 1024 + 512]),
                                wait=(("seF", sA), ("mld", MLD), ("veF", k.now("veF"))),
                                inc=(("veF", 1),))
                            M["ptdepA%d" % par] = ("veF", k.now("veF"))
                            k.op("vector", lambda e, par=par, ki=ki: e.tensor_mul(
                                PT[par][:, 512:1024], PT[par][:, 512:1024], MBUF[:, ki * 1024 + 512:(ki + 1) * 1024]),
                                wait=(("seF", k.now("seF")),),
                                inc=(("veF", 1),))
                            M["ptdepB%d" % par] = ("veF", k.now("veF"))
                        else:
                            M["ptdepA%d" % par] = ("seF", sA)
                            M["ptdepB%d" % par] = ("seF", k.now("seF"))

                    def fpv(i, kb, K_a=K_a, gpar=gpar, hpar=hpar, a=a):
                        par = i % 2
                        first, last = (kb == 0), (kb == K_a - 1)
                        ki = kb - 4 * a
                        trim = ki >= 2
                        qo, qw_ = (128, 128) if trim else (0, 256)
                        wA = [M["ptdepA%d" % par]]
                        wB = [M["ptdepB%d" % par]]
                        if first:
                            wA.append(("veF", M.get("finfree%d" % hpar, 0)))
                        for kc in range(4):
                            w = (tuple(wA) if kc == 0 else ()) + (tuple(wB) if kc == 2 else ())
                            k.op("tensor", lambda e, par=par, kc=kc, kb=kb, gpar=gpar, hpar=hpar,
                                 first=first, last=last, qo=qo, qw_=qw_: e.matmul(
                                PV[hpar][:, qo:qo + qw_],
                                VBLK[gpar][:, (kb * 4 + kc) * 128:(kb * 4 + kc + 1) * 128],
                                PT[par][:, kc * 256 + qo:kc * 256 + qo + qw_],
                                start=(first and kc == 0), stop=(last and kc == 3)),
                                wait=w)
                        for kc in range(4):
                            k.op("tensor", lambda e, par=par, kc=kc, hpar=hpar, first=first, last=last,
                                 qo=qo, qw_=qw_: e.matmul(
                                SMPB[hpar][0:1, qo:qo + qw_], onesc_b[:, 0:1],
                                PT[par][:, kc * 256 + qo:kc * 256 + qo + qw_],
                                start=(first and kc == 0), stop=(last and kc == 3)),
                                inc=(("peF", 1),) if kc == 3 else ())
                        M["ptfree%d" % par] = k.now("peF")

                    fqk(base + 0, 0, gw_=gw[g])
                    fexp(base + 0, 0)
                    for kb in range(1, K_a):
                        fqk(base + kb, kb, gw_=gw[g])
                        fpv(base + kb - 1, kb - 1)
                        if kb == 1 and M.get("_fin") is not None:
                            M.pop("_fin")()
                        fexp(base + kb, kb)
                    fpv(base + K_a - 1, K_a - 1)
                    gstep[0] = base + K_a
                    M["quse%d" % qpar] = k.now("peF")

                    def finalize(a=a, h=h, hpar=hpar, qpar=qpar):
                        pw = k.now("peF")
                        k.op("vector", lambda e, hpar=hpar: e.tensor_scalar(
                            sum_g[:], SMPB[hpar][0:1, 0:256], guard_c[0:1, 0:1], None, ALU.max),
                            wait=(("peF", pw), ("veF", k.now("veF"))), inc=(("veF", 1),))
                        k.op("scalar", lambda e: e.activation(sum_g[:], sum_g[:], AF.Ln),
                             wait=(("veF", k.now("veF")), ("seF", k.now("seF"))), inc=(("seF", 1),))
                        k.op("scalar", lambda e: e.activation(sum_g[:], sum_g[:], AF.Exp, scale=-1.0),
                             wait=(("seF", k.now("seF")),), inc=(("seF", 1),))
                        k.op("tensor", lambda e, hpar=hpar: e.matmul(
                            SMPB[hpar][:, 256:512], onesr_f[0:1, :], sum_g[0:1, 0:256], start=True, stop=True),
                            wait=(("seF", k.now("seF")),), inc=(("peF", 1),))
                        k.op("vector", lambda e, hpar=hpar: e.tensor_copy(rec_b[:], SMPB[hpar][:, 256:512]),
                             wait=(("peF", k.now("peF")), ("veF", k.now("veF"))), inc=(("veF", 1),))
                        k.op("vector", lambda e, hpar=hpar: e.tensor_tensor(
                            aost[:], PV[hpar][:, 0:256], rec_b[:], ALU.mult),
                            wait=(("veF", k.now("veF")), ("aow", k.now("aow"))), inc=(("veF", 1),))
                        M["finfree%d" % hpar] = k.now("veF")
                        M["qfree%d" % qpar] = k.now("peF")
                        k.dma("sync", AOsc[:, h * 1024 + a * 256:h * 1024 + (a + 1) * 256], aost[:],
                              "aow", wait=(("veF", k.now("veF")),))

                    M["_fin"] = finalize
                M["gfree%d" % gpar] = k.now("peF")
            M["mfree"] = k.now("veF")
        if M.get("_fin") is not None:
            M.pop("_fin")()
        assert nld[0] == len(s_chunks), (nld[0], len(s_chunks))
        while nrd[0] < nld[0]:
            sched_s_reduce("vector")
    AO_DONE = k.now("aow")

    with contextlib.ExitStack() as stB:
        woTb = SB(stB, "woTb", (128, NDC * D), BF16)
        k.dma("gpsimd", woTb.rearrange("p (c f) -> p c f", c=NDC)[:, 0:8, :], woT[0:1024, :].rearrange("(c p) f -> p c f", p=128), "wld",
              wait=k.fence())
        k.dma("gpsimd", woTb.rearrange("p (c f) -> p c f", c=NDC)[:, 8:16, :], woT[1024:2048, :].rearrange("(c p) f -> p c f", p=128), "wld")
        WO_LD = k.now("wld")
        psO = PS(stB, "psO", (128, D))
        psT2 = PS(stB, "psT2", (128, D), BF16)
        aostg = [SB(stB, f"aostg{i}", (128, NH * 128), BF16) for i in range(2)]
        x2t = SB(stB, "x2t", (128, D), F32)
        radj = SB(stB, "radj", (128, D), F32)
        xn2b = SB(stB, "xn2b", (128, D), BF16)
        ao4 = AOsc.rearrange("p (h j q) -> p h j q", j=TPC, q=128)
        ybuf = SB(stB, "ybuf", (128, D), F32)

        def b2_dma(j):
            par = j % 2
            k.dma("sync", aostg[par][:], ao4[:, :, j, :], "aold%d" % par,
                  wait=(("aow", AO_DONE), ("peO", M.get("aofree%d" % par, 0)))
                  + (k.fence() if j < 2 else ()))
            M["ald%d" % par] = k.now("aold%d" % par)
            k.dma("sync", xstg[par][:], xo[j * 128:(j + 1) * 128, :], "bx%d" % par,
                  wait=(("veA", M.get("xfree%d" % par, 0)), ("veO", M.get("bxfree%d" % par, 0))))

        def b2_mm(j):
            par = j % 2
            do_tr = j >= 1
            for c in range(NDC):
                for n in range(4):
                    w = ()
                    if c == 0 and n == 0:
                        w = (("wld", WO_LD), ("veO", M.get("psOfree", 0)),
                             ("aold%d" % par, M["ald%d" % par]), ("seO", M.get("psOfree_s", 0)))
                    k.op("tensor", lambda e, c=c, n=n, par=par: e.matmul(
                        psO[:, n * 512:(n + 1) * 512],
                        aostg[par][:, c * 128:(c + 1) * 128],
                        woTb[:, c * D + n * 512:c * D + (n + 1) * 512],
                        start=(c == 0), stop=False), wait=w)
                if do_tr and 8 <= c < 16:
                    for tsub in range(2):
                        tc = (c - 8) * 2 + tsub
                        wt = ()
                        if tc == 0:
                            wt = (("veO", M["xn2v"]), ("veO", M.get("pst2free", 0)))
                        k.op("tensor", lambda e, tc=tc: e.transpose(
                            psT2[:, tc * 128:(tc + 1) * 128], xn2b[:, tc * 128:(tc + 1) * 128], identb[:]),
                            wait=wt, inc=(("peO", 1),) if tc == NDC - 1 else ())
            if do_tr:
                M["trdone"] = k.now("peO")
            for n in range(4):
                k.op("tensor", lambda e, n=n: e.matmul(
                    psO[:, n * 512:(n + 1) * 512], onesr_b[0:1, :],
                    biasr[0:1, BO0 + n * 512:BO0 + (n + 1) * 512],
                    start=False, stop=True),
                    inc=(("peO", 1),) if n == 3 else ())
            M["aofree%d" % par] = k.now("peO")
            if do_tr:
                pst3 = psT2.rearrange("p (c q) -> p c q", q=128)
                k.op("vector", lambda e, j=j: e.tensor_copy(xn2f4[:, :, j - 1, :], pst3[:, :, :]),
                     wait=(("peO", M["trdone"]), ("veO", k.now("veO"))), inc=(("veO", 1),))
                M["pst2free"] = k.now("veO")
            k.op("scalar", lambda e: e.activation(ybuf[:], psO[:], AF.Identity),
                 wait=(("peO", M["aofree%d" % par]), ("veO", k.now("veO")), ("seO", k.now("seO"))),
                 inc=(("seO", 1),))
            M["psOfree_s"] = k.now("seO")

        def b2_post(j):
            par = j % 2
            sev = M["psOfree_s"]
            k.op("vector", lambda e: e.tensor_reduce(stat[:, 0:1], ybuf[:], AX.X, ALU.add),
                 wait=(("seO", sev), ("veA", k.now("veA")), ("seA", k.now("seA"))),
                 inc=(("veO", 1),))
            k.op("scalar", lambda e: e.activation(sqsc[:], ybuf[:], AF.Square, accum_out=stat[:, 1:2]),
                 wait=(("seO", k.now("seO")), ("seA", k.now("seA"))),
                 inc=(("seO", 1),))
            vw, sw = k.now("veO"), k.now("seO")
            k.op("vector", lambda e: e.tensor_scalar(stat[:, 2:3], stat[:, 0:1], 1.0 / D, None, ALU.mult),
                 wait=(("veO", vw), ("seO", sw)))
            k.op("vector", lambda e: e.tensor_scalar(stat[:, 3:4], stat[:, 1:2], 1.0 / D, None, ALU.mult),
                 inc=(("veO", 1),))
            k.op("vector", lambda e: e.tensor_mul(stat[:, 4:5], stat[:, 2:3], stat[:, 2:3]),
                 wait=(("veO", k.now("veO")),))
            k.op("vector", lambda e: e.tensor_sub(stat[:, 4:5], stat[:, 3:4], stat[:, 4:5]),
                 inc=(("veO", 1),))
            k.op("scalar", lambda e: e.activation(stat[:, 5:6], stat[:, 4:5], AF.Ln, bias=eps_ln[:, 0:1]),
                 wait=(("veO", k.now("veO")), ("seO", k.now("seO"))), inc=(("seO", 1),))
            k.op("scalar", lambda e: e.activation(stat[:, 5:6], stat[:, 5:6], AF.Exp, scale=-0.5),
                 wait=(("seO", k.now("seO")),), inc=(("seO", 1),))
            k.op("vector", lambda e: e.scalar_tensor_tensor(stat[:, 6:7], stat[:, 2:3], -1.0, stat[:, 5:6],
                                                            ALU.mult, ALU.mult),
                 wait=(("seO", k.now("seO")),), inc=(("veO", 1),))
            k.op("vector", lambda e, par=par: e.tensor_scalar(radj[:], xstg[par][:], stat[:, 6:7], None, ALU.add),
                 wait=(("bx%d" % par, k.now("bx%d" % par)), ("veO", k.now("veO"))), inc=(("veO", 1),))
            M["bxfree%d" % par] = k.now("veO")
            k.op("vector", lambda e: e.scalar_tensor_tensor(x2t[:], ybuf[:], stat[:, 5:6], radj[:],
                                                            ALU.mult, ALU.add),
                 wait=(("x2w", k.now("x2w")), ("veO", k.now("veO"))), inc=(("veO", 1),))
            k.dma("sync", X2d[j * 128:(j + 1) * 128, :], x2t[:], "x2w", wait=(("veO", k.now("veO")),))
            vln2 = ln_tile(x2t[:], xn2b[:], "veO", "seO", extra_wait=(("veO", k.now("veO")),))
            M["xn2v"] = vln2

        b2_dma(0)
        b2_dma(1)
        for j in range(TPC):
            b2_mm(j)
            b2_post(j)
            if j + 2 < TPC:
                b2_dma(j + 2)
        for tc in range(NDC):
            wt = (("veO", M["xn2v"]), ("veO", M.get("pst2free", 0))) if tc == 0 else ()
            k.op("tensor", lambda e, tc=tc: e.transpose(
                psT2[:, tc * 128:(tc + 1) * 128], xn2b[:, tc * 128:(tc + 1) * 128], identb[:]),
                wait=wt, inc=(("peO", 1),) if tc == NDC - 1 else ())
        pst3 = psT2.rearrange("p (c q) -> p c q", q=128)
        k.op("vector", lambda e: e.tensor_copy(xn2f4[:, :, TPC - 1, :], pst3[:, :, :]),
             wait=(("peO", k.now("peO")), ("veO", k.now("veO"))), inc=(("veO", 1),))
    X2W = k.now("x2w")
    XN2_DONE = (("veO", k.now("veO")),)

    while nld[0] < len(s_chunks):
        sched_s_load()
        sched_s_reduce("vector")
    while nrd[0] < nld[0]:
        sched_s_reduce("vector")
    with contextlib.ExitStack() as st1:
        psS = PS(st1, "psS", (1, 4))
        psS2 = PS(st1, "psS2", (128, 4))
        s_row = SB(st1, "s_row", (1, 4), F32)
        k.op("tensor", lambda e: e.matmul(psS[0:1, 0:3], onesc_f[:, 0:1], sS[:, 0:3], start=True, stop=True),
             wait=(("poS", k.now("poS")),), inc=(("peC", 1),))
        k.op("vector", lambda e: e.tensor_scalar(s_row[0:1, 0:3], psS[0:1, 0:3], INW1, None, ALU.mult),
             wait=(("peC", k.now("peC")),))
        k.op("vector", lambda e: e.scalar_tensor_tensor(s_row[0:1, 0:3], psS[0:1, 0:3], INW2, s_row[0:1, 0:3],
                                                        ALU.mult, ALU.add),
             inc=(("veC", 1),))
        k.op("tensor", lambda e: e.matmul(psS2[:, 0:3], onesr_f[0:1, :], s_row[0:1, 0:3], start=True, stop=True),
             wait=(("veC", k.now("veC")),), inc=(("peC", 1),))
        k.op("vector", lambda e: e.tensor_copy(s_bc[:, 0:3], psS2[:, 0:3]),
             wait=(("peC", k.now("peC")),), inc=(("veC", 1),))
        k.op("vector", lambda e: e.tensor_scalar(th_p[:, 0:3], s_bc[:, 0:3], 0.5, None, ALU.mult),
             wait=(("veC", k.now("veC")),))
        k.op("vector", lambda e: e.tensor_scalar(th_n[:, 0:3], s_bc[:, 0:3], -0.5, None, ALU.mult),
             inc=(("veC", 1),))
    TH = k.now("veC")

    with contextlib.ExitStack() as stA:
        w13stg = [SB(stA, f"w13stg{i}", (128, 2 * D)) for i in range(2)]
        q13 = [SB(stA, f"q13_{i}", (128, 2 * D), BF16) for i in range(2)]
        posb = SB(stA, "posb", (128, D), BF16)
        negb = SB(stA, "negb", (128, D), BF16)
        silu_s = SB(stA, "silu_s", (128, 1024), F32)
        g_st = SB(stA, "g_st", (128, 1024), BF16)
        psH1 = [PS(stA, f"psH1{i}", (128, 1024)) for i in range(2)]
        psH3 = [PS(stA, f"psH3{i}", (128, 1024)) for i in range(2)]
        def a_front(hc):
            par = hc % 2
            hcp = 128 if hc < NHC - 1 else LAST_HC
            for half, (wTsrc, thi, qu, snm) in enumerate(((w1T, 0, "sync", "wa"),
                                                          (w3T, 2, "scalar", "wc"))):
                sem = "%s%d" % (snm, par)
                wc = wTsrc[:, hc * 128:hc * 128 + hcp].rearrange("(c p) h -> p c h", p=128)
                w13dst = w13stg[par][:, half * D:half * D + NDC * hcp].rearrange("p (c h) -> p c h", h=hcp)
                k.dma(qu, w13dst[:, :, :], wc, sem,
                      wait=(("veC", M.get("wafree%d_%d" % (par, half), 0)),) + (k.fence() if hc < 2 else ()))
                WL = k.now(sem)
                qw = ((sem, WL), ("veC", TH), ("poS", k.now("poS")))
                wq = NDC * hcp if hc < 2 else D
                k.op("vector", lambda e, par=par, wq=wq, thi=thi, half=half: e.tensor_scalar(
                    posb[:, 0:wq], w13stg[par][:, half * D:half * D + wq], th_p[:, thi:thi + 1], None, ALU.is_gt),
                    wait=qw)
                k.op("vector", lambda e, par=par, wq=wq, thi=thi, half=half: e.tensor_scalar(
                    negb[:, 0:wq], w13stg[par][:, half * D:half * D + wq], th_n[:, thi:thi + 1], None, ALU.is_lt),
                    inc=(("veC", 1),))
                M["wafree%d_%d" % (par, half)] = k.now("veC")
                k.op("vector", lambda e, par=par, wq=wq, half=half: e.tensor_sub(
                    q13[par][:, half * D:half * D + wq], posb[:, 0:wq], negb[:, 0:wq]),
                    wait=(("veC", k.now("veC")), ("poS", k.now("poS")),
                          ("seC", M.get("psOfree_s", 0)),
                          ("peC", M.get("q13free%d" % par, 0))))
            M["qdone%d" % par] = k.now("veC")

        def a_mm(hc):
            par = hc % 2
            hcp = 128 if hc < NHC - 1 else LAST_HC
            w0 = (("veC", M["qdone%d" % par]),) + XN2_DONE + (("seC", M.get("hafree%d" % par, 0)),
                                                             ("veC", M.get("gstt%d" % par, 0)))
            for c in range(NDC):
                for n in range(2):
                    k.op("tensor", lambda e, par=par, c=c, n=n, hcp=hcp: e.matmul(
                        psH1[par][0:hcp, n * 512:(n + 1) * 512],
                        q13[par][:, c * hcp:(c + 1) * hcp],
                        xn2_fm[:, c * 1024 + n * 512:c * 1024 + (n + 1) * 512],
                        start=(c == 0), stop=(c == NDC - 1)),
                        wait=w0 if (c == 0 and n == 0) else ())
            for c in range(NDC):
                for n in range(2):
                    k.op("tensor", lambda e, par=par, c=c, n=n, hcp=hcp: e.matmul(
                        psH3[par][0:hcp, n * 512:(n + 1) * 512],
                        q13[par][:, D + c * hcp:D + (c + 1) * hcp],
                        xn2_fm[:, c * 1024 + n * 512:c * 1024 + (n + 1) * 512],
                        start=(c == 0), stop=(c == NDC - 1)),
                        inc=(("peC", 1),) if (c == NDC - 1 and n == 1) else ())
            M["q13free%d" % par] = k.now("peC")
            M["amm%d" % par] = k.now("peC")

        def a_epi(hc):
            par = hc % 2
            hcp = 128 if hc < NHC - 1 else LAST_HC
            k.op("scalar", lambda e, par=par, hcp=hcp: e.activation(
                silu_s[0:hcp, :], psH1[par][0:hcp, :], AF.Silu, scale=s_bc[0:hcp, 0:1]),
                wait=(("peC", M["amm%d" % par]), ("seC", k.now("seC")), ("veC", M.get("gw", 0))),
                inc=(("seC", 1),))
            M["hafree%d" % par] = k.now("seC")
            k.op("vector", lambda e, par=par, hc=hc, hcp=hcp: e.scalar_tensor_tensor(
                g_st[0:hcp, :], psH3[par][0:hcp, :], s_bc[0:hcp, 2:3], silu_s[0:hcp, :], ALU.mult, ALU.mult),
                wait=(("seC", k.now("seC")), ("gws", k.now("gws")), ("veC", k.now("veC"))),
                inc=(("veC", 1),))
            M["gstt%d" % par] = k.now("veC")
            M["gw"] = k.now("veC")
            k.dma("sync", Gsc[hc * 128:hc * 128 + hcp, :], g_st[0:hcp, :], "gws",
                  wait=(("veC", k.now("veC")),))

        a_front(0)
        a_front(1)
        for hc in range(NHC):
            a_mm(hc)
            if hc + 2 < NHC:
                a_front(hc + 2)
            a_epi(hc)
        G_DONE = k.now("gws")

    with contextlib.ExitStack() as stBB:
        w2stg = [SB(stBB, f"w2stg{i}", (128, 512)) for i in range(2)]
        q2b = [SB(stBB, f"q2b{i}", (128, 512), BF16) for i in range(2)]
        pos2 = SB(stBB, "pos2", (128, 512), BF16)
        neg2 = SB(stBB, "neg2", (128, 512), BF16)
        gstg = [SB(stBB, f"gstg{i}", (128, 1024), BF16) for i in range(2)]
        y2ev = SB(stBB, "y2ev", (128, 512), F32)
        psY = [PS(stBB, f"psY{i}", (128, 512)) for i in range(TPC)]
        def b_front(dc, hc):
            par = hc % 2
            hcp = 128 if hc < NHC - 1 else LAST_HC
            k.dma("sync", w2stg[par][0:hcp, :], w2T[hc * 128:hc * 128 + hcp, dc * 512:(dc + 1) * 512],
                  "wb%d" % par, wait=(("veC", M.get("wbfree%d" % par, 0)),)
                  + (k.fence() if dc == 0 and hc < 2 else ()))
            k.dma("scalar", gstg[par][0:hcp, :], Gsc[hc * 128:hc * 128 + hcp, :],
                  "gl%d" % par, wait=(("gws", G_DONE), ("peC", M.get("glfree%d" % par, 0)))
                  + (k.fence() if dc == 0 and hc < 2 else ()))
            M["gld%d" % par] = k.now("gl%d" % par)
            qw = (("wb%d" % par, k.now("wb%d" % par)), ("veC", TH),
                  ("poS", k.now("poS")))
            k.op("vector", lambda e, par=par, hcp=hcp: e.tensor_scalar(
                pos2[0:hcp, :], w2stg[par][0:hcp, :], th_p[0:hcp, 1:2], None, ALU.is_gt), wait=qw)
            k.op("vector", lambda e, par=par, hcp=hcp: e.tensor_scalar(
                neg2[0:hcp, :], w2stg[par][0:hcp, :], th_n[0:hcp, 1:2], None, ALU.is_lt),
                inc=(("veC", 1),))
            M["wbfree%d" % par] = k.now("veC")
            k.op("vector", lambda e, par=par, hcp=hcp: e.tensor_sub(
                q2b[par][0:hcp, :], pos2[0:hcp, :], neg2[0:hcp, :]),
                wait=(("veC", k.now("veC")), ("poS", k.now("poS")),
                      ("peC", M.get("q2free%d" % par, 0))))
            M["q2done%d" % par] = k.now("veC")

        def b_mm(dc, hc):
            par = hc % 2
            hcp = 128 if hc < NHC - 1 else LAST_HC
            w0 = (("veC", M["q2done%d" % par]), ("gl%d" % par, M["gld%d" % par]))
            if hc == 0:
                w0 = w0 + (("veC", M.get("yev", 0)),)
            for j in range(TPC):
                k.op("tensor", lambda e, par=par, j=j, hc=hc, hcp=hcp: e.matmul(
                    psY[j][:], gstg[par][0:hcp, j * 128:(j + 1) * 128], q2b[par][0:hcp, :],
                    start=(hc == 0), stop=(hc == NHC - 1)),
                    wait=w0 if j == 0 else (),
                    inc=(("peC", 1),) if j == TPC - 1 else ())
            M["q2free%d" % par] = k.now("peC")
            M["glfree%d" % par] = k.now("peC")

        for dc in range(4):
            b_front(dc, 0)
            b_front(dc, 1)
            for hc in range(NHC):
                b_mm(dc, hc)
                if hc + 2 < NHC:
                    b_front(dc, hc + 2)
            pev = k.now("peC")
            for j in range(TPC):
                k.op("vector", lambda e, j=j, dc=dc: e.tensor_scalar(
                    y2ev[:], psY[j][:], s_bc[:, 1:2], None, ALU.mult),
                    wait=(("peC", pev), ("y2w", k.now("y2w")), ("veC", k.now("veC"))), inc=(("veC", 1),))
                k.dma("sync", Y2sc[:, j * D + dc * 512:j * D + (dc + 1) * 512], y2ev[:], "y2w",
                      wait=(("veC", k.now("veC")),))
            M["yev"] = k.now("veC")
        Y2W = k.now("y2w")

    with contextlib.ExitStack() as stE:
        y2stg = [SB(stE, f"y2stg{i}", (128, D)) for i in range(2)]
        lnout = SB(stE, "lnout", (128, D), F32)
        outt = SB(stE, "outt", (128, D), F32)
        for j in range(TPC):
            par = j % 2
            k.dma("sync", xstg[par][:], X2d[j * 128:(j + 1) * 128, :], "ex%d" % par,
                  wait=(("x2w", X2W), ("veO", M.get("bxfree%d" % par, 0)),
                        ("veC", M.get("exfree%d" % par, 0))) + (k.fence() if j < 2 else ()))
            k.dma("scalar", y2stg[par][:], Y2sc[:, j * D:(j + 1) * D], "ey%d" % par,
                  wait=(("y2w", Y2W), ("veC", M.get("eyfree%d" % par, 0))) + (k.fence() if j < 2 else ()))
            vln3 = ln_tile(y2stg[par][:], lnout[:], "veC", "seC",
                           extra_wait=(("ey%d" % par, k.now("ey%d" % par)),
                                       ("veC", k.now("veC")), ("veO", k.now("veO")),
                                       ("seO", k.now("seO"))))
            M["eyfree%d" % par] = vln3
            k.op("vector", lambda e, par=par: e.tensor_tensor(outt[:], xstg[par][:], lnout[:], ALU.add),
                 wait=(("veC", vln3), ("ex%d" % par, k.now("ex%d" % par)),
                       ("outw", k.now("outw"))),
                 inc=(("veC", 1),))
            M["exfree%d" % par] = k.now("veC")
            k.dma("sync", out[j * 128:(j + 1) * 128, :], outt[:], "outw", wait=(("veC", k.now("veC")),))
    k.op("sync", lambda e: e.wait_ge(k.sems["outw"], k.now("outw")))

    block = es.enter_context(nc.Block())
    k.emit(block)
    es.close()
    return nc


def _trig_tables(pos):
    invf = 10000.0 ** (-np.arange(64, dtype=np.float64) / 64.0)
    fr = pos[:, :, None] * invf[None, None, :]
    k2s = np.round(fr / np.pi)
    sgs = 1.0 - 2.0 * (np.abs(k2s) % 2)
    k2c = np.round((fr + np.pi / 2) / np.pi)
    sgc = 1.0 - 2.0 * (np.abs(k2c) % 2)
    tob = lambda a: np.ascontiguousarray(a.transpose(1, 0, 2).reshape(128, -1).astype(np.float32))
    return tob(k2s), tob(sgs), tob(k2c), tob(sgc)


def _prep(inputs):
    x = np.ascontiguousarray(np.asarray(inputs["L0_attn_x"], np.float32).reshape(S, D))
    d = {}
    d["wqT"] = np.ascontiguousarray(np.asarray(inputs["L0_attn_Wq"], np.float32).T)
    d["wkT"] = np.ascontiguousarray(np.asarray(inputs["L0_attn_Wk"], np.float32).T)
    d["wvT"] = np.ascontiguousarray(np.asarray(inputs["L0_attn_Wv"], np.float32).T)
    d["woT"] = np.ascontiguousarray(np.asarray(inputs["L0_attn_Wo"], np.float32).T)
    d["bqr"] = np.asarray(inputs["L0_attn_bq"], np.float32).reshape(1, D)
    d["bkr"] = np.asarray(inputs["L0_attn_bk"], np.float32).reshape(1, KVD)
    d["bvr"] = np.asarray(inputs["L0_attn_bv"], np.float32).reshape(1, KVD)
    d["bor"] = np.asarray(inputs["L0_attn_bo"], np.float32).reshape(1, D)
    d["gain"] = np.asarray(inputs["L0_attn_q_gain"], np.float32).reshape(1, NH)
    d["w1T"] = np.ascontiguousarray(np.asarray(inputs["L0_ffn_w1"], np.float32).T)
    d["w3T"] = np.ascontiguousarray(np.asarray(inputs["L0_ffn_w3"], np.float32).T)
    d["w2T"] = np.ascontiguousarray(np.asarray(inputs["L0_ffn_w2"], np.float32).T)
    d["identity"] = np.eye(128).astype(ml_dtypes.bfloat16)
    d["irow"] = np.arange(64, dtype=np.float32).reshape(1, 64)
    pos = np.arange(S, dtype=np.float64).reshape(NT, 128)
    d["k2s"], d["sgs"], d["k2c"], d["sgc"] = _trig_tables(pos)
    d["tokix"] = np.ascontiguousarray(pos.T.astype(np.float32))
    d["_xfTt"] = np.ascontiguousarray(x.T.reshape(D, NT, 128).transpose(1, 0, 2))
    return x, d


def _core_inputs(x, shared, s_):
    xt = x.reshape(NT, 128, D)
    pos = np.arange(S, dtype=np.float64).reshape(NT, 128)
    m = dict(shared)
    m["xf"] = x
    m["xfTt"] = shared["_xfTt"]
    m.pop("_xfTt", None)
    own = np.arange(s_, NT, NCORE)
    m["xo"] = np.ascontiguousarray(xt[own].reshape(TPC * 128, D))
    m["xoTt"] = np.ascontiguousarray(xt[own].transpose(0, 2, 1))
    posq = pos[own]
    m["tokq"] = np.ascontiguousarray(posq.T.astype(np.float32))
    m["qk2s"], m["qsgs"], m["qk2c"], m["qsgc"] = _trig_tables(posq)
    bn = np.zeros((16, 128, 1024), np.float32)
    for a in range(4):
        for ki in range(4):
            kb = 4 * a + ki
            for kc in range(4):
                kglob = kb * 512 + kc * 128 + np.arange(128)
                for sub in range(2):
                    jslot = 2 * a + sub
                    qglob = (8 * jslot + s_) * 128 + np.arange(128)
                    bn[a * 4 + ki, :, kc * 256 + sub * 128:kc * 256 + (sub + 1) * 128] = (
                        kglob[:, None] <= qglob[None, :])
    m["binm"] = bn.astype(ml_dtypes.bfloat16)
    return m


def kernel(**inputs):
    x, shared = _prep(inputs)
    nc = build_nc(debug=False)
    in_maps = [_core_inputs(x, shared, s_) for s_ in range(NCORE)]
    res = run_bass_kernel_spmd(nc, in_maps, core_ids=list(range(NCORE)))
    outf = np.empty((S, D), np.float32)
    outv = outf.reshape(NT, 128, D)
    for s_ in range(NCORE):
        outv[s_::NCORE] = np.asarray(res.results[s_]["out"]).reshape(TPC, 128, D)
    return outf.reshape(1, S, D)


# version 19
# speedup vs baseline: 1.4078x; 1.0276x over previous
import contextlib
import numpy as np
import ml_dtypes
import concourse.bass as bass
import concourse.mybir as mybir
from concourse.bass_utils import run_bass_kernel_spmd

F32 = mybir.dt.float32
BF16 = mybir.dt.bfloat16
AF = mybir.ActivationFunctionType
ALU = mybir.AluOpType
AX = mybir.AxisListType

NCORE = 8
S, D, HID = 8192, 2048, 5461
NH, KVH, HD = 16, 4, 128
KVD = KVH * HD
NT = S // 128
TPC = NT // NCORE
NDC = D // 128
NHC = (HID + 127) // 128
LAST_HC = HID - 128 * (NHC - 1)
EPS_LN = 1e-5
EPS_RMS = 1.1920929e-07
GB_MUL = 11.32
N_W = HID * D


def _split2(v):
    c1 = np.float32(v)
    return float(c1), float(np.float32(np.float64(v) - np.float64(c1)))


PI1 = float(np.float32(3.140625))
_pr = np.float64(np.pi) - np.float64(PI1)
PI2 = float(np.float32(_pr))
PI3 = float(np.float32(_pr - np.float64(np.float32(_pr))))
HP1, HP2 = _split2(np.pi / 2)
LNB1, LNB2 = _split2(-np.log(10000.0) / 64.0)
ISQ1, ISQ2 = _split2(1.0 / np.sqrt(128.0))
INW1, INW2 = _split2(1.0 / N_W)


class Sched:
    ENGS = ("sync", "gpsimd", "vector", "scalar", "tensor")

    def __init__(self, nc, es):
        self.nc, self.es = nc, es
        self.ops = {e: [] for e in self.ENGS}
        self.dma_sems = set()
        self.cnt = {}
        self.mark = {}
        self.sems = {}

    def _sem(self, name):
        if name not in self.sems:
            self.sems[name] = self.es.enter_context(self.nc.semaphore(name))
            self.cnt[name] = 0
        return self.sems[name]

    def now(self, name):
        name = self._tr(name)
        self._sem(name)
        return self.cnt[name]

    AUTO = {"vector": "auV", "scalar": "auS", "gpsimd": "auG"}

    @staticmethod
    def _tr(n):
        if n.startswith("ve"):
            return "auV"
        if n.startswith("se"):
            return "auS"
        if n.startswith("po"):
            return "auG"
        if n.startswith("pe"):
            return "pe"
        return n

    def op(self, eng, fn, wait=(), inc=(), dma=False):
        wait = tuple((self._tr(n), v) for n, v in wait)
        inc = tuple((self._tr(n), kk) for n, kk in inc
                    if self._tr(n) not in ("auV", "auS", "auG"))
        if not dma and eng in self.AUTO:
            au = self.AUTO[eng]
            self._sem(au)
            wait = wait + ((au, self.cnt[au]),)
            inc = inc + ((au, 1),)
        seen = set()
        wait2 = []
        for n, v in wait:
            self._sem(n)
            assert isinstance(v, int), (n, v)
            if (n, v) not in seen:
                seen.add((n, v))
                wait2.append((n, v))
        best = {}
        for n, v in wait2:
            best[n] = max(best.get(n, -1), v)
        wait = tuple(best.items())
        for n, kk in inc:
            self._sem(n)
            self.cnt[n] += kk
        self.ops[eng].append((fn, wait, inc))

    def dma(self, eng, out, in_, sem, wait=()):
        self.dma_sems.add(sem)
        self.op(eng, lambda e, o=out, i=in_: e.dma_start(out=o, in_=i),
                wait=wait, inc=((sem, 16),), dma=True)

    def fence(self):
        names = {"auV", "auS", "auG", "pe"} | self.dma_sems
        return tuple((n, self.cnt[n]) for n in names if n in self.cnt)

    def emit(self, block):
        for eng_name in self.ENGS:
            items = self.ops[eng_name]
            if not items:
                continue

            def body(e, items=items):
                lw = {}
                for fn, waits, incs in items:
                    for n, v in waits:
                        if lw.get(n, -1) < v:
                            e.wait_ge(self.sems[n], v)
                            lw[n] = v
                    instr = fn(e)
                    for n, kk in incs:
                        instr = instr.then_inc(self.sems[n], kk)

            getattr(block, eng_name)(body)


def build_nc(debug=False):
    nc = bass.Bass("TRN2", target_bir_lowering=False)
    t = lambda name, shape, dt=F32: nc.dram_tensor(name, list(shape), dt, kind="ExternalInput").ap()
    xf = t("xf", (S, D))
    xfTt = t("xfTt", (NT, D, 128))
    xo = t("xo", (TPC * 128, D))
    xoTt = t("xoTt", (TPC, D, 128))
    wqT = t("wqT", (D, D)); wkT = t("wkT", (D, KVD)); wvT = t("wvT", (D, KVD))
    woT = t("woT", (D, D))
    bqr = t("bqr", (1, D)); bkr = t("bkr", (1, KVD)); bvr = t("bvr", (1, KVD)); bor = t("bor", (1, D))
    gain = t("gain", (1, NH))
    w1T = t("w1T", (D, HID)); w3T = t("w3T", (D, HID)); w2T = t("w2T", (HID, D))
    identity = t("identity", (128, 128), BF16)
    ksf = t("ksf", (128, NT * 64)); kcf = t("kcf", (128, NT * 64))
    qsf = t("qsf", (128, TPC * 64)); qcf = t("qcf", (128, TPC * 64))
    binm = t("binm", (16, 128, 1024), BF16)
    out = nc.dram_tensor("out", [TPC * 128, D], F32, kind="ExternalOutput").ap()
    skind = "ExternalOutput" if debug else "Internal"
    Kd = nc.dram_tensor("Kd", [KVH, 128, S], BF16, kind=skind).ap()
    Vd = nc.dram_tensor("Vd", [KVH, S, HD], BF16, kind=skind).ap()
    Qsc = nc.dram_tensor("Qsc", [128, NH * TPC * 128], BF16, kind=skind).ap()
    AOsc = nc.dram_tensor("AOsc", [128, NH * TPC * 128], BF16, kind=skind).ap()
    X2d = nc.dram_tensor("X2d", [TPC * 128, D], F32, kind=skind).ap()
    Gsc = nc.dram_tensor("Gsc", [NHC * 128, TPC * 128], BF16, kind=skind).ap()
    Y2sc = nc.dram_tensor("Y2sc", [128, TPC * D], F32, kind=skind).ap()

    es = contextlib.ExitStack()
    k = Sched(nc, es)
    M = k.mark

    def SB(st, n, s_, d=F32):
        return st.enter_context(nc.sbuf_tensor(n, list(s_), d))

    def PS(st, n, s_, d=F32):
        return st.enter_context(nc.psum_tensor(n, list(s_), d))

    identb = SB(es, "identb", (128, 128), BF16)
    onesc_b = SB(es, "onesc_b", (128, 1), BF16)
    onesr_b = SB(es, "onesr_b", (1, 128), BF16)
    onesr_f = SB(es, "onesr_f", (1, 128), F32)
    onesc_f = SB(es, "onesc_f", (128, 1), F32)
    eps_ln = SB(es, "eps_ln", (128, 1), F32)
    eps_rms = SB(es, "eps_rms", (128, 1), F32)
    guard_c = SB(es, "guard_c", (1, 1), F32)
    biasr = SB(es, "biasr", (1, 2 * D + 2 * KVD), BF16)
    BQ0, BK0, BV0, BO0 = 0, D, D + KVD, D + 2 * KVD
    gain_r = SB(es, "gain_r", (1, NH), F32)
    negm_row = SB(es, "negm_row", (1, NH), F32)
    negm_bc = SB(es, "negm_bc", (128, NH), F32)
    mult_g = SB(es, "mult_g", (1, NH), F32)
    gs_bc = SB(es, "gs_bc", (128, NH), F32)
    sS = SB(es, "sS", (128, 4), F32)
    th_p = SB(es, "th_p", (128, 4), F32)
    th_n = SB(es, "th_n", (128, 4), F32)
    s_bc = SB(es, "s_bc", (128, 4), F32)
    stat = SB(es, "stat", (128, 8), F32)
    sqsc = SB(es, "sqsc", (128, D), F32)
    xstg = [SB(es, f"xstg{i}", (128, D)) for i in range(2)]
    rmsd = SB(es, "rmsd", (128, NH), F32)
    ropeu = SB(es, "ropeu", (128, NH * 64), BF16)
    ropev = SB(es, "ropev", (128, NH * 64), BF16)
    ropeu3 = ropeu.rearrange("p (h d) -> p h d", d=64)
    ropev3 = ropev.rearrange("p (h d) -> p h d", d=64)

    k.dma("sync", identb[:], identity, "cld")
    k.dma("sync", gain_r[:], gain, "cld")
    CLD = k.now("cld")
    for fn in (lambda e: e.memset(onesc_b[:], 1.0), lambda e: e.memset(onesr_b[:], 1.0),
               lambda e: e.memset(onesr_f[:], 1.0), lambda e: e.memset(onesc_f[:], 1.0),
               lambda e: e.memset(eps_ln[:], EPS_LN), lambda e: e.memset(eps_rms[:], EPS_RMS),
               lambda e: e.memset(guard_c[:], 1e-37)):
        k.op("vector", fn)
    k.op("vector", lambda e: e.memset(sS[:], 0.0))
    M_SS = k.now("veA")
    k.dma("gpsimd", biasr[0:1, BQ0:BQ0 + D], bqr, "bld")
    k.dma("gpsimd", biasr[0:1, BK0:BK0 + KVD], bkr, "bld")
    k.dma("gpsimd", biasr[0:1, BV0:BV0 + KVD], bvr, "bld")
    k.dma("gpsimd", biasr[0:1, BO0:BO0 + D], bor, "bld")
    BLD = k.now("bld")
    k.op("vector", lambda e: e.tensor_scalar(mult_g[:], gain_r[:], ISQ1, None, ALU.mult),
         wait=(("cld", CLD), ("veA", k.now("veA"))))
    k.op("vector", lambda e: e.scalar_tensor_tensor(mult_g[:], gain_r[:], ISQ2, mult_g[:], ALU.mult, ALU.add),
         inc=(("veA", 1),))
    k.op("vector", lambda e: e.tensor_scalar(negm_row[:], gain_r[:], -1.0, None, ALU.mult),
         wait=(("veA", k.now("veA")),))
    k.op("vector", lambda e: e.tensor_max(negm_row[:], negm_row[:], gain_r[:]))
    k.op("vector", lambda e: e.tensor_scalar(negm_row[:], negm_row[:], -GB_MUL, -1.0, ALU.mult, ALU.add),
         inc=(("veA", 1),))
    VA = k.now("veA")
    with contextlib.ExitStack() as st0:
        psB = PS(st0, "psB", (128, NH))
        k.op("tensor", lambda e: e.matmul(psB[:], onesr_f[0:1, :], negm_row[0:1, :], start=True, stop=True),
             wait=(("veA", VA),), inc=(("peA", 1),))
        k.op("vector", lambda e: e.tensor_copy(negm_bc[:], psB[:]),
             wait=(("peA", k.now("peA")),), inc=(("veA", 1),))
        k.op("tensor", lambda e: e.matmul(psB[:], onesr_f[0:1, :], mult_g[0:1, :], start=True, stop=True),
             wait=(("veA", k.now("veA")),), inc=(("peA", 1),))
        k.op("vector", lambda e: e.tensor_copy(gs_bc[:], psB[:]),
             wait=(("peA", k.now("peA")),), inc=(("veA", 1),))
    VE_INVF = k.now("veA")

    def load_trig(st, tag, ntile, src_sin, src_cos, sin_t, cos_t, R):
        n = ntile * 64
        stg = SB(st, f"trst_{tag}", (128, n), F32)
        sem = "tg_" + tag
        sin4 = sin_t.rearrange("p (t r i) -> p t r i", r=R, i=64)
        cos4 = cos_t.rearrange("p (t r i) -> p t r i", r=R, i=64)
        for which, (srcap, dst4) in enumerate(((src_sin, sin4), (src_cos, cos4))):
            k.dma("sync", stg[:], srcap, sem,
                  wait=(k.fence() if which == 0 else (("veA", M.get("trfree_" + tag, 0)),)))
            tl = k.now(sem)
            st3 = stg.rearrange("p (t i) -> p t i", i=64)
            for r_ in range(R):
                k.op("vector", lambda e, r_=r_, dst4=dst4, st3=st3: e.tensor_copy(
                    dst4[:, :, r_, :], st3[:, :, :]),
                    wait=((sem, tl),) if r_ == 0 else (),
                    inc=(("veA", 1),) if r_ == R - 1 else ())
            M["trfree_" + tag] = k.now("veA")

    def ln_tile(xin_ap, xout_ap, tv, ts_, extra_wait=()):
        k.op("vector", lambda e: e.tensor_reduce(stat[:, 0:1], xin_ap, AX.X, ALU.add),
             wait=extra_wait + ((tv, k.now(tv)),), inc=((tv, 1),))
        k.op("scalar", lambda e: e.activation(sqsc[:], xin_ap, AF.Square, accum_out=stat[:, 1:2]),
             wait=extra_wait + ((tv, k.now(tv)), (ts_, k.now(ts_))), inc=((ts_, 1),))
        vw, sw = k.now(tv), k.now(ts_)
        k.op("vector", lambda e: e.tensor_scalar(stat[:, 2:3], stat[:, 0:1], 1.0 / D, None, ALU.mult),
             wait=((tv, vw), (ts_, sw)))
        k.op("vector", lambda e: e.tensor_scalar(stat[:, 3:4], stat[:, 1:2], 1.0 / D, None, ALU.mult),
             inc=((tv, 1),))
        k.op("vector", lambda e: e.tensor_mul(stat[:, 4:5], stat[:, 2:3], stat[:, 2:3]),
             wait=((tv, k.now(tv)),))
        k.op("vector", lambda e: e.tensor_sub(stat[:, 4:5], stat[:, 3:4], stat[:, 4:5]),
             inc=((tv, 1),))
        k.op("scalar", lambda e: e.activation(stat[:, 5:6], stat[:, 4:5], AF.Ln, bias=eps_ln[:, 0:1]),
             wait=((tv, k.now(tv)), (ts_, k.now(ts_))), inc=((ts_, 1),))
        k.op("scalar", lambda e: e.activation(stat[:, 5:6], stat[:, 5:6], AF.Exp, scale=-0.5),
             wait=((ts_, k.now(ts_)),), inc=((ts_, 1),))
        k.op("vector", lambda e: e.scalar_tensor_tensor(stat[:, 6:7], stat[:, 2:3], -1.0, stat[:, 5:6], ALU.mult, ALU.mult),
             wait=((ts_, k.now(ts_)),), inc=((tv, 1),))
        k.op("vector", lambda e: e.tensor_scalar(xout_ap, xin_ap, stat[:, 5:6], stat[:, 6:7], ALU.mult, ALU.add),
             wait=((tv, k.now(tv)),), inc=((tv, 1),))
        return k.now(tv)

    def rope(src, dst, nheads, sin_ap, cos_ap, tv, wait0, eng="vector", rep=None):
        rep = nheads if rep is None else rep
        ng = nheads // rep
        s3 = src.rearrange("p (h d) -> p h d", d=HD)
        d3 = dst.rearrange("p (h d) -> p h d", d=HD)
        sin3 = sin_ap.rearrange("p (h i) -> p h i", i=64)
        cos3 = cos_ap.rearrange("p (h i) -> p h i", i=64)
        for gi in range(ng):
            hh = slice(gi * rep, (gi + 1) * rep)
            t1, t2 = s3[:, hh, 0:64], s3[:, hh, 64:128]
            k.op(eng, lambda e, hh=hh, t1=t1: e.tensor_tensor(d3[:, hh, 0:64], t1, cos3, ALU.mult),
                 wait=wait0 if gi == 0 else ())
            k.op(eng, lambda e, hh=hh, t2=t2: e.tensor_tensor(d3[:, hh, 64:128], t2, cos3, ALU.mult))
            k.op(eng, lambda e, hh=hh, t2=t2: e.tensor_tensor(ropeu3[:, hh, :], t2, sin3, ALU.mult))
            k.op(eng, lambda e, hh=hh, t1=t1: e.tensor_tensor(ropev3[:, hh, :], t1, sin3, ALU.mult),
                 inc=((tv, 1),) if gi == ng - 1 else ())
        vv = k.now(tv)
        for gi in range(ng):
            hh = slice(gi * rep, (gi + 1) * rep)
            k.op(eng, lambda e, hh=hh: e.tensor_tensor(d3[:, hh, 0:64], d3[:, hh, 0:64], ropeu3[:, hh, :], ALU.add),
                 wait=((tv, vv),) if gi == 0 else ())
            k.op(eng, lambda e, hh=hh: e.tensor_tensor(d3[:, hh, 64:128], d3[:, hh, 64:128], ropev3[:, hh, :], ALU.subtract),
                 inc=((tv, 1),) if gi == ng - 1 else ())
        return k.now(tv)

    SPW = 2048
    NROW = N_W // 128
    NCH1 = (NROW + SPW - 1) // SPW
    s_chunks = []
    for mi, wT in ((0, w1T), (2, w3T), (1, w2T)):
        flat = wT.flatten().rearrange("(p r) -> p r", p=128)
        for cc in range(NCH1):
            c0 = cc * SPW
            c1 = min(NROW, c0 + SPW)
            s_chunks.append((mi, flat[:, c0:c1], c1 - c0))

    spstg = [SB(es, f"spstg{i}", (128, SPW)) for i in range(2)]
    spred = SB(es, "spred", (128, 1), F32)
    spredV = SB(es, "spredV", (128, 1), F32)
    spabs = sqsc
    nsp = [0]

    nld = [0]
    nrd = [0]

    def sched_s_load():
        idx = nld[0]
        if idx >= len(s_chunks):
            return
        nld[0] += 1
        mi, ap, w = s_chunks[idx]
        par = idx % 2
        k.dma("sync", spstg[par][:, 0:w], ap, "sp%d" % par,
              wait=(M.get("spfree%d" % par, ("veA", 0)),))
        M["spld%d" % idx] = k.now("sp%d" % par)

    def sched_s_reduce(eng):
        idx = nrd[0]
        if idx >= nld[0] or idx >= len(s_chunks):
            return
        nrd[0] += 1
        mi, ap, w = s_chunks[idx]
        par = idx % 2
        ldw = (("sp%d" % par, M["spld%d" % idx]),)
        if eng == "scalar":
            k.op("scalar", lambda e, w=w, par=par: e.activation(
                spabs[:, 0:w], spstg[par][:, 0:w], AF.Abs, accum_out=spred[:, 0:1]),
                wait=ldw + (("poS", k.now("poS")), ("veA", M_SS)))
            red, rsem = spred, ("seA", k.now("seA"))
        else:
            k.op("vector", lambda e, w=w, par=par: e.tensor_reduce(
                spredV[:], spstg[par][:, 0:w], AX.X, ALU.add, apply_absolute_value=True),
                wait=ldw + (("poS", k.now("poS")), ("veA", M_SS)))
            red, rsem = spredV, ("veA", k.now("veA"))
        k.op("gpsimd", lambda e, mi=mi, red=red: e.tensor_add(sS[:, mi:mi + 1], sS[:, mi:mi + 1], red[:]),
             wait=(rsem,), inc=(("poS", 1),))
        M["spfree%d" % par] = rsem

    with contextlib.ExitStack() as stS:
        xnfP2 = [PS(stS, f"xnfP{i}", (128, D), BF16) for i in range(2)]
        xTst = [SB(stS, f"xTst{i}", (128, D), F32) for i in range(2)]
        xT_bf = [SB(stS, f"xT_bf{i}", (128, D), BF16) for i in range(2)]
        stat2 = [SB(stS, f"stat2_{i}", (128, 32), F32) for i in range(2)]

        def front_dma(par, xsrc_tok, xsrc_ft, xsem, tsem):
            k.dma("sync", xstg[par][:], xsrc_tok, xsem,
                  wait=(("veA", M.get("xfree%d" % par, 0)),))
            k.dma("scalar", xTst[par].rearrange("p (c q) -> p c q", q=128), xsrc_ft,
                  tsem, wait=(("seA", M.get("xtc%d" % par, 0)),))

        def front_cast(par, tsem):
            k.op("scalar", lambda e, par=par: e.activation(xT_bf[par][:], xTst[par][:], AF.Identity),
                 wait=((tsem, k.now(tsem)), ("peA", M.get("xTfree%d" % par, 0))),
                 inc=(("seA", 1),))
            M["xtc%d" % par] = k.now("seA")
            M["cast%d" % par] = k.now("seA")

        def front_stats(par, xsem):
            st_ = stat2[par]
            ldw = ((xsem, k.now(xsem)),)
            for i4 in range(4):
                k.op("vector", lambda e, st_=st_, par=par, i4=i4: e.bn_stats(
                    st_[:, i4 * 6:(i4 + 1) * 6], xstg[par][:, i4 * 512:(i4 + 1) * 512]),
                    wait=ldw if i4 == 0 else ())
            M["lnsq%d" % par] = k.now("veA")
            M["xfree%d" % par] = k.now("veA")

        def stats_back(t):
            par = t % 2
            st_ = stat2[par]
            k.op("vector", lambda e, st_=st_: e.bn_aggr(st_[:, 24:26], st_[:, 0:24]),
                 wait=(("veA", M["lnsq%d" % par]),))
            k.op("scalar", lambda e, st_=st_: e.activation(st_[:, 27:28], st_[:, 25:26], AF.Ln,
                                                           bias=eps_ln[:, 0:1]),
                 wait=(("veA", k.now("veA")),))
            k.op("scalar", lambda e, st_=st_: e.activation(st_[:, 27:28], st_[:, 27:28], AF.Exp, scale=-0.5),
                 inc=(("seA", 1),))
            k.op("vector", lambda e, st_=st_: e.tensor_scalar(st_[:, 28:29], st_[:, 24:25], st_[:, 27:28],
                                                              -1.0, ALU.mult, ALU.mult),
                 wait=(("seA", k.now("seA")),), inc=(("veA", 1),))
            M["nmr%d" % par] = k.now("veA")

        def build_corr(st, tag, wTb, nfeat, bias_off, wld_val):
            cs = SB(st, f"cs_{tag}", (128, nfeat), F32)
            bb = SB(st, f"bb_{tag}", (128, nfeat), F32)
            csrow = SB(st, f"csrow_{tag}", (1, nfeat), F32)
            with contextlib.ExitStack() as st0:
                psC = PS(st0, f"psC_{tag}", (128, 512))
                psB2 = PS(st0, f"psB2_{tag}", (128, 512))
                for n in range(nfeat // 512):
                    for c in range(NDC):
                        k.op("tensor", lambda e, n=n, c=c: e.matmul(
                            psC[0:1, :], onesc_b[:, 0:1],
                            wTb[:, c * nfeat + n * 512:c * nfeat + (n + 1) * 512],
                            start=(c == 0), stop=(c == NDC - 1)),
                            wait=((("wld", wld_val), ("veA", M.get("csf_" + tag, 0)))
                                  if c == 0 else ()),
                            inc=((("peA", 1),) if c == NDC - 1 else ()))
                    k.op("vector", lambda e, n=n: e.tensor_copy(
                        csrow[0:1, n * 512:(n + 1) * 512], psC[0:1, :]),
                        wait=(("peA", k.now("peA")),), inc=(("veA", 1),))
                    M["csf_" + tag] = k.now("veA")
                    k.op("tensor", lambda e, n=n: e.matmul(
                        psB2[:, :], onesr_f[0:1, :], csrow[0:1, n * 512:(n + 1) * 512],
                        start=True, stop=True),
                        wait=(("veA", k.now("veA")),),
                        inc=(("peA", 1),))
                    k.op("vector", lambda e, n=n: e.tensor_copy(
                        cs[:, n * 512:(n + 1) * 512], psB2[:, :]),
                        wait=(("peA", k.now("peA")),), inc=(("veA", 1),))
                    k.op("tensor", lambda e, n=n: e.matmul(
                        psB2[:, :], onesr_b[0:1, :],
                        biasr[0:1, bias_off + n * 512:bias_off + (n + 1) * 512],
                        start=True, stop=True),
                        wait=(("veA", k.now("veA")), ("bld", BLD)), inc=(("peA", 1),))
                    k.op("vector", lambda e, n=n: e.tensor_copy(
                        bb[:, n * 512:(n + 1) * 512], psB2[:, :]),
                        wait=(("peA", k.now("peA")),), inc=(("veA", 1),))
                    M["bbf_" + tag] = k.now("veA")
            return cs, bb

        with contextlib.ExitStack() as stQ:
            qsin = SB(stQ, "qsin", (128, TPC * 64 * KVH), BF16)
            qcos = SB(stQ, "qcos", (128, TPC * 64 * KVH), BF16)
            with contextlib.ExitStack() as stT:
                load_trig(stT, "q", TPC, qsf, qcf, qsin, qcos, KVH)
            wqTb = SB(stQ, "wqTb", (128, NDC * D), BF16)
            k.dma("gpsimd", wqTb.rearrange("p (c f) -> p c f", c=NDC)[:, :, :],
                  wqT.rearrange("(c p) f -> p c f", p=128), "wld",
                  wait=(("bld", BLD),) + k.fence())
            WQ_LD = k.now("wld")
            cs_q, bb_q = build_corr(stQ, "q", wqTb, D, BQ0, WQ_LD)
            M["qpsfree"] = k.now("veA")
            psQ = PS(stQ, "psQ", (128, D))
            qcbuf = SB(stQ, "qcbuf", (128, D), F32)
            q_bf = SB(stQ, "q_bf", (128, D), BF16)
            q_ro = SB(stQ, "q_ro", (128, D), BF16)
            qspill = SB(stQ, "qspill", (128, D), BF16)
            qsc4 = Qsc.rearrange("p (h j q) -> p h j q", j=TPC, q=128)

            def q_front(j, what):
                par = j % 2
                if what == "dma":
                    front_dma(par, xo[j * 128:(j + 1) * 128, :],
                              xoTt[j:j + 1, :, :].rearrange("o (c p) q -> p (o c) q", p=128),
                              "qx%d" % par, "qt%d" % par)
                else:
                    front_cast(par, "qt%d" % par)
                    front_stats(par, "qx%d" % par)

            def q_back(j):
                par = j % 2
                stats_back(j)
                st_ = stat2[par]
                for n in range(4):
                    for c in range(NDC):
                        w = ()
                        if n == 0 and c == 0:
                            w = (("seA", M["cast%d" % par]), ("wld", WQ_LD),
                                 ("veA", M.get("qpsfree", 0)))
                        k.op("tensor", lambda e, n=n, c=c, par=par: e.matmul(
                            psQ[:, n * 512:(n + 1) * 512], xT_bf[par][:, c * 128:(c + 1) * 128],
                            wqTb[:, c * D + n * 512:c * D + (n + 1) * 512],
                            start=(c == 0), stop=(c == NDC - 1)), wait=w,
                            inc=(("peA", 1),) if (n == 3 and c == NDC - 1) else ())
                M["xTfree%d" % par] = k.now("peA")
                peq = k.now("peA")
                k.op("vector", lambda e, st_=st_: e.scalar_tensor_tensor(
                    qcbuf[:], psQ[:], st_[:, 27:28], bb_q[:], ALU.mult, ALU.add),
                     wait=(("peA", peq), ("veA", M["nmr%d" % par])))
                k.op("vector", lambda e, st_=st_: e.scalar_tensor_tensor(
                    qcbuf[:], cs_q[:], st_[:, 28:29], qcbuf[:], ALU.mult, ALU.add),
                     inc=(("veA", 1),))
                M["qpsfree"] = k.now("veA")
                k.op("scalar", lambda e: e.activation(sqsc[:], qcbuf[:], AF.Square),
                     wait=(("veA", k.now("veA")),), inc=(("seA", 1),))
                sq3 = sqsc.rearrange("p (h d) -> p h d", d=HD)
                k.op("vector", lambda e: e.tensor_reduce(rmsd[:, 0:NH], sq3[:, :, :], AX.X, ALU.add),
                     wait=(("seA", k.now("seA")),))
                k.op("scalar", lambda e: e.activation(rmsd[:, 0:NH], rmsd[:, 0:NH], AF.Ln,
                                                      bias=eps_rms[:, 0:1], scale=1.0 / HD),
                     wait=(("veA", k.now("veA")),))
                k.op("scalar", lambda e: e.activation(rmsd[:, 0:NH], rmsd[:, 0:NH], AF.Exp, scale=-0.5),
                     inc=(("seA", 1),))
                k.op("vector", lambda e: e.tensor_mul(rmsd[:, 0:NH], rmsd[:, 0:NH], gs_bc[:, 0:NH]),
                     wait=(("seA", k.now("seA")),))
                for h in range(NH):
                    k.op("vector", lambda e, h=h: e.tensor_scalar(
                        q_bf[:, h * HD:(h + 1) * HD], qcbuf[:, h * HD:(h + 1) * HD],
                        rmsd[:, h:h + 1], None, ALU.mult),
                        inc=(("veA", 1),) if h == NH - 1 else ())
                vro = rope(q_bf, q_ro, NH, qsin[:, j * 256:(j + 1) * 256], qcos[:, j * 256:(j + 1) * 256],
                           "veA", (), rep=KVH)
                for h in range(NH):
                    k.op("tensor", lambda e, h=h, par=par: e.transpose(
                        xnfP2[par][:, h * 128:(h + 1) * 128], q_ro[:, h * 128:(h + 1) * 128], identb[:]),
                        wait=(("veA", vro), ("veA", M.get("qspv%d" % par, 0))) if h == 0 else (),
                        inc=(("peA", 1),) if h == NH - 1 else ())
                k.op("vector", lambda e, par=par: e.tensor_copy(qspill[:], xnfP2[par][:]),
                     wait=(("peA", k.now("peA")), ("qw", k.now("qw"))), inc=(("veA", 1),))
                M["qspv%d" % par] = k.now("veA")
                qsp3 = qspill.rearrange("p (h q) -> p h q", q=128)
                k.dma("sync", qsc4[:, :, j, :], qsp3[:, :, :], "qw", wait=(("veA", k.now("veA")),))

            q_front(0, "dma")
            q_front(0, "rest")
            q_front(1, "dma")
            q_front(1, "rest")
            for j in range(TPC):
                if j + 2 < TPC:
                    q_front(j + 2, "dma")
                q_back(j)
                if j + 2 < TPC:
                    q_front(j + 2, "rest")
        QS_DONE = (("qw", k.now("qw")),)

        with contextlib.ExitStack() as stK:
            ksin = SB(stK, "ksin", (128, NT * 64 * KVH), BF16)
            kcos = SB(stK, "kcos", (128, NT * 64 * KVH), BF16)
            with contextlib.ExitStack() as stT:
                load_trig(stT, "k", NT, ksf, kcf, ksin, kcos, KVH)
            wkTb = SB(stK, "wkTb", (128, NDC * KVD), BF16)
            wvTb = SB(stK, "wvTb", (128, NDC * KVD), BF16)
            k.dma("gpsimd", wkTb.rearrange("p (c f) -> p c f", c=NDC)[:, :, :],
                  wkT.rearrange("(c p) f -> p c f", p=128), "wld", wait=k.fence())
            k.dma("gpsimd", wvTb.rearrange("p (c f) -> p c f", c=NDC)[:, :, :],
                  wvT.rearrange("(c p) f -> p c f", p=128), "wld")
            WKV_LD = k.now("wld")
            cs_k, bb_k = build_corr(stK, "k", wkTb, KVD, BK0, WKV_LD)
            cs_v, bb_v = build_corr(stK, "v", wvTb, KVD, BV0, WKV_LD)
            for _s in ("kpsfree0", "kpsfree1", "vpsfree0", "vpsfree1"):
                M[_s] = k.now("veA")
            psK = [PS(stK, f"psK{i}", (128, KVD)) for i in range(2)]
            psV = [PS(stK, f"psV{i}", (128, KVD)) for i in range(2)]
            kcbuf = SB(stK, "kcbuf", (128, KVD), F32)
            k_bf = [SB(stK, f"k_bf{i}", (128, KVD), BF16) for i in range(2)]
            k_ro = [SB(stK, f"k_ro{i}", (128, KVD), BF16) for i in range(2)]
            kT_st = [SB(stK, f"kT_st{i}", (128, KVD), BF16) for i in range(2)]
            v_st = [SB(stK, f"v_st{i}", (128, KVD), BF16) for i in range(2)]
            sqk = SB(stK, "sqk", (128, KVD), F32)
            vcbuf = sqk

            def kv_front(tt, what):
                par = tt % 2
                if what == "dma":
                    front_dma(par, xf[tt * 128:(tt + 1) * 128, :],
                              xfTt[tt:tt + 1, :, :].rearrange("o (c p) q -> p (o c) q", p=128),
                              "kx%d" % par, "kt%d" % par)
                else:
                    front_cast(par, "kt%d" % par)
                    front_stats(par, "kx%d" % par)

            def kv_s2(tt):
                stats_back(tt)

            def kv_s34K(tt):
                par = tt % 2
                for c in range(NDC):
                    w = ()
                    if c == 0:
                        w = (("seA", M["cast%d" % par]), ("wld", WKV_LD),
                             ("veA", M.get("kpsfree%d" % par, 0)))
                    k.op("tensor", lambda e, c=c, par=par: e.matmul(
                        psK[par][:], xT_bf[par][:, c * 128:(c + 1) * 128], wkTb[:, c * KVD:(c + 1) * KVD],
                        start=(c == 0), stop=(c == NDC - 1)), wait=w,
                        inc=(("peA", 1),) if c == NDC - 1 else ())
                M["pek%d" % par] = k.now("peA")

            def kv_s4V(tt, ktrans_of=None):
                par = tt % 2
                for c in range(NDC):
                    w = (("veA", M.get("vpsfree%d" % par, 0)),) if c == 0 else ()
                    k.op("tensor", lambda e, c=c, par=par: e.matmul(
                        psV[par][:], xT_bf[par][:, c * 128:(c + 1) * 128], wvTb[:, c * KVD:(c + 1) * KVD],
                        start=(c == 0), stop=(c == NDC - 1)), wait=w,
                        inc=(("peA", 1),) if c == NDC - 1 else ())
                    if ktrans_of is not None and 4 <= c < 8:
                        g = c - 4
                        kpar = ktrans_of % 2
                        wt = ()
                        if g == 0:
                            wt = (("poA", M["ropedone%d" % kpar]),
                                  ("seA", M.get("ktcs%d" % kpar, 0)),
                                  ("veA", M.get("qspv%d" % kpar, 0)))
                        k.op("tensor", lambda e, g=g, kpar=kpar: e.transpose(
                            xnfP2[kpar][:, g * 128:(g + 1) * 128], k_ro[kpar][:, g * 128:(g + 1) * 128], identb[:]),
                            wait=wt, inc=(("peA", 1),) if g == KVH - 1 else ())
                        if g == KVH - 1:
                            M["ktrans%d" % kpar] = k.now("peA")
                M["pev%d" % par] = k.now("peA")
                M["xTfree%d" % par] = k.now("peA")

            def kv_s5a(tt):
                par = tt % 2
                st_ = stat2[par]
                k.op("vector", lambda e, par=par, st_=st_: e.scalar_tensor_tensor(
                    kcbuf[:], psK[par][:], st_[:, 27:28], bb_k[:], ALU.mult, ALU.add),
                     wait=(("peA", M["pek%d" % par]), ("veA", M["nmr%d" % par]),
                           ("seA", M.get("sqdone", 0))))
                k.op("vector", lambda e, st_=st_: e.scalar_tensor_tensor(
                    kcbuf[:], cs_k[:], st_[:, 28:29], kcbuf[:], ALU.mult, ALU.add),
                     inc=(("veA", 1),))
                M["kpsfree%d" % par] = k.now("veA")
                k.op("scalar", lambda e: e.activation(sqk[:], kcbuf[:], AF.Square),
                     wait=(("veA", k.now("veA")),), inc=(("seA", 1),))
                M["sqdone"] = k.now("seA")
                sqk3 = sqk.rearrange("p (h d) -> p h d", d=HD)
                k.op("vector", lambda e: e.tensor_reduce(rmsd[:, 0:KVH], sqk3[:, :, :], AX.X, ALU.add),
                     wait=(("seA", k.now("seA")),))
                k.op("scalar", lambda e: e.activation(rmsd[:, 0:KVH], rmsd[:, 0:KVH], AF.Ln,
                                                      bias=eps_rms[:, 0:1], scale=1.0 / HD),
                     wait=(("veA", k.now("veA")),))
                k.op("scalar", lambda e: e.activation(rmsd[:, 0:KVH], rmsd[:, 0:KVH], AF.Exp, scale=-0.5),
                     inc=(("seA", 1),))
                sk = k.now("seA")
                for g in range(KVH):
                    k.op("vector", lambda e, g=g, par=par: e.tensor_scalar(
                        k_bf[par][:, g * HD:(g + 1) * HD], kcbuf[:, g * HD:(g + 1) * HD],
                        rmsd[:, g:g + 1], None, ALU.mult),
                        wait=(("seA", sk), ("poA", M.get("ropedone%d" % par, 0))) if g == 0 else (),
                        inc=(("veA", 1),) if g == KVH - 1 else ())
                vro = rope(k_bf[par], k_ro[par], KVH, ksin[:, tt * 256:(tt + 1) * 256], kcos[:, tt * 256:(tt + 1) * 256],
                           "poA", (("veA", k.now("veA")), ("pe", M.get("ktrans%d" % par, 0))), eng="gpsimd")
                M["ropedone%d" % par] = vro
                k.op("vector", lambda e, par=par, st_=st_: e.scalar_tensor_tensor(
                    vcbuf[:], psV[par][:], st_[:, 27:28], bb_v[:], ALU.mult, ALU.add),
                     wait=(("peA", M["pev%d" % par]),))
                k.op("vector", lambda e, par=par, st_=st_: e.scalar_tensor_tensor(
                    v_st[par][:], cs_v[:], st_[:, 28:29], vcbuf[:], ALU.mult, ALU.add),
                     wait=(("vsp%d" % par, k.now("vsp%d" % par)),), inc=(("veA", 1),))
                M["vpsfree%d" % par] = k.now("veA")
                vst3 = v_st[par].rearrange("p (g h) -> p g h", h=HD)
                vd_t = Vd[:, tt * 128:(tt + 1) * 128, :].rearrange("g p h -> p g h")
                k.dma("sync", vd_t, vst3[:, :, :], "vsp%d" % par, wait=(("veA", k.now("veA")),))

            def kv_s5b(tt):
                par = tt % 2
                k.op("scalar", lambda e, par=par: e.activation(kT_st[par][:], xnfP2[par][:, 0:KVD], AF.Identity),
                     wait=(("peA", M["ktrans%d" % par]), ("ksp%d" % par, k.now("ksp%d" % par))), inc=(("seA", 1),))
                M["ktcs%d" % par] = k.now("seA")
                kst3 = kT_st[par].rearrange("p (g q) -> p g q", q=128)
                kd_t = Kd[:, :, tt * 128:(tt + 1) * 128].rearrange("g p q -> p g q")
                k.dma("scalar", kd_t, kst3[:, :, :], "ksp%d" % par, wait=(("seA", M["ktcs%d" % par]),))

            kv_front(0, "dma")
            kv_front(0, "rest")
            kv_front(1, "dma")
            kv_front(1, "rest")
            kv_s2(0)
            for tt in range(NT):
                if tt + 2 < NT:
                    kv_front(tt + 2, "dma")
                if tt + 1 < NT:
                    kv_s2(tt + 1)
                kv_s34K(tt)
                kv_s4V(tt, ktrans_of=(tt - 1) if tt >= 1 else None)
                kv_s5a(tt)
                if tt >= 1:
                    kv_s5b(tt - 1)
                if tt % 4 == 3:
                    sched_s_load()
                    if nld[0] >= 2:
                        sched_s_reduce("scalar")
                if tt + 2 < NT:
                    kv_front(tt + 2, "rest")
            kparF = (NT - 1) % 2
            for g in range(KVH):
                wt = ()
                if g == 0:
                    wt = (("poA", M["ropedone%d" % kparF]),
                          ("seA", M.get("ktcs%d" % kparF, 0)),
                          ("veA", M.get("qspv%d" % kparF, 0)))
                k.op("tensor", lambda e, g=g, kparF=kparF: e.transpose(
                    xnfP2[kparF][:, g * 128:(g + 1) * 128], k_ro[kparF][:, g * 128:(g + 1) * 128], identb[:]),
                    wait=wt, inc=(("peA", 1),) if g == KVH - 1 else ())
            M["ktrans%d" % kparF] = k.now("peA")
            kv_s5b(NT - 1)
    KVSP = (("ksp0", k.now("ksp0")), ("ksp1", k.now("ksp1")),
            ("vsp0", k.now("vsp0")), ("vsp1", k.now("vsp1"))) + QS_DONE

    xn2_fm = SB(es, "xn2_fm", (128, NDC * TPC * 128), BF16)
    xn2f4 = xn2_fm.rearrange("p (c j q) -> p c j q", j=TPC, q=128)
    with contextlib.ExitStack() as stF:
        KBLK = [SB(stF, f"KBLK{i}", (128, 16 * 512), BF16) for i in range(2)]
        VBLK = [SB(stF, f"VBLK{i}", (128, 16 * 512), BF16) for i in range(2)]
        MBUF = SB(stF, "MBUF", (128, 4 * 1024), BF16)
        PT = [SB(stF, f"PT{i}", (128, 1024), BF16) for i in range(2)]
        Qst = [SB(stF, f"Qst{i}", (128, 256), BF16) for i in range(2)]
        aost = SB(stF, "aost", (128, 256), BF16)
        sum_g = SB(stF, "sum_g", (1, 256), F32)
        rec_b = SB(stF, "rec_b", (128, 256), F32)
        ST = [PS(stF, f"ST{i}", (128, 1024)) for i in range(2)]
        PV = [PS(stF, f"PV{i}", (128, 512)) for i in range(2)]
        SMPB = [PS(stF, f"SMPB{i}", (128, 512)) for i in range(2)]
        gstep = [0]
        sched_s_load()

        for a in range(4):
            K_a = 4 * a + 4
            gw = {}

            def load_g(a_, g_):
                gpar_ = (a_ * KVH + g_) % 2
                K_a_ = 4 * a_ + 4
                w = (("peF", M.get("gfree%d" % gpar_, 0)),)
                if a_ == 0 and g_ < 2:
                    w = w + KVSP + k.fence()
                k.dma("sync", KBLK[gpar_][:, 0:K_a_ * 512], Kd[g_, :, 0:K_a_ * 512], "kbl%d" % gpar_, wait=w)
                vsrc = Vd[g_, 0:K_a_ * 512, :].rearrange("(n p) h -> p n h", p=128)
                k.dma("sync", VBLK[gpar_].rearrange("p (n h) -> p n h", h=HD)[:, 0:K_a_ * 4, :], vsrc,
                      "kbl%d" % gpar_)
                gw[g_] = k.now("kbl%d" % gpar_)

            load_g(a, 0)
            k.dma("sync", MBUF.rearrange("p (n f) -> p n f", n=4)[:, :, :],
                  binm[a * 4:(a + 1) * 4, :, :].rearrange("n p f -> p n f"), "mld",
                  wait=(("veF", M.get("mfree", 0)),))
            MLD = k.now("mld")
            def qload(a_, h_):
                qp = h_ % 2
                k.dma("sync", Qst[qp][:], Qsc[:, h_ * 1024 + a_ * 256:h_ * 1024 + (a_ + 1) * 256],
                      "qld%d" % qp, wait=(("peF", M.get("quse%d" % qp, 0)),))
                M["qldc%d" % qp] = k.now("qld%d" % qp)

            qload(a, 0)
            for g in range(KVH):
                gpar = (a * KVH + g) % 2
                if g + 1 < KVH:
                    load_g(a, g + 1)
                for u in range(4):
                    h = g * 4 + u
                    hpar = h % 2
                    qpar = h % 2
                    sched_s_reduce("vector")
                    sched_s_reduce("vector")
                    sched_s_load()
                    sched_s_load()
                    if h + 1 < NH:
                        qload(a, h + 1)
                    QLD = M["qldc%d" % qpar]
                    base = gstep[0]

                    def fqk(i, kb, gpar=gpar, qpar=qpar, gw_=None, QLD=QLD, a=a):
                        par = i % 2
                        trim = (kb - 4 * a) >= 2
                        qo, qw_ = (128, 128) if trim else (0, 256)
                        w = (("seF", M.get("stfree%d" % par, 0)),
                             ("kbl%d" % gpar, gw_), ("qld%d" % qpar, QLD))
                        for kc in range(4):
                            k.op("tensor", lambda e, par=par, kc=kc, kb=kb, gpar=gpar, qpar=qpar,
                                 qo=qo, qw_=qw_: e.matmul(
                                ST[par][:, kc * 256 + qo:kc * 256 + qo + qw_],
                                KBLK[gpar][:, kb * 512 + kc * 128:kb * 512 + (kc + 1) * 128],
                                Qst[qpar][:, qo:qo + qw_], start=True, stop=True),
                                wait=w if kc == 0 else (),
                                inc=(("peF", 1),) if kc == 3 else ())
                        M["qk%d" % par] = k.now("peF")

                    def fexp(i, kb, h=h, a=a):
                        par = i % 2
                        ki = kb - 4 * a
                        trim = ki >= 2
                        wq = (("peF", M["qk%d" % par]), ("peF", M.get("ptfree%d" % par, 0)),
                              ("seF", k.now("seF")))
                        if trim:
                            pt3 = PT[par].rearrange("p (c q) -> p c q", q=256)
                            st3 = ST[par].rearrange("p (c q) -> p c q", q=256)
                            k.op("scalar", lambda e, par=par, h=h, pt3=pt3, st3=st3: e.activation(
                                pt3[:, :, 128:256], st3[:, :, 128:256], AF.Exp, bias=negm_bc[:, h:h + 1]),
                                wait=wq, inc=(("seF", 1),))
                            M["stfree%d" % par] = k.now("seF")
                            mb3 = MBUF.rearrange("p (n c q) -> p n c q", n=4, q=256)
                            k.op("vector", lambda e, par=par, ki=ki, pt3=pt3, mb3=mb3: e.tensor_tensor(
                                pt3[:, :, 128:256], pt3[:, :, 128:256], mb3[:, ki, :, 128:256], ALU.mult),
                                wait=(("seF", k.now("seF")), ("mld", MLD), ("veF", k.now("veF"))),
                                inc=(("veF", 1),))
                            M["ptdepA%d" % par] = ("veF", k.now("veF"))
                            M["ptdepB%d" % par] = ("veF", k.now("veF"))
                            return
                        k.op("scalar", lambda e, par=par, h=h: e.activation(
                            PT[par][:, 0:512], ST[par][:, 0:512], AF.Exp, bias=negm_bc[:, h:h + 1]),
                            wait=wq, inc=(("seF", 1),))
                        sA = k.now("seF")
                        k.op("scalar", lambda e, par=par, h=h: e.activation(
                            PT[par][:, 512:1024], ST[par][:, 512:1024], AF.Exp, bias=negm_bc[:, h:h + 1]),
                            inc=(("seF", 1),))
                        M["stfree%d" % par] = k.now("seF")
                        if ki >= 0:
                            k.op("vector", lambda e, par=par, ki=ki: e.tensor_mul(
                                PT[par][:, 0:512], PT[par][:, 0:512], MBUF[:, ki * 1024:ki * 1024 + 512]),
                                wait=(("seF", sA), ("mld", MLD), ("veF", k.now("veF"))),
                                inc=(("veF", 1),))
                            M["ptdepA%d" % par] = ("veF", k.now("veF"))
                            k.op("vector", lambda e, par=par, ki=ki: e.tensor_mul(
                                PT[par][:, 512:1024], PT[par][:, 512:1024], MBUF[:, ki * 1024 + 512:(ki + 1) * 1024]),
                                wait=(("seF", k.now("seF")),),
                                inc=(("veF", 1),))
                            M["ptdepB%d" % par] = ("veF", k.now("veF"))
                        else:
                            M["ptdepA%d" % par] = ("seF", sA)
                            M["ptdepB%d" % par] = ("seF", k.now("seF"))

                    def fpv(i, kb, K_a=K_a, gpar=gpar, hpar=hpar, a=a):
                        par = i % 2
                        first, last = (kb == 0), (kb == K_a - 1)
                        ki = kb - 4 * a
                        trim = ki >= 2
                        qo, qw_ = (128, 128) if trim else (0, 256)
                        wA = [M["ptdepA%d" % par]]
                        wB = [M["ptdepB%d" % par]]
                        if first:
                            wA.append(("veF", M.get("finfree%d" % hpar, 0)))
                        for kc in range(4):
                            w = (tuple(wA) if kc == 0 else ()) + (tuple(wB) if kc == 2 else ())
                            k.op("tensor", lambda e, par=par, kc=kc, kb=kb, gpar=gpar, hpar=hpar,
                                 first=first, last=last, qo=qo, qw_=qw_: e.matmul(
                                PV[hpar][:, qo:qo + qw_],
                                VBLK[gpar][:, (kb * 4 + kc) * 128:(kb * 4 + kc + 1) * 128],
                                PT[par][:, kc * 256 + qo:kc * 256 + qo + qw_],
                                start=(first and kc == 0), stop=(last and kc == 3)),
                                wait=w)
                        for kc in range(4):
                            k.op("tensor", lambda e, par=par, kc=kc, hpar=hpar, first=first, last=last,
                                 qo=qo, qw_=qw_: e.matmul(
                                SMPB[hpar][0:1, qo:qo + qw_], onesc_b[:, 0:1],
                                PT[par][:, kc * 256 + qo:kc * 256 + qo + qw_],
                                start=(first and kc == 0), stop=(last and kc == 3)),
                                inc=(("peF", 1),) if kc == 3 else ())
                        M["ptfree%d" % par] = k.now("peF")

                    fqk(base + 0, 0, gw_=gw[g])
                    fexp(base + 0, 0)
                    for kb in range(1, K_a):
                        fqk(base + kb, kb, gw_=gw[g])
                        fpv(base + kb - 1, kb - 1)
                        if kb == 2 and M.get("_fin") is not None:
                            M.pop("_fin")()
                        fexp(base + kb, kb)
                    fpv(base + K_a - 1, K_a - 1)
                    gstep[0] = base + K_a
                    M["quse%d" % qpar] = k.now("peF")

                    def finalize(a=a, h=h, hpar=hpar, qpar=qpar):
                        pw = k.now("peF")
                        k.op("vector", lambda e, hpar=hpar: e.tensor_scalar(
                            sum_g[:], SMPB[hpar][0:1, 0:256], guard_c[0:1, 0:1], None, ALU.max),
                            wait=(("peF", pw), ("veF", k.now("veF"))), inc=(("veF", 1),))
                        k.op("scalar", lambda e: e.activation(sum_g[:], sum_g[:], AF.Ln),
                             wait=(("veF", k.now("veF")), ("seF", k.now("seF"))), inc=(("seF", 1),))
                        k.op("scalar", lambda e: e.activation(sum_g[:], sum_g[:], AF.Exp, scale=-1.0),
                             wait=(("seF", k.now("seF")),), inc=(("seF", 1),))
                        k.op("tensor", lambda e, hpar=hpar: e.matmul(
                            SMPB[hpar][:, 256:512], onesr_f[0:1, :], sum_g[0:1, 0:256], start=True, stop=True),
                            wait=(("seF", k.now("seF")),), inc=(("peF", 1),))
                        k.op("vector", lambda e, hpar=hpar: e.tensor_copy(rec_b[:], SMPB[hpar][:, 256:512]),
                             wait=(("peF", k.now("peF")), ("veF", k.now("veF"))), inc=(("veF", 1),))
                        k.op("vector", lambda e, hpar=hpar: e.tensor_tensor(
                            aost[:], PV[hpar][:, 0:256], rec_b[:], ALU.mult),
                            wait=(("veF", k.now("veF")), ("aow", k.now("aow"))), inc=(("veF", 1),))
                        M["finfree%d" % hpar] = k.now("veF")
                        M["qfree%d" % qpar] = k.now("peF")
                        k.dma("sync", AOsc[:, h * 1024 + a * 256:h * 1024 + (a + 1) * 256], aost[:],
                              "aow", wait=(("veF", k.now("veF")),))

                    M["_fin"] = finalize
                M["gfree%d" % gpar] = k.now("peF")
            M["mfree"] = k.now("veF")
        if M.get("_fin") is not None:
            M.pop("_fin")()
        assert nld[0] == len(s_chunks), (nld[0], len(s_chunks))
        while nrd[0] < nld[0]:
            sched_s_reduce("vector")
    AO_DONE = k.now("aow")

    with contextlib.ExitStack() as stB:
        woTb = SB(stB, "woTb", (128, NDC * D), BF16)
        k.dma("gpsimd", woTb.rearrange("p (c f) -> p c f", c=NDC)[:, 0:8, :], woT[0:1024, :].rearrange("(c p) f -> p c f", p=128), "wld",
              wait=k.fence())
        k.dma("gpsimd", woTb.rearrange("p (c f) -> p c f", c=NDC)[:, 8:16, :], woT[1024:2048, :].rearrange("(c p) f -> p c f", p=128), "wld")
        WO_LD = k.now("wld")
        psO = PS(stB, "psO", (128, D))
        psT2 = PS(stB, "psT2", (128, D), BF16)
        aostg = [SB(stB, f"aostg{i}", (128, NH * 128), BF16) for i in range(2)]
        x2t = SB(stB, "x2t", (128, D), F32)
        radj = SB(stB, "radj", (128, D), F32)
        xn2b = SB(stB, "xn2b", (128, D), BF16)
        ao4 = AOsc.rearrange("p (h j q) -> p h j q", j=TPC, q=128)
        ybuf = SB(stB, "ybuf", (128, D), F32)

        def b2_dma(j):
            par = j % 2
            k.dma("sync", aostg[par][:], ao4[:, :, j, :], "aold%d" % par,
                  wait=(("aow", AO_DONE), ("peO", M.get("aofree%d" % par, 0)))
                  + (k.fence() if j < 2 else ()))
            M["ald%d" % par] = k.now("aold%d" % par)
            k.dma("sync", xstg[par][:], xo[j * 128:(j + 1) * 128, :], "bx%d" % par,
                  wait=(("veA", M.get("xfree%d" % par, 0)), ("veO", M.get("bxfree%d" % par, 0))))

        def b2_mm(j):
            par = j % 2
            do_tr = j >= 1
            for c in range(NDC):
                for n in range(4):
                    w = ()
                    if c == 0 and n == 0:
                        w = (("wld", WO_LD), ("veO", M.get("psOfree", 0)),
                             ("aold%d" % par, M["ald%d" % par]), ("seO", M.get("psOfree_s", 0)))
                    k.op("tensor", lambda e, c=c, n=n, par=par: e.matmul(
                        psO[:, n * 512:(n + 1) * 512],
                        aostg[par][:, c * 128:(c + 1) * 128],
                        woTb[:, c * D + n * 512:c * D + (n + 1) * 512],
                        start=(c == 0), stop=False), wait=w)
                if do_tr and 8 <= c < 16:
                    for tsub in range(2):
                        tc = (c - 8) * 2 + tsub
                        wt = ()
                        if tc == 0:
                            wt = (("veO", M["xn2v"]), ("veO", M.get("pst2free", 0)))
                        k.op("tensor", lambda e, tc=tc: e.transpose(
                            psT2[:, tc * 128:(tc + 1) * 128], xn2b[:, tc * 128:(tc + 1) * 128], identb[:]),
                            wait=wt, inc=(("peO", 1),) if tc == NDC - 1 else ())
            if do_tr:
                M["trdone"] = k.now("peO")
            for n in range(4):
                k.op("tensor", lambda e, n=n: e.matmul(
                    psO[:, n * 512:(n + 1) * 512], onesr_b[0:1, :],
                    biasr[0:1, BO0 + n * 512:BO0 + (n + 1) * 512],
                    start=False, stop=True),
                    inc=(("peO", 1),) if n == 3 else ())
            M["aofree%d" % par] = k.now("peO")
            if do_tr:
                pst3 = psT2.rearrange("p (c q) -> p c q", q=128)
                k.op("vector", lambda e, j=j: e.tensor_copy(xn2f4[:, :, j - 1, :], pst3[:, :, :]),
                     wait=(("peO", M["trdone"]), ("veO", k.now("veO"))), inc=(("veO", 1),))
                M["pst2free"] = k.now("veO")
            k.op("scalar", lambda e: e.activation(ybuf[:], psO[:], AF.Identity),
                 wait=(("peO", M["aofree%d" % par]), ("veO", k.now("veO")), ("seO", k.now("seO"))),
                 inc=(("seO", 1),))
            M["psOfree_s"] = k.now("seO")

        def b2_post(j):
            par = j % 2
            sev = M["psOfree_s"]
            k.op("vector", lambda e: e.tensor_reduce(stat[:, 0:1], ybuf[:], AX.X, ALU.add),
                 wait=(("seO", sev), ("veA", k.now("veA")), ("seA", k.now("seA"))),
                 inc=(("veO", 1),))
            k.op("scalar", lambda e: e.activation(sqsc[:], ybuf[:], AF.Square, accum_out=stat[:, 1:2]),
                 wait=(("seO", k.now("seO")), ("seA", k.now("seA"))),
                 inc=(("seO", 1),))
            vw, sw = k.now("veO"), k.now("seO")
            k.op("vector", lambda e: e.tensor_scalar(stat[:, 2:3], stat[:, 0:1], 1.0 / D, None, ALU.mult),
                 wait=(("veO", vw), ("seO", sw)))
            k.op("vector", lambda e: e.tensor_scalar(stat[:, 3:4], stat[:, 1:2], 1.0 / D, None, ALU.mult),
                 inc=(("veO", 1),))
            k.op("vector", lambda e: e.tensor_mul(stat[:, 4:5], stat[:, 2:3], stat[:, 2:3]),
                 wait=(("veO", k.now("veO")),))
            k.op("vector", lambda e: e.tensor_sub(stat[:, 4:5], stat[:, 3:4], stat[:, 4:5]),
                 inc=(("veO", 1),))
            k.op("scalar", lambda e: e.activation(stat[:, 5:6], stat[:, 4:5], AF.Ln, bias=eps_ln[:, 0:1]),
                 wait=(("veO", k.now("veO")), ("seO", k.now("seO"))), inc=(("seO", 1),))
            k.op("scalar", lambda e: e.activation(stat[:, 5:6], stat[:, 5:6], AF.Exp, scale=-0.5),
                 wait=(("seO", k.now("seO")),), inc=(("seO", 1),))
            k.op("vector", lambda e: e.scalar_tensor_tensor(stat[:, 6:7], stat[:, 2:3], -1.0, stat[:, 5:6],
                                                            ALU.mult, ALU.mult),
                 wait=(("seO", k.now("seO")),), inc=(("veO", 1),))
            k.op("vector", lambda e, par=par: e.tensor_scalar(radj[:], xstg[par][:], stat[:, 6:7], None, ALU.add),
                 wait=(("bx%d" % par, k.now("bx%d" % par)), ("veO", k.now("veO"))), inc=(("veO", 1),))
            M["bxfree%d" % par] = k.now("veO")
            k.op("vector", lambda e: e.scalar_tensor_tensor(x2t[:], ybuf[:], stat[:, 5:6], radj[:],
                                                            ALU.mult, ALU.add),
                 wait=(("x2w", k.now("x2w")), ("veO", k.now("veO"))), inc=(("veO", 1),))
            k.dma("sync", X2d[j * 128:(j + 1) * 128, :], x2t[:], "x2w", wait=(("veO", k.now("veO")),))
            vln2 = ln_tile(x2t[:], xn2b[:], "veO", "seO", extra_wait=(("veO", k.now("veO")),))
            M["xn2v"] = vln2

        b2_dma(0)
        b2_dma(1)
        for j in range(TPC):
            b2_mm(j)
            b2_post(j)
            if j + 2 < TPC:
                b2_dma(j + 2)
        for tc in range(NDC):
            wt = (("veO", M["xn2v"]), ("veO", M.get("pst2free", 0))) if tc == 0 else ()
            k.op("tensor", lambda e, tc=tc: e.transpose(
                psT2[:, tc * 128:(tc + 1) * 128], xn2b[:, tc * 128:(tc + 1) * 128], identb[:]),
                wait=wt, inc=(("peO", 1),) if tc == NDC - 1 else ())
        pst3 = psT2.rearrange("p (c q) -> p c q", q=128)
        k.op("vector", lambda e: e.tensor_copy(xn2f4[:, :, TPC - 1, :], pst3[:, :, :]),
             wait=(("peO", k.now("peO")), ("veO", k.now("veO"))), inc=(("veO", 1),))
    X2W = k.now("x2w")
    XN2_DONE = (("veO", k.now("veO")),)

    while nld[0] < len(s_chunks):
        sched_s_load()
        sched_s_reduce("vector")
    while nrd[0] < nld[0]:
        sched_s_reduce("vector")
    with contextlib.ExitStack() as st1:
        psS = PS(st1, "psS", (1, 4))
        psS2 = PS(st1, "psS2", (128, 4))
        s_row = SB(st1, "s_row", (1, 4), F32)
        k.op("tensor", lambda e: e.matmul(psS[0:1, 0:3], onesc_f[:, 0:1], sS[:, 0:3], start=True, stop=True),
             wait=(("poS", k.now("poS")),), inc=(("peC", 1),))
        k.op("vector", lambda e: e.tensor_scalar(s_row[0:1, 0:3], psS[0:1, 0:3], INW1, None, ALU.mult),
             wait=(("peC", k.now("peC")),))
        k.op("vector", lambda e: e.scalar_tensor_tensor(s_row[0:1, 0:3], psS[0:1, 0:3], INW2, s_row[0:1, 0:3],
                                                        ALU.mult, ALU.add),
             inc=(("veC", 1),))
        k.op("tensor", lambda e: e.matmul(psS2[:, 0:3], onesr_f[0:1, :], s_row[0:1, 0:3], start=True, stop=True),
             wait=(("veC", k.now("veC")),), inc=(("peC", 1),))
        k.op("vector", lambda e: e.tensor_copy(s_bc[:, 0:3], psS2[:, 0:3]),
             wait=(("peC", k.now("peC")),), inc=(("veC", 1),))
        k.op("vector", lambda e: e.tensor_scalar(th_p[:, 0:3], s_bc[:, 0:3], 0.5, None, ALU.mult),
             wait=(("veC", k.now("veC")),))
        k.op("vector", lambda e: e.tensor_scalar(th_n[:, 0:3], s_bc[:, 0:3], -0.5, None, ALU.mult),
             inc=(("veC", 1),))
    TH = k.now("veC")

    with contextlib.ExitStack() as stA:
        w13stg = [SB(stA, f"w13stg{i}", (128, 2 * D)) for i in range(2)]
        q13 = [SB(stA, f"q13_{i}", (128, 2 * D), BF16) for i in range(2)]
        posb = SB(stA, "posb", (128, D), BF16)
        negb = SB(stA, "negb", (128, D), BF16)
        silu_s = SB(stA, "silu_s", (128, 1024), F32)
        g_st = SB(stA, "g_st", (128, 1024), BF16)
        psH1 = [PS(stA, f"psH1{i}", (128, 1024)) for i in range(2)]
        psH3 = [PS(stA, f"psH3{i}", (128, 1024)) for i in range(2)]
        def a_front(hc):
            par = hc % 2
            hcp = 128 if hc < NHC - 1 else LAST_HC
            for half, (wTsrc, thi, qu, snm) in enumerate(((w1T, 0, "sync", "wa"),
                                                          (w3T, 2, "scalar", "wc"))):
                sem = "%s%d" % (snm, par)
                wc = wTsrc[:, hc * 128:hc * 128 + hcp].rearrange("(c p) h -> p c h", p=128)
                w13dst = w13stg[par][:, half * D:half * D + NDC * hcp].rearrange("p (c h) -> p c h", h=hcp)
                k.dma(qu, w13dst[:, :, :], wc, sem,
                      wait=(("veC", M.get("wafree%d_%d" % (par, half), 0)),) + (k.fence() if hc < 2 else ()))
                WL = k.now(sem)
                qw = ((sem, WL), ("veC", TH), ("poS", k.now("poS")))
                wq = NDC * hcp if hc < 2 else D
                k.op("vector", lambda e, par=par, wq=wq, thi=thi, half=half: e.tensor_scalar(
                    posb[:, 0:wq], w13stg[par][:, half * D:half * D + wq], th_p[:, thi:thi + 1], None, ALU.is_gt),
                    wait=qw)
                k.op("vector", lambda e, par=par, wq=wq, thi=thi, half=half: e.tensor_scalar(
                    negb[:, 0:wq], w13stg[par][:, half * D:half * D + wq], th_n[:, thi:thi + 1], None, ALU.is_lt),
                    inc=(("veC", 1),))
                M["wafree%d_%d" % (par, half)] = k.now("veC")
                k.op("vector", lambda e, par=par, wq=wq, half=half: e.tensor_sub(
                    q13[par][:, half * D:half * D + wq], posb[:, 0:wq], negb[:, 0:wq]),
                    wait=(("veC", k.now("veC")), ("poS", k.now("poS")),
                          ("seC", M.get("psOfree_s", 0)),
                          ("peC", M.get("q13free%d" % par, 0))))
            M["qdone%d" % par] = k.now("veC")

        def a_mm(hc):
            par = hc % 2
            hcp = 128 if hc < NHC - 1 else LAST_HC
            w0 = (("veC", M["qdone%d" % par]),) + XN2_DONE + (("seC", M.get("hafree%d" % par, 0)),
                                                             ("veC", M.get("gstt%d" % par, 0)))
            for c in range(NDC):
                for n in range(2):
                    k.op("tensor", lambda e, par=par, c=c, n=n, hcp=hcp: e.matmul(
                        psH1[par][0:hcp, n * 512:(n + 1) * 512],
                        q13[par][:, c * hcp:(c + 1) * hcp],
                        xn2_fm[:, c * 1024 + n * 512:c * 1024 + (n + 1) * 512],
                        start=(c == 0), stop=(c == NDC - 1)),
                        wait=w0 if (c == 0 and n == 0) else ())
            for c in range(NDC):
                for n in range(2):
                    k.op("tensor", lambda e, par=par, c=c, n=n, hcp=hcp: e.matmul(
                        psH3[par][0:hcp, n * 512:(n + 1) * 512],
                        q13[par][:, D + c * hcp:D + (c + 1) * hcp],
                        xn2_fm[:, c * 1024 + n * 512:c * 1024 + (n + 1) * 512],
                        start=(c == 0), stop=(c == NDC - 1)),
                        inc=(("peC", 1),) if (c == NDC - 1 and n == 1) else ())
            M["q13free%d" % par] = k.now("peC")
            M["amm%d" % par] = k.now("peC")

        def a_epi(hc):
            par = hc % 2
            hcp = 128 if hc < NHC - 1 else LAST_HC
            k.op("scalar", lambda e, par=par, hcp=hcp: e.activation(
                silu_s[0:hcp, :], psH1[par][0:hcp, :], AF.Silu, scale=s_bc[0:hcp, 0:1]),
                wait=(("peC", M["amm%d" % par]), ("seC", k.now("seC")), ("veC", M.get("gw", 0))),
                inc=(("seC", 1),))
            M["hafree%d" % par] = k.now("seC")
            k.op("vector", lambda e, par=par, hc=hc, hcp=hcp: e.scalar_tensor_tensor(
                g_st[0:hcp, :], psH3[par][0:hcp, :], s_bc[0:hcp, 2:3], silu_s[0:hcp, :], ALU.mult, ALU.mult),
                wait=(("seC", k.now("seC")), ("gws", k.now("gws")), ("veC", k.now("veC"))),
                inc=(("veC", 1),))
            M["gstt%d" % par] = k.now("veC")
            M["gw"] = k.now("veC")
            k.dma("sync", Gsc[hc * 128:hc * 128 + hcp, :], g_st[0:hcp, :], "gws",
                  wait=(("veC", k.now("veC")),))

        a_front(0)
        a_front(1)
        for hc in range(NHC):
            a_mm(hc)
            if hc + 2 < NHC:
                a_front(hc + 2)
            a_epi(hc)
        G_DONE = k.now("gws")

    with contextlib.ExitStack() as stBB:
        w2stg = [SB(stBB, f"w2stg{i}", (128, 512)) for i in range(2)]
        q2b = [SB(stBB, f"q2b{i}", (128, 512), BF16) for i in range(2)]
        pos2 = SB(stBB, "pos2", (128, 512), BF16)
        neg2 = SB(stBB, "neg2", (128, 512), BF16)
        gstg = [SB(stBB, f"gstg{i}", (128, 1024), BF16) for i in range(2)]
        y2ev = SB(stBB, "y2ev", (128, 512), F32)
        psY = [PS(stBB, f"psY{i}", (128, 512)) for i in range(TPC)]
        def b_front(dc, hc):
            par = hc % 2
            hcp = 128 if hc < NHC - 1 else LAST_HC
            k.dma("sync", w2stg[par][0:hcp, :], w2T[hc * 128:hc * 128 + hcp, dc * 512:(dc + 1) * 512],
                  "wb%d" % par, wait=(("veC", M.get("wbfree%d" % par, 0)),)
                  + (k.fence() if dc == 0 and hc < 2 else ()))
            k.dma("scalar", gstg[par][0:hcp, :], Gsc[hc * 128:hc * 128 + hcp, :],
                  "gl%d" % par, wait=(("gws", G_DONE), ("peC", M.get("glfree%d" % par, 0)))
                  + (k.fence() if dc == 0 and hc < 2 else ()))
            M["gld%d" % par] = k.now("gl%d" % par)
            qw = (("wb%d" % par, k.now("wb%d" % par)), ("veC", TH),
                  ("poS", k.now("poS")))
            k.op("vector", lambda e, par=par, hcp=hcp: e.tensor_scalar(
                pos2[0:hcp, :], w2stg[par][0:hcp, :], th_p[0:hcp, 1:2], None, ALU.is_gt), wait=qw)
            k.op("vector", lambda e, par=par, hcp=hcp: e.tensor_scalar(
                neg2[0:hcp, :], w2stg[par][0:hcp, :], th_n[0:hcp, 1:2], None, ALU.is_lt),
                inc=(("veC", 1),))
            M["wbfree%d" % par] = k.now("veC")
            k.op("vector", lambda e, par=par, hcp=hcp: e.tensor_sub(
                q2b[par][0:hcp, :], pos2[0:hcp, :], neg2[0:hcp, :]),
                wait=(("veC", k.now("veC")), ("poS", k.now("poS")),
                      ("peC", M.get("q2free%d" % par, 0))))
            M["q2done%d" % par] = k.now("veC")

        def b_mm(dc, hc):
            par = hc % 2
            hcp = 128 if hc < NHC - 1 else LAST_HC
            w0 = (("veC", M["q2done%d" % par]), ("gl%d" % par, M["gld%d" % par]))
            if hc == 0:
                w0 = w0 + (("veC", M.get("yev", 0)),)
            for j in range(TPC):
                k.op("tensor", lambda e, par=par, j=j, hc=hc, hcp=hcp: e.matmul(
                    psY[j][:], gstg[par][0:hcp, j * 128:(j + 1) * 128], q2b[par][0:hcp, :],
                    start=(hc == 0), stop=(hc == NHC - 1)),
                    wait=w0 if j == 0 else (),
                    inc=(("peC", 1),) if j == TPC - 1 else ())
            M["q2free%d" % par] = k.now("peC")
            M["glfree%d" % par] = k.now("peC")

        for dc in range(4):
            b_front(dc, 0)
            b_front(dc, 1)
            for hc in range(NHC):
                b_mm(dc, hc)
                if hc + 2 < NHC:
                    b_front(dc, hc + 2)
            pev = k.now("peC")
            for j in range(TPC):
                k.op("vector", lambda e, j=j, dc=dc: e.tensor_scalar(
                    y2ev[:], psY[j][:], s_bc[:, 1:2], None, ALU.mult),
                    wait=(("peC", pev), ("y2w", k.now("y2w")), ("veC", k.now("veC"))), inc=(("veC", 1),))
                k.dma("sync", Y2sc[:, j * D + dc * 512:j * D + (dc + 1) * 512], y2ev[:], "y2w",
                      wait=(("veC", k.now("veC")),))
            M["yev"] = k.now("veC")
        Y2W = k.now("y2w")

    with contextlib.ExitStack() as stE:
        y2stg = [SB(stE, f"y2stg{i}", (128, D)) for i in range(2)]
        lnout = SB(stE, "lnout", (128, D), F32)
        outt = SB(stE, "outt", (128, D), F32)
        for j in range(TPC):
            par = j % 2
            k.dma("sync", xstg[par][:], X2d[j * 128:(j + 1) * 128, :], "ex%d" % par,
                  wait=(("x2w", X2W), ("veO", M.get("bxfree%d" % par, 0)),
                        ("veC", M.get("exfree%d" % par, 0))) + (k.fence() if j < 2 else ()))
            k.dma("scalar", y2stg[par][:], Y2sc[:, j * D:(j + 1) * D], "ey%d" % par,
                  wait=(("y2w", Y2W), ("veC", M.get("eyfree%d" % par, 0))) + (k.fence() if j < 2 else ()))
            vln3 = ln_tile(y2stg[par][:], lnout[:], "veC", "seC",
                           extra_wait=(("ey%d" % par, k.now("ey%d" % par)),
                                       ("veC", k.now("veC")), ("veO", k.now("veO")),
                                       ("seO", k.now("seO"))))
            M["eyfree%d" % par] = vln3
            k.op("vector", lambda e, par=par: e.tensor_tensor(outt[:], xstg[par][:], lnout[:], ALU.add),
                 wait=(("veC", vln3), ("ex%d" % par, k.now("ex%d" % par)),
                       ("outw", k.now("outw"))),
                 inc=(("veC", 1),))
            M["exfree%d" % par] = k.now("veC")
            k.dma("sync", out[j * 128:(j + 1) * 128, :], outt[:], "outw", wait=(("veC", k.now("veC")),))
    k.op("sync", lambda e: e.wait_ge(k.sems["outw"], k.now("outw")))

    block = es.enter_context(nc.Block())
    k.emit(block)
    es.close()
    return nc


def _trig_tables():
    inv = (np.float32(1.0) / np.float32(10000.0) **
           (np.arange(0, HD, 2, dtype=np.float32) / np.float32(HD))).astype(np.float32)
    tpos = np.arange(S, dtype=np.float32)
    freqs = (tpos[:, None] * inv[None, :]).astype(np.float32)
    return np.sin(freqs), np.cos(freqs)


def _tile_major(tab):
    return np.ascontiguousarray(tab.reshape(NT, 128, 64).transpose(1, 0, 2).reshape(128, NT * 64))


def _prep(inputs):
    x = np.ascontiguousarray(np.asarray(inputs["L0_attn_x"], np.float32).reshape(S, D))
    d = {}
    d["wqT"] = np.ascontiguousarray(np.asarray(inputs["L0_attn_Wq"], np.float32).T)
    d["wkT"] = np.ascontiguousarray(np.asarray(inputs["L0_attn_Wk"], np.float32).T)
    d["wvT"] = np.ascontiguousarray(np.asarray(inputs["L0_attn_Wv"], np.float32).T)
    d["woT"] = np.ascontiguousarray(np.asarray(inputs["L0_attn_Wo"], np.float32).T)
    d["bqr"] = np.asarray(inputs["L0_attn_bq"], np.float32).reshape(1, D)
    d["bkr"] = np.asarray(inputs["L0_attn_bk"], np.float32).reshape(1, KVD)
    d["bvr"] = np.asarray(inputs["L0_attn_bv"], np.float32).reshape(1, KVD)
    d["bor"] = np.asarray(inputs["L0_attn_bo"], np.float32).reshape(1, D)
    d["gain"] = np.asarray(inputs["L0_attn_q_gain"], np.float32).reshape(1, NH)
    d["w1T"] = np.ascontiguousarray(np.asarray(inputs["L0_ffn_w1"], np.float32).T)
    d["w3T"] = np.ascontiguousarray(np.asarray(inputs["L0_ffn_w3"], np.float32).T)
    d["w2T"] = np.ascontiguousarray(np.asarray(inputs["L0_ffn_w2"], np.float32).T)
    d["identity"] = np.eye(128).astype(ml_dtypes.bfloat16)
    sin_t, cos_t = _trig_tables()
    d["ksf"] = _tile_major(sin_t)
    d["kcf"] = _tile_major(cos_t)
    d["_sin"], d["_cos"] = sin_t, cos_t
    d["_xfTt"] = np.ascontiguousarray(x.T.reshape(D, NT, 128).transpose(1, 0, 2))
    return x, d


def _core_inputs(x, shared, s_):
    xt = x.reshape(NT, 128, D)
    m = dict(shared)
    m["xf"] = x
    m["xfTt"] = shared["_xfTt"]
    sin_t, cos_t = m.pop("_sin"), m.pop("_cos")
    m.pop("_xfTt", None)
    own = np.arange(s_, NT, NCORE)
    m["xo"] = np.ascontiguousarray(xt[own].reshape(TPC * 128, D))
    m["xoTt"] = np.ascontiguousarray(xt[own].transpose(0, 2, 1))
    sq = sin_t.reshape(NT, 128, 64)[own]
    cq = cos_t.reshape(NT, 128, 64)[own]
    m["qsf"] = np.ascontiguousarray(sq.transpose(1, 0, 2).reshape(128, TPC * 64))
    m["qcf"] = np.ascontiguousarray(cq.transpose(1, 0, 2).reshape(128, TPC * 64))
    bn = np.zeros((16, 128, 1024), np.float32)
    for a in range(4):
        for ki in range(4):
            kb = 4 * a + ki
            for kc in range(4):
                kglob = kb * 512 + kc * 128 + np.arange(128)
                for sub in range(2):
                    jslot = 2 * a + sub
                    qglob = (8 * jslot + s_) * 128 + np.arange(128)
                    bn[a * 4 + ki, :, kc * 256 + sub * 128:kc * 256 + (sub + 1) * 128] = (
                        kglob[:, None] <= qglob[None, :])
    m["binm"] = bn.astype(ml_dtypes.bfloat16)
    return m


def kernel(**inputs):
    x, shared = _prep(inputs)
    nc = build_nc(debug=False)
    in_maps = [_core_inputs(x, shared, s_) for s_ in range(NCORE)]
    res = run_bass_kernel_spmd(nc, in_maps, core_ids=list(range(NCORE)))
    outf = np.empty((S, D), np.float32)
    outv = outf.reshape(NT, 128, D)
    for s_ in range(NCORE):
        outv[s_::NCORE] = np.asarray(res.results[s_]["out"]).reshape(TPC, 128, D)
    return outf.reshape(1, S, D)
